# version 01
# baseline (speedup 1.0000x reference)
import math
import numpy as np
import ml_dtypes
from contextlib import ExitStack
import concourse.bass as bass
import concourse.mybir as mybir
from concourse.bass_utils import run_bass_kernel_spmd

F32 = mybir.dt.float32
BF16 = mybir.dt.bfloat16
I32 = mybir.dt.int32
AF = mybir.ActivationFunctionType
ALU = mybir.AluOpType
AX = mybir.AxisListType

NCORES = 8
T = 4096
D = 1024
NTT = T // 128
NKD = D // 128
BF = ml_dtypes.bfloat16


class Sem:
    def __init__(self, handle):
        self.h = handle
        self.n = 0

    def inc(self, instr, k=1):
        instr.then_inc(self.h, k)
        self.n += k
        return instr


class DmaRing:

    def __init__(self, c, name, nslots):
        self.sems = [c.sem(f"{name}_r{i}") for i in range(nslots)]
        self.nslots = nslots
        self.base = [s.n for s in self.sems]

    def load(self, eng, m, dst_ap, src_ap, **kw):
        s = self.sems[m % self.nslots]
        s.inc(eng.dma_start(dst_ap, src_ap, **kw), 16)

    def wait(self, eng, m):
        s = self.sems[m % self.nslots]
        eng.wait_ge(s.h, self.base[m % self.nslots] + 16 * (m // self.nslots + 1))


class Ctx:
    def __init__(self, nc, ctx):
        self.nc = nc
        self.ctx = ctx
        self._sems = {}

    def sem(self, name):
        if name not in self._sems:
            self._sems[name] = Sem(self.ctx.enter_context(self.nc.semaphore(name)))
        return self._sems[name]

    def sb(self, name, shape, dtype):
        return self.ctx.enter_context(self.nc.sbuf_tensor(name, shape, dtype))

    def ps(self, name, shape, dtype=F32):
        return self.ctx.enter_context(self.nc.psum_tensor(name, shape, dtype))


def emit_identity(c, block, with_f32=False):
    ident = c.sb("ident_bf", [128, 128], BF16)
    s = c.sem("s_ident")
    identf = c.sb("ident_f", [128, 128], F32) if with_f32 else None

    @block.gpsimd
    def _(g):
        s.inc(g.memset(ident[:, :], 1.0))
        g.wait_ge(s.h, s.n)
        s.inc(g.affine_select(ident[:, :], ident[:, :], [[1, 128]],
                              ALU.is_equal, 0.0, base=0, channel_multiplier=-1))
        if with_f32:
            s.inc(g.memset(identf[:, :], 1.0))
            g.wait_ge(s.h, s.n)
            s.inc(g.affine_select(identf[:, :], identf[:, :], [[1, 128]],
                                  ALU.is_equal, 0.0, base=0, channel_multiplier=-1))
    if with_f32:
        return ident, identf, s
    return ident, s


def emit_weight_T(c, block, name, hbm_ap, rows, cols, ident, s_ident, psA, psB,
                  dma_eng="gpsimd", evac_eng="vector", stage=None,
                  stage_free=None, wT_sb=None):
    nr, kcn = rows // 128, cols // 128
    wsb = stage if stage is not None else c.sb(name + "_w", [128, nr * cols], BF16)
    wT = wT_sb if wT_sb is not None else c.sb(name + "_wT", [128, kcn * rows], BF16)
    sld = c.sem(name + "_ld")
    str_ = c.sem(name + "_tr")
    sdone = c.sem(name + "_done")
    r = hbm_ap.rearrange("(a p) q -> a p q", p=128)
    dma_block = block.gpsimd if dma_eng == "gpsimd" else block.sync

    @dma_block
    def _(g):
        if stage_free is not None:
            for sh, cnt in stage_free:
                g.wait_ge(sh, cnt)
        for a in range(nr):
            sld.inc(g.dma_start(wsb[:, a * cols:(a + 1) * cols], r[a]), 16)

    base_tr, base_dn, base_ld = str_.n, sdone.n, sld.n - 16 * nr

    @block.tensor
    def _(t):
        t.wait_ge(s_ident.h, s_ident.n)
        for i in range(nr * kcn):
            a, kc = divmod(i, kcn)
            t.wait_ge(sld.h, base_ld + 16 * nr)
            if i >= 2:
                t.wait_ge(sdone.h, base_dn + i - 1)
            ps = psA if i % 2 == 0 else psB
            str_.inc(t.transpose(ps[:, 0:128],
                                 wsb[:, a * cols + kc * 128: a * cols + (kc + 1) * 128],
                                 ident[:, :]))

    evac_block = block.vector if evac_eng == "vector" else block.scalar

    @evac_block
    def _(v):
        for i in range(nr * kcn):
            a, kc = divmod(i, kcn)
            v.wait_ge(str_.h, base_tr + i + 1)
            ps = psA if i % 2 == 0 else psB
            cp = v.tensor_copy if evac_eng == "vector" else v.copy
            sdone.inc(cp(wT[:, kc * rows + a * 128: kc * rows + (a + 1) * 128],
                         ps[:, 0:128]))
    return wT, sdone


def emit_weight_direct(c, block, name, hbm_ap, rows, cols, dma_eng="gpsimd",
                       dtype=BF16):
    dma_block = block.gpsimd if dma_eng == "gpsimd" else block.sync
    s = c.sem(name + "_ld")
    if rows <= 128:
        wsb = c.sb(name + "_w", [rows, cols], dtype)

        @dma_block
        def _(g):
            s.inc(g.dma_start(wsb[:, :], hbm_ap), 16)
        return wsb, s
    assert rows % 128 == 0
    nr = rows // 128
    wsb = c.sb(name + "_w", [128, nr * cols], dtype)
    r = hbm_ap.rearrange("(a p) q -> a p q", p=128)

    @dma_block
    def _(g):
        for a in range(nr):
            s.inc(g.dma_start(wsb[:, a * cols:(a + 1) * cols], r[a]), 16)
    return wsb, s


def lnT_tile(lnT, k, m):
    col = (k * NTT + m) * 128
    return lnT[:, col:col + 128]


def emit_preln_stream(c, block, name, x_hbm, lnT, ident, s_ident, banks,
                      tensor_pre_waits=(), eps=1e-5, out_stats=False):
    psA, psB, psC, psD = banks
    xr = x_hbm.rearrange("(m p) d -> m p d", p=128)
    NSL = 4
    x1 = c.sb(name + "_x1", [128, NSL * D], F32)
    x2 = c.sb(name + "_x2", [128, NSL * D], F32)
    lnr = c.sb(name + "_lnr", [128, 2 * D], BF16)
    sq = c.sb(name + "_sqd", [128, D], F32)
    sums = c.sb(name + "_sums", [128, NTT], F32)
    sqs = c.sb(name + "_sqs", [128, NTT], F32)
    mu = c.sb(name + "_mu", [128, NTT], F32)
    rstd = c.sb(name + "_rstd", [128, NTT], F32)
    epst = c.sb(name + "_epsln", [128, 1], F32)

    s_x1 = c.sem(name + "_sx1")
    s_x2 = c.sem(name + "_sx2")
    s_red = c.sem(name + "_sred")
    s_sqd = c.sem(name + "_ssqd")
    s_stat = c.sem(name + "_sstat")
    s_ln = c.sem(name + "_sln")
    s_tr = c.sem(name + "_str")
    s_tileA = c.sem(name + "_stileA")
    s_tileB = c.sem(name + "_stileB")

    bx1, bx2 = s_x1.n, s_x2.n
    bred, bsqd, bln = s_red.n, s_sqd.n, s_ln.n
    btr, btileA, btileB = s_tr.n, s_tileA.n, s_tileB.n

    @block.sync
    def _(sy):
        for m in range(NTT):
            if m >= NSL:
                sy.wait_ge(s_red.h, bred + m - NSL + 1)
                sy.wait_ge(s_sqd.h, bsqd + m - NSL + 1)
            s_x1.inc(sy.dma_start(x1[:, (m % NSL) * D:(m % NSL + 1) * D], xr[m]), 16)

    @block.gpsimd
    def _(g):
        for m in range(NTT):
            if m >= NSL:
                g.wait_ge(s_ln.h, bln + m - NSL + 1)
            s_x2.inc(g.dma_start(x2[:, (m % NSL) * D:(m % NSL + 1) * D], xr[m]), 16)

    @block.vector
    def _(v):
        s_stat.inc(v.memset(epst[:, :], eps))
        for m in range(NTT):
            v.wait_ge(s_x1.h, bx1 + (m + 1) * 16)
            s_red.inc(v.reduce_sum(sums[:, m:m + 1],
                                   x1[:, (m % NSL) * D:(m % NSL + 1) * D], axis=AX.X))

    @block.scalar
    def _(s):
        for m in range(NTT):
            s.wait_ge(s_x1.h, bx1 + (m + 1) * 16)
            s_sqd.inc(s.activation(sq[:, :], x1[:, (m % NSL) * D:(m % NSL + 1) * D],
                                   AF.Square, accum_out=sqs[:, m:m + 1]))

    @block.vector
    def _(v):
        v.wait_ge(s_sqd.h, bsqd + NTT)
        v.tensor_scalar_mul(mu[:, :], sums[:, :], 1.0 / D)
        v.tensor_scalar_mul(rstd[:, :], sqs[:, :], 1.0 / D)
        v.tensor_tensor(sums[:, :], mu[:, :], mu[:, :], ALU.mult)
        s_stat.inc(v.tensor_tensor(rstd[:, :], rstd[:, :], sums[:, :], ALU.subtract))

    @block.scalar
    def _(s):
        s.wait_ge(s_stat.h, s_stat.n)
        s_stat.inc(s.activation(rstd[:, :], rstd[:, :], AF.Sqrt, bias=epst[:, 0:1]))

    @block.vector
    def _(v):
        v.wait_ge(s_stat.h, s_stat.n)
        s_stat.inc(v.reciprocal(rstd[:, :], rstd[:, :]))
        v.wait_ge(s_stat.h, s_stat.n)
        for m in range(NTT):
            v.wait_ge(s_x2.h, bx2 + (m + 1) * 16)
            if m >= 2:
                v.wait_ge(s_tr.h, btr + m - 1)
            s_ln.inc(v.tensor_scalar(lnr[:, (m % 2) * D:(m % 2 + 1) * D],
                                     x2[:, (m % NSL) * D:(m % NSL + 1) * D],
                                     mu[:, m:m + 1], rstd[:, m:m + 1],
                                     ALU.subtract, ALU.mult))

    @block.tensor
    def _(t):
        t.wait_ge(s_ident.h, s_ident.n)
        for sem, val in tensor_pre_waits:
            t.wait_ge(sem.h, val)
        for m in range(NTT):
            t.wait_ge(s_ln.h, bln + m + 1)
            if m >= 2:
                t.wait_ge(s_tileA.h, btileA + m - 1)
                t.wait_ge(s_tileB.h, btileB + m - 1)
            pa, pb = (psA, psB) if m % 2 == 0 else (psC, psD)
            for k in range(NKD):
                ps = pa if k % 2 == 0 else pb
                ins = t.transpose(ps[:, (k // 2) * 128:(k // 2 + 1) * 128],
                                  lnr[:, (m % 2) * D + k * 128:(m % 2) * D + (k + 1) * 128],
                                  ident[:, :])
                if k == NKD - 1:
                    s_tr.inc(ins)

    lnT4 = lnT[:, :].rearrange("p (k m q) -> p k m q", k=NKD, m=NTT)

    @block.scalar
    def _(s):
        for m in range(NTT):
            s.wait_ge(s_tr.h, btr + m + 1)
            pa = psA if m % 2 == 0 else psC
            pb = psB if m % 2 == 0 else psD
            s_tileA.inc(s.copy(lnT4[:, 0:NKD:2, m, :],
                               pa[:, 0:512].rearrange("p (a q) -> p a q", q=128)))
            s_tileB.inc(s.copy(lnT4[:, 1:NKD:2, m, :],
                               pb[:, 0:512].rearrange("p (a q) -> p a q", q=128)))

    out = dict(s_tileA=s_tileA, s_tileB=s_tileB, btileA=btileA, btileB=btileB)
    if out_stats:
        out.update(mu=mu, rstd=rstd, s_stat=s_stat)
    return out


class PrelnRing:

    def __init__(self, c, name, x_hbm, ident, s_ident, psA, psB, nslots=4,
                 eps=1e-5, xring=None, x1_sb=None, nx=4):
        self.c, self.name = c, name
        self.ident, self.s_ident = ident, s_ident
        self.psA, self.psB = psA, psB
        self.nslots = nslots
        self.eps = eps
        self.xr = x_hbm.rearrange("(m p) d -> m p d", p=128)
        self.x1 = x1_sb if x1_sb is not None else c.sb(name + "_x1", [128, 2 * D], F32)
        self.nx = nx
        self.xring = xring if xring is not None else c.sb(name + "_xr", [128, nx * D], F32)
        self.lnr = c.sb(name + "_lnr", [128, 2 * D], BF16)
        self.lnTr = c.sb(name + "_lnTr", [128, nslots * D], BF16)
        self.sq = c.sb(name + "_sqd", [128, D], F32)
        self.sums = c.sb(name + "_sums", [128, NTT], F32)
        self.sqs = c.sb(name + "_sqs", [128, NTT], F32)
        self.mu = c.sb(name + "_mu", [128, NTT], F32)
        self.rstd = c.sb(name + "_rstd", [128, NTT], F32)
        self.epst = c.sb(name + "_epsln", [128, 1], F32)
        self.r_x1 = DmaRing(c, name + "_x1r", 2)
        self.r_x2 = DmaRing(c, name + "_x2r", nx)
        self.s_red = c.sem(name + "_sred")
        self.s_sqd = c.sem(name + "_ssqd")
        self.s_stat = c.sem(name + "_sstat")
        self.s_ln = c.sem(name + "_sln")
        self.s_tr = c.sem(name + "_str")
        self.s_tile = c.sem(name + "_stile")
        self.b = dict(ln=self.s_ln.n, tr=self.s_tr.n, tile=self.s_tile.n)

    def emit_stats(self, block):
        me = self
        me.b["red"], me.b["sqd"] = me.s_red.n, me.s_sqd.n

        @block.sync
        def _(sy):
            for m in range(NTT):
                if m >= 2:
                    sy.wait_ge(me.s_red.h, me.b["red"] + m - 1)
                    sy.wait_ge(me.s_sqd.h, me.b["sqd"] + m - 1)
                me.r_x1.load(sy, m, me.x1[:, (m % 2) * D:(m % 2 + 1) * D], me.xr[m])

        @block.vector
        def _(v):
            me.s_stat.inc(v.memset(me.epst[:, :], me.eps))
            for m in range(NTT):
                me.r_x1.wait(v, m)
                me.s_red.inc(v.reduce_sum(me.sums[:, m:m + 1],
                                          me.x1[:, (m % 2) * D:(m % 2 + 1) * D],
                                          axis=AX.X))

        @block.scalar
        def _(s):
            for m in range(NTT):
                me.r_x1.wait(s, m)
                if m >= 1:
                    s.wait_ge(me.s_sqd.h, me.b["sqd"] + m)
                me.s_sqd.inc(s.activation(me.sq[:, :],
                                          me.x1[:, (m % 2) * D:(m % 2 + 1) * D],
                                          AF.Square, accum_out=me.sqs[:, m:m + 1]))

        @block.vector
        def _(v):
            v.wait_ge(me.s_sqd.h, me.b["sqd"] + NTT)
            v.wait_ge(me.s_red.h, me.b["red"] + NTT)
            me.s_stat.inc(v.tensor_scalar_mul(me.mu[:, :], me.sums[:, :], 1.0 / D))
            me.s_stat.inc(v.tensor_scalar_mul(me.rstd[:, :], me.sqs[:, :], 1.0 / D))
            v.wait_ge(me.s_stat.h, me.s_stat.n)
            me.s_stat.inc(v.tensor_tensor(me.sums[:, :], me.mu[:, :], me.mu[:, :],
                                          ALU.mult))
            v.wait_ge(me.s_stat.h, me.s_stat.n)
            me.s_stat.inc(v.tensor_tensor(me.rstd[:, :], me.rstd[:, :], me.sums[:, :],
                                          ALU.subtract))

        @block.scalar
        def _(s):
            s.wait_ge(me.s_stat.h, me.s_stat.n)
            me.s_stat.inc(s.activation(me.rstd[:, :], me.rstd[:, :], AF.Sqrt,
                                       bias=me.epst[:, 0:1]))

        @block.vector
        def _(v):
            v.wait_ge(me.s_stat.h, me.s_stat.n)
            me.s_stat.inc(v.reciprocal(me.rstd[:, :], me.rstd[:, :]))

    def emit_x2(self, block, consumer=None):
        me = self

        @block.sync
        def _(sy):
            for m in range(NTT):
                if m >= me.nx and consumer is not None:
                    for sh, cnt in consumer(m - me.nx):
                        sy.wait_ge(sh, cnt)
                me.r_x2.load(sy, m, me.xring[:, (m % me.nx) * D:(m % me.nx + 1) * D],
                             me.xr[m])

    def apply(self, v, m):
        me = self
        if m == 0:
            v.wait_ge(me.s_stat.h, me.s_stat.n)
        me.r_x2.wait(v, m)
        if m >= 2:
            v.wait_ge(me.s_tr.h, me.b["tr"] + m - 1)
        me.s_ln.inc(v.tensor_scalar(me.lnr[:, (m % 2) * D:(m % 2 + 1) * D],
                                    me.xring[:, (m % me.nx) * D:(m % me.nx + 1) * D],
                                    me.mu[:, m:m + 1], me.rstd[:, m:m + 1],
                                    ALU.subtract, ALU.mult))

    def transposes(self, t, m):
        me = self
        if m == 0:
            t.wait_ge(me.s_ident.h, me.s_ident.n)
        t.wait_ge(me.s_ln.h, me.b["ln"] + m + 1)
        if m >= 2:
            t.wait_ge(me.s_tile.h, me.b["tile"] + m - 1)
        pa = me.psA if m % 2 == 0 else me.psB
        for k in range(NKD):
            ins = t.transpose(pa[:, k * 128:(k + 1) * 128],
                              me.lnr[:, (m % 2) * D + k * 128:(m % 2) * D + (k + 1) * 128],
                              me.ident[:, :])
            if k == NKD - 1:
                me.s_tr.inc(ins)

    def evac(self, s, m):
        me = self
        s.wait_ge(me.s_tr.h, me.b["tr"] + m + 1)
        pa = me.psA if m % 2 == 0 else me.psB
        cp = s.copy if hasattr(s, "copy") else s.tensor_copy
        sl = m % me.nslots
        dst = me.lnTr[:, sl * 128:].rearrange("p (a q) -> p a q", q=128)[
            :, 0:(me.nslots * (NKD - 1)) + 1:me.nslots, :]
        me.s_tile.inc(cp(dst, pa[:, :].rearrange("p (a q) -> p a q", q=128)))

    def lnT_tile(self, k, m):
        sl = (k * self.nslots + (m % self.nslots)) * 128
        return self.lnTr[:, sl:sl + 128]

    def lnT_chunk(self, k, m0):
        assert m0 % self.nslots == 0
        return self.lnTr[:, k * self.nslots * 128:(k + 1) * self.nslots * 128]

    def lnT_span4(self, k, m0):
        s0 = m0 % self.nslots
        assert m0 % 4 == 0 and s0 + 4 <= self.nslots
        sl = (k * self.nslots + s0) * 128
        return self.lnTr[:, sl:sl + 512]



def emit_wT_stream(c, block, name, hbm_ap, rows, cols, ident, s_ident, psA, psB,
                   row_tiles, on_tile, pre_waits=(), evac_eng="vector",
                   stage=None, r_stage=None):
    kcn = cols // 128
    if stage is None:
        stage = c.sb(name + "_st", [128, 2 * cols], BF16)
    if r_stage is None:
        r_stage = DmaRing(c, name + "_rst", 2)
    s_tr = c.sem(name + "_tr")
    s_ev = c.sem(name + "_ev")
    b_tr, b_ev = s_tr.n, s_ev.n
    r = hbm_ap.rearrange("(a p) q -> a p q", p=128)

    @block.gpsimd
    def _(g):
        for i, a in enumerate(row_tiles):
            if i >= 2:
                g.wait_ge(s_tr.h, b_tr + i - 1)
            r_stage.load(g, i, stage[:, (i % 2) * cols:(i % 2 + 1) * cols], r[a])

    @block.tensor
    def _(t):
        t.wait_ge(s_ident.h, s_ident.n)
        for sem_, cnt in pre_waits:
            t.wait_ge(sem_, cnt)
        for i, a in enumerate(row_tiles):
            r_stage.wait(t, i)
            if i >= 2:
                t.wait_ge(s_ev.h, b_ev + i - 1)
            ps = psA if i % 2 == 0 else psB
            for kc in range(kcn):
                ins = t.transpose(ps[:, kc * 128:(kc + 1) * 128],
                                  stage[:, (i % 2) * cols + kc * 128:(i % 2) * cols + (kc + 1) * 128],
                                  ident[:, :])
            s_tr.inc(ins)

    eb = block.vector if evac_eng == "vector" else block.scalar

    @eb
    def _(v):
        for i, a in enumerate(row_tiles):
            v.wait_ge(s_tr.h, b_tr + i + 1)
            ps = psA if i % 2 == 0 else psB
            s_ev.inc(on_tile(v, a, ps))
    return s_tr, s_ev



def build_l2ffn():
    nc = bass.Bass("TRN2", target_bir_lowering=False)
    x_h = nc.dram_tensor("x", [T, D], F32, kind="ExternalInput").ap()
    w_h = nc.dram_tensor("weight", [D, D], F32, kind="ExternalInput").ap()
    di_h = nc.dram_tensor("disha", [64, D], F32, kind="ExternalInput").ap()
    y_h = nc.dram_tensor("y", [T, D], F32, kind="ExternalOutput").ap()

    with ExitStack() as ctx:
        c = Ctx(nc, ctx)
        lnT = c.sb("lnT", [128, NKD * NTT * 128], BF16)
        poolT = c.sb("poolT", [64, T], BF16)
        selm = c.sb("selm", [128, 64], BF16)
        x3 = c.sb("x3", [128, 4 * D], F32)
        yst = c.sb("yst", [128, 4 * D], F32)
        pbA = c.ps("pbA", [128, 1024], BF16)
        pbB = c.ps("pbB", [128, 1024], BF16)
        pbC = c.ps("pbC", [128, 1024], BF16)
        pbD = c.ps("pbD", [128, 1024], BF16)
        pf0 = c.ps("pf0", [128, 512], F32)
        pf1 = c.ps("pf1", [128, 512], F32)
        pfp = c.ps("pfp", [128, 512], F32)

        s_sel = c.sem("s_sel")
        s_pool = c.sem("s_pool")
        s_poolT = c.sem("s_poolT")
        s_mm = c.sem("s_mm")
        s_x3 = c.sem("s_x3")
        s_add = c.sem("s_add")
        s_st = c.sem("s_st")

        with nc.Block() as block:
            ident, s_ident = emit_identity(c, block)

            @block.gpsimd
            def _(g):
                s_sel.inc(g.memset(selm[:, :], 1.0))
                g.wait_ge(s_sel.h, s_sel.n)
                s_sel.inc(g.affine_select(selm[0:64, :], selm[0:64, :], [[1, 64]],
                                          ALU.is_equal, 0.0, base=0, channel_multiplier=-1))
                g.wait_ge(s_sel.h, s_sel.n)
                s_sel.inc(g.affine_select(selm[64:128, :], selm[64:128, :], [[1, 64]],
                                          ALU.is_equal, 0.0, base=0, channel_multiplier=-1))

            wT, s_w = emit_weight_T(c, block, "w", w_h, D, D, ident, s_ident,
                                    pbA, pbB, dma_eng="gpsimd", evac_eng="vector")
            di_sb, s_di = emit_weight_direct(c, block, "di", di_h, 64, D)
            pre = emit_preln_stream(c, block, "ln", x_h, lnT, ident, s_ident,
                                    (pbA, pbB, pbC, pbD),
                                    tensor_pre_waits=[(s_w, 64)])
            s_tA, s_tB = pre["s_tileA"], pre["s_tileB"]
            btA, btB = pre["btileA"], pre["btileB"]

            @block.tensor
            def _(t):
                t.wait_ge(s_sel.h, s_sel.n)
                for tc in range(8):
                    t.wait_ge(s_tA.h, btA + tc * 4 + 4)
                    t.wait_ge(s_tB.h, btB + tc * 4 + 4)
                    if tc >= 1:
                        t.wait_ge(s_poolT.h, tc)
                    for k in range(NKD):
                        ins = t.matmul(pfp[0:64, 0:512], selm[:, :],
                                       lnT[:, k * T + tc * 512: k * T + (tc + 1) * 512],
                                       start=(k == 0), stop=(k == NKD - 1))
                        if k == NKD - 1:
                            s_pool.inc(ins)

            @block.scalar
            def _(s):
                for tc in range(8):
                    s.wait_ge(s_pool.h, tc + 1)
                    s_poolT.inc(s.copy(poolT[:, tc * 512:(tc + 1) * 512],
                                       pfp[0:64, 0:512]))

            @block.sync
            def _(sy):
                xr = x_h.rearrange("(m p) d -> m p d", p=128)
                for m in range(NTT):
                    if m >= 4:
                        sy.wait_ge(s_add.h, 2 * (m - 4 + 1))
                    s_x3.inc(sy.dma_start(x3[:, (m % 4) * D:(m % 4 + 1) * D], xr[m]), 16)

            @block.tensor
            def _(t):
                t.wait_ge(s_w.h, s_w.n)
                t.wait_ge(s_di.h, s_di.n)
                t.wait_ge(s_poolT.h, 8)
                for m in range(NTT):
                    t.wait_ge(s_tA.h, btA + m + 1)
                    t.wait_ge(s_tB.h, btB + m + 1)
                    for ch in range(2):
                        i = 2 * m + ch
                        if i >= 2:
                            t.wait_ge(s_add.h, i - 1)
                        pf = pf0 if i % 2 == 0 else pf1
                        for k in range(NKD):
                            t.matmul(pf[:, :], lnT_tile(lnT, k, m),
                                     wT[:, k * D + ch * 512: k * D + (ch + 1) * 512],
                                     start=(k == 0), stop=False)
                        s_mm.inc(t.matmul(pf[:, :], poolT[:, m * 128:(m + 1) * 128],
                                          di_sb[:, ch * 512:(ch + 1) * 512],
                                          start=False, stop=True))

            @block.vector
            def _(v):
                for m in range(NTT):
                    for ch in range(2):
                        i = 2 * m + ch
                        v.wait_ge(s_mm.h, i + 1)
                        v.wait_ge(s_x3.h, 16 * (m + 1))
                        if m >= 2 and ch == 0:
                            v.wait_ge(s_st.h, 16 * (m - 1))
                        pf = pf0 if i % 2 == 0 else pf1
                        s_add.inc(v.tensor_tensor(
                            yst[:, (m % 2) * D + ch * 512:(m % 2) * D + (ch + 1) * 512],
                            x3[:, (m % 4) * D + ch * 512:(m % 4) * D + (ch + 1) * 512],
                            pf[:, :], ALU.add))

            @block.gpsimd
            def _(g):
                yr = y_h.rearrange("(m p) d -> m p d", p=128)
                for m in range(NTT):
                    g.wait_ge(s_add.h, 2 * (m + 1))
                    s_st.inc(g.dma_start(yr[m],
                                         yst[:, (m % 2) * D:(m % 2 + 1) * D]), 16)
                g.wait_ge(s_st.h, 16 * NTT)
    return nc



def emit_col_load(c, block, name, hbm_1d, n, dma_eng="sync"):
    cols = n // 128
    sb = c.sb(name + "_col", [128, cols], F32)
    s = c.sem(name + "_cld")
    r = hbm_1d.rearrange("(a p) -> p a", p=128)
    dma_block = block.gpsimd if dma_eng == "gpsimd" else block.sync

    @dma_block
    def _(g):
        s.inc(g.dma_start(sb[:, :], r, allow_slow_non_contiguous=True), 16)
    return sb, s


def emit_row_load(c, block, name, hbm_1d, n, dma_eng="sync", dtype=F32):
    sb = c.sb(name + "_row", [1, n], dtype)
    s = c.sem(name + "_rld")
    dma_block = block.gpsimd if (dma_eng == "gpsimd" or dtype != F32) else block.sync

    @dma_block
    def _(g):
        s.inc(g.dma_start(sb[:, :], hbm_1d.rearrange("(o n) -> o n", o=1)), 16)
    return sb, s


def emit_row_replicate(c, block, name, row_sb, s_row, n, onesf, s_ones, ps_f32,
                       prev=None):
    rep = c.sb(name + "_rep", [128, n], F32)
    s_mm = c.sem(name + "_repmm")
    s_dn = c.sem(name + "_repdn")
    nch = (n + 511) // 512
    base_dn = s_dn.n

    @block.tensor
    def _(t):
        t.wait_ge(s_row.h, s_row.n)
        t.wait_ge(s_ones.h, s_ones.n)
        if prev is not None:
            t.wait_ge(prev.h, prev.n)
        for ch in range(nch):
            lo, hi = ch * 512, min(n, (ch + 1) * 512)
            if ch >= 1:
                t.wait_ge(s_dn.h, base_dn + ch)
            s_mm.inc(t.matmul(ps_f32[:, 0:hi - lo], onesf[:, 0:128],
                              row_sb[:, lo:hi], start=True, stop=True))

    base = s_mm.n - nch

    @block.vector
    def _(v):
        for ch in range(nch):
            lo, hi = ch * 512, min(n, (ch + 1) * 512)
            v.wait_ge(s_mm.h, base + ch + 1)
            s_dn.inc(v.tensor_copy(rep[:, lo:hi], ps_f32[:, 0:hi - lo]))
    return rep, s_dn, s_mm


def build_l1attn(debug=False):
    nc = bass.Bass("TRN2", target_bir_lowering=False)
    x_h = nc.dram_tensor("x", [T, D], F32, kind="ExternalInput").ap()
    llr_h = nc.dram_tensor("log_lambda_raw", [D], F32, kind="ExternalInput").ap()
    wb_h = nc.dram_tensor("Wb", [D, D], F32, kind="ExternalInput").ap()
    bb_h = nc.dram_tensor("bb", [D], F32, kind="ExternalInput").ap()
    wc_h = nc.dram_tensor("Wc", [D, D], F32, kind="ExternalInput").ap()
    bc_h = nc.dram_tensor("bc", [D], F32, kind="ExternalInput").ap()
    gam_h = nc.dram_tensor("gamma", [D], F32, kind="ExternalInput").ap()
    bet_h = nc.dram_tensor("beta", [D], F32, kind="ExternalInput").ap()
    y_h = nc.dram_tensor("y", [T, D], F32, kind="ExternalOutput").ap()

    LN_EPS = 1e-5
    dbg = {}
    if debug:
        for nm, shp in [("d_oml", [128, D]), ("d_bbm", [128, D]), ("d_lam8", [128, 8 * 512]),
                        ("d_hbuf", [128, 2 * 8 * 512]), ("d_valm", [128, 2 * D]),
                        ("d_zb", [128, 4 * D]), ("d_zmu", [128, NTT]), ("d_zrstd", [128, NTT]),
                        ("d_valT", [128, 2 * 8 * 512]), ("d_lnTr", [128, 2 * D])]:
            dbg[nm] = nc.dram_tensor(nm, shp, F32, kind="ExternalOutput").ap()
    with ExitStack() as ctx:
        c = Ctx(nc, ctx)
        valm = c.sb("valm", [128, 2 * D], BF16)
        x3 = c.sb("x3", [128, 4 * D], F32)

        hbuf = c.sb("hbuf", [128, 2 * 8 * 512], BF16)
        lam8 = c.sb("lam8", [128, 8 * 512], BF16)
        lamc = c.sb("lamc", [128, NKD], F32)
        zb = c.sb("zb", [128, 4 * D], F32)
        zlnb = c.sb("zlnb", [128, D], F32)
        yst = c.sb("yst", [128, 4 * D], F32)
        onesf = c.sb("onesf", [1, 128], F32)
        zs = c.sb("zs", [128, NTT], F32)
        zq = c.sb("zq", [128, NTT], F32)
        zmu = c.sb("zmu", [128, NTT], F32)
        zrstd = c.sb("zrstd", [128, NTT], F32)
        epsz = c.sb("epsz", [128, 1], F32)
        llr_row = c.sb("llr_row", [1, D], F32)
        oml_row = c.sb("oml_row", [1, D], F32)
        bbm_row = c.sb("bbm_row", [1, D], F32)

        pbA = c.ps("pbA", [128, 1024], BF16)
        pbB = c.ps("pbB", [128, 1024], BF16)
        pbC = c.ps("pbC", [128, 1024], BF16)
        pbD = c.ps("pbD", [128, 1024], BF16)
        pf0 = c.ps("pf0", [128, 512], F32)
        pf1 = c.ps("pf1", [128, 512], F32)
        pf2 = c.ps("pf2", [128, 512], F32)
        pf3 = c.ps("pf3", [128, 512], F32)

        s_ones = c.sem("s_ones")
        s_rows = c.sem("s_rows")
        s_cols = c.sem("s_cols")
        s_mm1 = c.sem("s_mm1")
        s_val = c.sem("s_val")
        s_vtr = c.sem("s_vtr")
        s_vte = c.sem("s_vte")
        s_scan = c.sem("s_scan")
        s_sfen = c.sem("s_sfen")
        s_mm2 = c.sem("s_mm2")
        s_zc = c.sem("s_zc")
        s_zred = c.sem("s_zred")
        s_zsq = c.sem("s_zsq")
        s_zv = c.sem("s_zv")
        s_zq2 = c.sem("s_zq2")
        s_fen = c.sem("s_fen")
        s_x3 = c.sem("s_x3l1")
        s_y = c.sem("s_y")
        s_st = c.sem("s_st")

        with nc.Block() as block:
            ident, s_ident = emit_identity(c, block)
            llrc, s_llr = emit_col_load(c, block, "llr", llr_h, D, dma_eng="sync")
            row_stage = c.sb("row_stage", [1, D], F32)
            s_rs = c.sem("s_rs")
            s_sig = c.sem("s_sig")

            @block.vector
            def _(v):
                s_ones.inc(v.memset(epsz[:, :], LN_EPS))
                s_ones.inc(v.memset(onesf[:, :], 1.0))

            @block.scalar
            def _(s):
                s.wait_ge(s_llr.h, s_llr.n)
                s_cols.inc(s.activation(lamc[:, :], llrc[:, :], AF.Sigmoid))

            @block.vector
            def _(v):
                v.wait_ge(s_cols.h, s_cols.n)
                s_cols.inc(v.memset(lam8[:, :], 0.0))
                v.wait_ge(s_cols.h, s_cols.n)
                for a in range(NKD):
                    ins = v.tensor_scalar_add(lam8[:, a * 512:(a + 1) * 512],
                                              lam8[:, a * 512:(a + 1) * 512],
                                              lamc[:, a:a + 1])
                s_cols.inc(ins)

            wstage = c.sb("wstage", [128, NKD * D], BF16)
            valT = wstage
            wbT, s_wb = emit_weight_T(c, block, "wb", wb_h, D, D, ident, s_ident,
                                      pbA, pbB, dma_eng="gpsimd", evac_eng="vector",
                                      stage=wstage)
            wcT, s_wc = emit_weight_T(c, block, "wc", wc_h, D, D, ident, s_ident,
                                      pbC, pbD, dma_eng="gpsimd", evac_eng="vector",
                                      stage=wstage,
                                      stage_free=[(c.sem("wb_tr").h, 64)])

            @block.sync
            def _(sy):
                s_rs.inc(sy.dma_start(row_stage[:, :],
                                      llr_h.rearrange("(o n) -> o n", o=1)), 16)

            @block.scalar
            def _(s):
                s.wait_ge(s_rs.h, 16)
                s_sig.inc(s.activation(row_stage[:, :], row_stage[:, :], AF.Sigmoid))

            oml_rep, s_omlrep, s_omlmm = emit_row_replicate(
                c, block, "omlr", row_stage, s_sig, D, onesf, s_ones, pf2)

            @block.sync
            def _(sy):
                for sem_, hb in ((s_omlmm, bb_h),):
                    sy.wait_ge(sem_.h, sem_.n)
                    s_rs.inc(sy.dma_start(row_stage[:, :],
                                          hb.rearrange("(o n) -> o n", o=1)), 16)

            s_rs2 = c.sem("s_rs")
            bbm_rep, s_bbmrep, s_bbmm = emit_row_replicate(
                c, block, "bbmr", row_stage, s_rs, D, onesf, s_ones, pf2,
                prev=s_omlrep)

            @block.sync
            def _(sy):
                sy.wait_ge(s_bbmm.h, s_bbmm.n)
                s_rs.inc(sy.dma_start(row_stage[:, :],
                                      bc_h.rearrange("(o n) -> o n", o=1)), 16)

            bc_rep, s_bcrep, s_bcmm = emit_row_replicate(
                c, block, "bcr", row_stage, s_rs, D, onesf, s_ones, pf2,
                prev=s_bbmrep)

            @block.sync
            def _(sy):
                sy.wait_ge(s_bcmm.h, s_bcmm.n)
                s_rs.inc(sy.dma_start(row_stage[:, :],
                                      gam_h.rearrange("(o n) -> o n", o=1)), 16)

            gam_rep, s_gamrep, s_gammm = emit_row_replicate(
                c, block, "gamr", row_stage, s_rs, D, onesf, s_ones, pf2,
                prev=s_bcrep)

            @block.sync
            def _(sy):
                sy.wait_ge(s_gammm.h, s_gammm.n)
                s_rs.inc(sy.dma_start(row_stage[:, :],
                                      bet_h.rearrange("(o n) -> o n", o=1)), 16)

            bet_rep, s_betrep, s_betmm = emit_row_replicate(
                c, block, "betr", row_stage, s_rs, D, onesf, s_ones, pf2,
                prev=s_gamrep)

            s_fix = c.sem("s_fix")

            @block.vector
            def _(v):
                v.wait_ge(s_omlrep.h, s_omlrep.n)
                s_fix.inc(v.tensor_scalar(oml_rep[:, :], oml_rep[:, :], -1.0, 1.0,
                                          ALU.mult, ALU.add))
                v.wait_ge(s_fix.h, s_fix.n)
                v.wait_ge(s_bbmrep.h, s_bbmrep.n)
                s_fix.inc(v.tensor_tensor(bbm_rep[:, :], bbm_rep[:, :], oml_rep[:, :],
                                          ALU.mult))

            r_x3 = DmaRing(c, "x3r", 4)
            r_st = DmaRing(c, "str", 4)
            pre = PrelnRing(c, "ln", x_h, ident, s_ident, pbA, pbB, nslots=2)
            pre.emit_stats(block)
            pre.emit_x2(block, consumer=lambda m: [(pre.s_ln.h, m + 1)])

            def _ph2(t, tc):
                t.wait_ge(s_scan.h, 8 * (tc + 1))
                for m in range(4 * tc, 4 * tc + 4):
                    for chn in range(2):
                        j = 2 * m + chn
                        if j >= 2:
                            t.wait_ge(s_zc.h, j - 1)
                        pf = pf2 if j % 2 == 0 else pf3
                        for a in range(NKD):
                            ins = t.matmul(
                                pf[:, :],
                                hbuf[:, (tc % 2) * 4096 + a * 512 + (m % 4) * 128:
                                     (tc % 2) * 4096 + a * 512 + (m % 4) * 128 + 128],
                                wcT[:, a * D + chn * 512:a * D + (chn + 1) * 512],
                                start=(a == 0), stop=(a == NKD - 1))
                        s_mm2.inc(ins)

            @block.tensor
            def _(t):
                t.wait_ge(s_wb.h, s_wb.n)
                t.wait_ge(s_wc.h, s_wc.n)
                t.wait_ge(s_betrep.h, s_betrep.n)
                for tc in range(8):
                    for m in range(4 * tc, 4 * tc + 4):
                        pre.transposes(t, m)
                        t.wait_ge(pre.s_tile.h, m + 1)
                        for chn in range(2):
                            i = 2 * m + chn
                            if i >= 2:
                                t.wait_ge(s_val.h, i - 1)
                            pf = pf0 if i % 2 == 0 else pf1
                            for k in range(NKD):
                                ins = t.matmul(pf[:, :], pre.lnT_tile(k, m),
                                               wbT[:, k * D + chn * 512:k * D + (chn + 1) * 512],
                                               start=(k == 0), stop=(k == NKD - 1))
                            s_mm1.inc(ins)
                        t.wait_ge(s_val.h, 2 * (m + 1))
                        if m >= 2:
                            t.wait_ge(s_vte.h, m - 1)
                        pv = pbC if m % 2 == 0 else pbD
                        for a in range(NKD):
                            ins = t.transpose(pv[:, a * 128:(a + 1) * 128],
                                              valm[:, (m % 2) * D + a * 128:(m % 2) * D + (a + 1) * 128],
                                              ident[:, :])
                        s_vtr.inc(ins)
                    if tc >= 1:
                        _ph2(t, tc - 1)
                _ph2(t, 7)

            def _zpipe(v, tc):
                sl4 = 4 * tc
                for m in range(4 * tc, 4 * tc + 4):
                    for chn in range(2):
                        j = 2 * m + chn
                        v.wait_ge(s_mm2.h, j + 1)
                        if m >= 4 and chn == 0:
                            v.wait_ge(s_zsq.h, m - 3)
                        pf = pf2 if j % 2 == 0 else pf3
                        sl = (m % 4) * D + chn * 512
                        xsl = (m % 4) * D + chn * 512
                        r_x3.wait(v, m)
                        s_fen.inc(v.tensor_scalar(zb[:, sl:sl + 512], x3[:, xsl:xsl + 512],
                                                  pre.mu[:, m:m + 1], pre.rstd[:, m:m + 1],
                                                  ALU.subtract, ALU.mult))
                        v.wait_ge(s_fen.h, s_fen.n)
                        s_zc.inc(v.tensor_tensor(zb[:, sl:sl + 512], zb[:, sl:sl + 512],
                                                 pf[:, :], ALU.add))
                        v.wait_ge(s_zc.h, s_zc.n)
                        s_fen.inc(v.tensor_tensor(zb[:, sl:sl + 512], zb[:, sl:sl + 512],
                                                  bc_rep[:, chn * 512:chn * 512 + 512],
                                                  ALU.add))
                        v.wait_ge(s_fen.h, s_fen.n)
                    s_zred.inc(v.reduce_sum(zs[:, m:m + 1],
                                            zb[:, (m % 4) * D:(m % 4 + 1) * D],
                                            axis=AX.X))
                v.wait_ge(s_zred.h, 4 * (tc + 1))
                v.wait_ge(s_zsq.h, 4 * (tc + 1))
                v.tensor_scalar_mul(zmu[:, sl4:sl4 + 4], zs[:, sl4:sl4 + 4], 1.0 / D)
                s_fen.inc(v.tensor_scalar_mul(zrstd[:, sl4:sl4 + 4],
                                              zq[:, sl4:sl4 + 4], 1.0 / D))
                v.wait_ge(s_fen.h, s_fen.n)
                s_fen.inc(v.tensor_tensor(zq[:, sl4:sl4 + 4], zmu[:, sl4:sl4 + 4],
                                          zmu[:, sl4:sl4 + 4], ALU.mult))
                v.wait_ge(s_fen.h, s_fen.n)
                s_zv.inc(v.tensor_tensor(zrstd[:, sl4:sl4 + 4], zrstd[:, sl4:sl4 + 4],
                                         zq[:, sl4:sl4 + 4], ALU.subtract))
                v.wait_ge(s_zq2.h, tc + 1)
                s_fen.inc(v.reciprocal(zrstd[:, sl4:sl4 + 4], zrstd[:, sl4:sl4 + 4]))
                v.wait_ge(s_fen.h, s_fen.n)
                for m in range(4 * tc, 4 * tc + 4):
                    if m >= 4:
                        r_st.wait(v, m - 4)
                    v.wait_ge(s_y.h, s_y.n)
                    s_fen.inc(v.tensor_scalar(zlnb[:, :], zb[:, (m % 4) * D:(m % 4 + 1) * D],
                                              zmu[:, m:m + 1], zrstd[:, m:m + 1],
                                              ALU.subtract, ALU.mult))
                    v.wait_ge(s_fen.h, s_fen.n)
                    s_fen.inc(v.tensor_tensor(zlnb[:, :], zlnb[:, :], gam_rep[:, :],
                                              ALU.mult))
                    v.wait_ge(s_fen.h, s_fen.n)
                    s_fen.inc(v.tensor_tensor(zlnb[:, :], zlnb[:, :], bet_rep[:, :],
                                              ALU.add))
                    v.wait_ge(s_fen.h, s_fen.n)
                    s_y.inc(v.tensor_tensor(yst[:, (m % 4) * D:(m % 4 + 1) * D],
                                            x3[:, (m % 4) * D:(m % 4 + 1) * D],
                                            zlnb[:, :], ALU.add))

            @block.vector
            def _(v):
                v.wait_ge(s_omlrep.h, s_omlrep.n)
                v.wait_ge(s_bbmrep.h, s_bbmrep.n)
                v.wait_ge(s_bcrep.h, s_bcrep.n)
                v.wait_ge(s_gamrep.h, s_gamrep.n)
                v.wait_ge(s_betrep.h, s_betrep.n)
                for tc in range(8):
                    for m in range(4 * tc, 4 * tc + 4):
                        pre.apply(v, m)
                        for chn in range(2):
                            i = 2 * m + chn
                            v.wait_ge(s_mm1.h, i + 1)
                            pf = pf0 if i % 2 == 0 else pf1
                            sl = (m % 2) * D + chn * 512
                            v.wait_ge(s_y.h, s_y.n)
                            s_fen.inc(v.tensor_tensor(zlnb[:, chn * 512:(chn + 1) * 512],
                                                      pf[:, :],
                                                      oml_rep[:, chn * 512:(chn + 1) * 512],
                                                      ALU.mult))
                            v.wait_ge(s_fen.h, s_fen.n)
                            s_val.inc(v.tensor_tensor(valm[:, sl:sl + 512],
                                                      zlnb[:, chn * 512:(chn + 1) * 512],
                                                      bbm_rep[:, chn * 512:(chn + 1) * 512],
                                                      ALU.add))
                    v.wait_ge(s_vte.h, 4 * (tc + 1))
                    if tc >= 2:
                        v.wait_ge(s_mm2.h, 8 * (tc - 1))
                    v.wait_ge(s_scan.h, 8 * tc)
                    for a in range(NKD):
                        init = 0.0 if tc == 0 else hbuf[:, ((tc - 1) % 2) * 4096 + a * 512 + 511:
                                                        ((tc - 1) % 2) * 4096 + a * 512 + 512]
                        ins = v.tensor_tensor_scan(
                            hbuf[:, (tc % 2) * 4096 + a * 512:(tc % 2) * 4096 + (a + 1) * 512],
                            lam8[:, a * 512:(a + 1) * 512],
                            valT[:, (tc % 2) * 4096 + a * 512:(tc % 2) * 4096 + (a + 1) * 512],
                            init, ALU.mult, ALU.add)
                        s_scan.inc(ins)
                    if tc >= 1:
                        _zpipe(v, tc - 1)
                _zpipe(v, 7)

            def _zstats(s, tc):
                for m in range(4 * tc, 4 * tc + 4):
                    s.wait_ge(s_zred.h, m + 1)
                    if m == 0:
                        s.wait_ge(pre.s_sqd.h, NTT)
                    else:
                        s.wait_ge(s_zsq.h, m)
                    s_zsq.inc(s.activation(pre.sq[:, :],
                                           zb[:, (m % 4) * D:(m % 4 + 1) * D],
                                           AF.Square, accum_out=zq[:, m:m + 1]))
                s.wait_ge(s_zv.h, tc + 1)
                s_zq2.inc(s.activation(zrstd[:, 4 * tc:4 * tc + 4],
                                       zrstd[:, 4 * tc:4 * tc + 4],
                                       AF.Sqrt, bias=epsz[:, 0:1]))

            @block.scalar
            def _(s):
                for tc in range(8):
                    for m in range(4 * tc, 4 * tc + 4):
                        pre.evac(s, m)
                        s.wait_ge(s_vtr.h, m + 1)
                        if tc >= 2:
                            s.wait_ge(s_scan.h, 8 * (tc - 1))
                        pv = pbC if m % 2 == 0 else pbD
                        navail = (2 * 8 * 512 - ((tc % 2) * 4096 + (m % 4) * 128)) // 128
                        dst = valT[:, (tc % 2) * 4096 + (m % 4) * 128:].rearrange(
                            "p (a q) -> p a q", q=128)[:, 0:4 * 7 + 1:4, :]
                        s_vte.inc(s.copy(dst,
                                         pv[:, :].rearrange("p (a q) -> p a q", q=128)))
                    if tc >= 1:
                        _zstats(s, tc - 1)
                _zstats(s, 7)

            @block.gpsimd
            def _(g):
                xr3 = x_h.rearrange("(m p) d -> m p d", p=128)
                yr = y_h.rearrange("(m p) d -> m p d", p=128)
                for m in range(NTT):
                    if m >= 4:
                        g.wait_ge(s_y.h, m - 3)
                    r_x3.load(g, m, x3[:, (m % 4) * D:(m % 4 + 1) * D], xr3[m])
                    if m >= 3:
                        g.wait_ge(s_y.h, m - 2)
                        r_st.load(g, m - 3, yr[m - 3],
                                  yst[:, ((m - 3) % 4) * D:((m - 3) % 4 + 1) * D])
                for m in (NTT - 3, NTT - 2, NTT - 1):
                    g.wait_ge(s_y.h, m + 1)
                    r_st.load(g, m, yr[m], yst[:, (m % 4) * D:(m % 4 + 1) * D])
                for s_ in r_st.sems:
                    g.wait_ge(s_.h, s_.n)
                if debug:
                    for nm, src_sb in [("d_oml", oml_rep), ("d_bbm", bbm_rep),
                                       ("d_lam8", lam8), ("d_hbuf", hbuf),
                                       ("d_valm", valm), ("d_zb", zb),
                                       ("d_zmu", zmu), ("d_zrstd", zrstd),
                                       ("d_valT", valT), ("d_lnTr", pre.lnTr)]:
                        s_st.inc(g.dma_start(dbg[nm], src_sb[:, :]), 16)
                    g.wait_ge(s_st.h, s_st.n)
    return nc



def build_l3ffn():
    FF = 4096
    NKF = FF // 128
    E = 8
    SG = 4
    NSG = NTT // SG
    TOKSG = SG * 128

    nc = bass.Bass("TRN2", target_bir_lowering=False)
    x_h = nc.dram_tensor("x", [T, D], F32, kind="ExternalInput").ap()
    wr_h = nc.dram_tensor("Wr", [E, D], F32, kind="ExternalInput").ap()
    w1_h = nc.dram_tensor("W1", [E, FF, D], F32, kind="ExternalInput").ap()
    b1_h = nc.dram_tensor("b1", [E, FF], F32, kind="ExternalInput").ap()
    w2_h = nc.dram_tensor("W2", [E, D, FF], F32, kind="ExternalInput").ap()
    b2_h = nc.dram_tensor("b2", [E, D], F32, kind="ExternalInput").ap()
    y_h = nc.dram_tensor("y", [T, D], F32, kind="ExternalOutput").ap()

    with ExitStack() as ctx:
        c = Ctx(nc, ctx)
        hT = c.sb("hT", [128, NKF * TOKSG], BF16)
        w1stg = c.sb("w1stg", [128, 2 * D], BF16)
        w1T = c.sb("w1T", [128, 2 * NKD * 128], BF16)
        w2stg = c.sb("w2stg", [128, FF], BF16)
        w2dpT = c.sb("w2dpT", [128, NKF * 128], BF16)
        oaT = c.sb("oaT", [128, NKD * TOKSG], F32)
        yst = c.sb("yst", [128, 2 * D], F32)
        gates = c.sb("gates", [128, NTT * E], F32)
        gateT = c.sb("gateT", [8, TOKSG], BF16)
        growT = c.sb("growT", [1, TOKSG], F32)
        grep = c.sb("grep", [128, TOKSG], F32)
        onesf = c.sb("onesf", [1, 128], F32)
        wrTf = c.sb("wrTf", [128, NKD * E], F32)
        wrstgf = c.sb("wrstgf", [8, D], F32)
        srep = c.sb("srep", [128, E], F32)
        scol = c.sb("scol", [8, 1], F32)
        xTf = c.sb("xTf", [128, D], F32)
        rlog = c.sb("rlog", [128, E], F32)
        b2all = c.sb("b2all", [8, D], BF16)
        b1c = c.sb("b1c", [128, E * NKF], F32)
        rmx = c.sb("rmx", [128, 1], F32)
        rsum = c.sb("rsum", [128, 1], F32)
        rprob = c.sb("rprob", [128, E], F32)
        rp2 = c.sb("rp2", [128, E], F32)
        rmsk = c.sb("rmsk", [128, E], F32)
        rmsk2 = c.sb("rmsk2", [128, E], F32)
        rm1 = c.sb("rm1", [128, 1], F32)
        rm2 = c.sb("rm2", [128, 1], F32)

        pbA = c.ps("pbA", [128, 1024], BF16)
        pbB = c.ps("pbB", [128, 1024], BF16)
        pf0 = c.ps("pf0", [128, 512], F32)
        pf1 = c.ps("pf1", [128, 512], F32)
        pf2 = c.ps("pf2", [128, 512], F32)
        pf3 = c.ps("pf3", [128, 512], F32)

        s_ones = c.sem("s_ones")
        s_wr = c.sem("s_wr")
        s_cst = c.sem("s_cst")
        s_rmx = c.sem("s_rmx")
        s_rexp = c.sem("s_rexp")
        s_gate = c.sem("s_gate")
        s_gT = c.sem("s_gT")
        s_gTe = c.sem("s_gTe")
        s_b2i = c.sem("s_b2i")
        s_b2e = c.sem("s_b2e")
        s_w1t = c.sem("s_w1t")
        s_w1e = c.sem("s_w1e")
        s_hmm = c.sem("s_hmm")
        s_hev = c.sem("s_hev")
        s_hg = c.sem("s_hg")
        s_grw = c.sem("s_grw")
        s_gre = c.sem("s_gre")
        s_w2t = c.sem("s_w2t")
        s_w2e = c.sem("s_w2e")
        s_ymm = c.sem("s_ymm")
        s_yac = c.sem("s_yac")
        s_ftr = c.sem("s_ftr")
        s_yfc = c.sem("s_yfc")
        s_y = c.sem("s_y")
        s_fen = c.sem("s_fen")
        r_w1 = DmaRing(c, "w1r", 2)
        r_w2 = DmaRing(c, "w2r", 1)
        r_st = DmaRing(c, "str", 2)

        w1r = w1_h.rearrange("e (a p) q -> e a p q", p=128)
        w2r = w2_h.rearrange("e (dp p) q -> e dp p q", p=128)
        b1r = b1_h.rearrange("e (a p) -> e p a", p=128)
        yr = y_h.rearrange("(m p) d -> m p d", p=128)

        with nc.Block() as block:
            ident, identf, s_ident = emit_identity(c, block, with_f32=True)

            @block.vector
            def _(v):
                s_ones.inc(v.memset(onesf[:, :], 1.0))

            @block.gpsimd
            def _(g):
                s_cst.inc(g.dma_start(b2all[:, :], b2_h), 16)
                s_cst.inc(g.dma_start(wrstgf[:, :], wr_h), 16)

                for e in range(E):
                    s_cst.inc(g.dma_start(b1c[:, e * NKF:(e + 1) * NKF], b1r[e],
                                          allow_slow_non_contiguous=True), 16)

            @block.vector
            def _(v):
                v.wait_ge(s_cst.h, s_cst.n)
                s_wr.inc(v.reduce_sum(scol[:, :], wrstgf[:, :], axis=AX.X))

            @block.tensor
            def _(t):
                t.wait_ge(s_ident.h, s_ident.n)
                t.wait_ge(s_wr.h, s_wr.n)
                for k in range(NKD):
                    ins = t.transpose(pf2[:, k * E:(k + 1) * E],
                                      wrstgf[:, k * 128:(k + 1) * 128], identf[0:8, 0:8])
                s_wr.inc(ins)
                s_wr.inc(t.transpose(pf3[0:1, 0:8], scol[:, :], identf[0:8, 0:8]))

            @block.vector
            def _(v):
                v.wait_ge(s_wr.h, s_wr.n)
                s_wr.inc(v.tensor_copy(wrTf[:, :], pf2[:, 0:NKD * E]))
                s_wr.inc(v.tensor_copy(growT[:, 0:8], pf3[0:1, 0:8]))

            @block.tensor
            def _(t):
                t.wait_ge(s_wr.h, s_wr.n)
                t.wait_ge(s_ones.h, s_ones.n)
                s_wr.inc(t.matmul(pf2[:, 0:8], onesf[:, 0:128], growT[:, 0:8],
                                  start=True, stop=True))

            @block.vector
            def _(v):
                v.wait_ge(s_wr.h, s_wr.n)
                s_wr.inc(v.tensor_copy(srep[:, :], pf2[:, 0:8]))

            pre = PrelnRing(c, "ln", x_h, ident, s_ident, pbA, pbB, nslots=SG, nx=8)
            pre.emit_stats(block)
            pre.emit_x2(block, consumer=lambda m: [(s_y.h, m + 1)])

            for sg in range(NSG):
                t0 = sg * SG
                for m in range(t0, t0 + SG):
                    @block.tensor
                    def _(t, m=m):
                        if m == 0:
                            t.wait_ge(s_wr.h, s_wr.n)
                        pre.transposes(t, m)
                        if m % SG == 0:
                            t.wait_ge(s_yfc.h, s_yfc.n)
                        t.wait_ge(s_rmx.h, s_rmx.n)
                        t.wait_ge(s_gTe.h, s_gTe.n)
                        pre.r_x2.wait(t, m)
                        for k in range(NKD):
                            pfx = pf2 if k < 4 else pf3
                            ins = t.transpose(
                                pfx[:, (k % 4) * 128:(k % 4 + 1) * 128],
                                pre.xring[:, (m % pre.nx) * D + k * 128:(m % pre.nx) * D + (k + 1) * 128],
                                identf[:, :])
                        s_rmx.inc(ins)

                    @block.scalar
                    def _(s, m=m):
                        pre.evac(s, m)
                        s.wait_ge(s_rmx.h, s_rmx.n)
                        s.copy(xTf[:, 0:512], pf2[:, :])
                        s_rexp.inc(s.copy(xTf[:, 512:1024], pf3[:, :]))

                    @block.tensor
                    def _(t, m=m):
                        t.wait_ge(s_rexp.h, s_rexp.n)
                        for k in range(NKD):
                            ins = t.matmul(pf2[0:128, 0:E], xTf[:, k * 128:(k + 1) * 128],
                                           wrTf[:, k * E:(k + 1) * E],
                                           start=(k == 0), stop=(k == NKD - 1))
                        s_rmx.inc(ins)

                    @block.vector
                    def _(v, m=m):
                        pre.apply(v, m)
                        v.wait_ge(s_rmx.h, s_rmx.n)

                        def F(ins):
                            s_fen.inc(ins)
                            v.wait_ge(s_fen.h, s_fen.n)
                        F(v.scalar_tensor_tensor(rlog[:, :], srep[:, :],
                                                 pre.mu[:, m:m + 1], pf2[:, 0:E],
                                                 ALU.mult, ALU.subtract))
                        F(v.tensor_scalar(rlog[:, :], rlog[:, :],
                                          pre.rstd[:, m:m + 1], -1.0,
                                          ALU.mult, ALU.mult))
                        F(v.reduce_max(rmx[:, :], rlog[:, :], axis=AX.X, negate=True))
                        s_rmx.inc(v.engine_nop())

                    @block.scalar
                    def _(s, m=m):
                        s.wait_ge(s_rmx.h, s_rmx.n)
                        if m >= 1:
                            s.wait_ge(s_gate.h, m)
                        s_rexp.inc(s.activation(rprob[:, :], rlog[:, :], AF.Exp,
                                                bias=rmx[:, 0:1],
                                                accum_out=rsum[:, 0:1]))

                    @block.vector
                    def _(v, m=m):
                        v.wait_ge(s_rexp.h, s_rexp.n)

                        def F(ins):
                            s_fen.inc(ins)
                            v.wait_ge(s_fen.h, s_fen.n)
                        F(v.reciprocal(rsum[:, :], rsum[:, :]))
                        F(v.tensor_scalar_mul(rprob[:, :], rprob[:, :], rsum[:, 0:1]))
                        F(v.reduce_max(rm1[:, :], rprob[:, :], axis=AX.X))
                        F(v.tensor_single_scalar(rmsk[:, :], rprob[:, :], rm1[:, 0:1],
                                                 ALU.is_ge))
                        F(v.scalar_tensor_tensor(rp2[:, :], rmsk[:, :], -1e30,
                                                 rprob[:, :], ALU.mult, ALU.add))
                        F(v.reduce_max(rm2[:, :], rp2[:, :], axis=AX.X))
                        F(v.tensor_single_scalar(rmsk2[:, :], rp2[:, :], rm2[:, 0:1],
                                                 ALU.is_ge))
                        F(v.tensor_tensor(rm2[:, :], rm1[:, :], rm2[:, :], ALU.add))
                        F(v.reciprocal(rm2[:, :], rm2[:, :]))
                        F(v.tensor_tensor(rmsk[:, :], rmsk[:, :], rmsk2[:, :], ALU.add))
                        F(v.tensor_tensor(rmsk[:, :], rmsk[:, :], rprob[:, :], ALU.mult))
                        s_gate.inc(v.tensor_single_scalar(gates[:, m * E:(m + 1) * E],
                                                          rmsk[:, :], rm2[:, 0:1],
                                                          ALU.mult))

                    @block.tensor
                    def _(t, m=m, sg=sg):
                        t.wait_ge(s_gate.h, s_gate.n)
                        if sg >= 1:
                            t.wait_ge(s_b2i.h, NKD * sg)
                        t.wait_ge(s_gTe.h, s_gTe.n - (1 if m % SG else 0) + 0)
                        s_gT.inc(t.transpose(pf3[0:E, 0:128],
                                             gates[:, m * E:(m + 1) * E], identf[:, :]))

                    @block.scalar
                    def _(s, m=m, t0=t0):
                        s.wait_ge(s_gT.h, s_gT.n)
                        s_gTe.inc(s.copy(gateT[:, (m - t0) * 128:(m - t0 + 1) * 128],
                                         pf3[0:E, 0:128]))

                for dp in range(NKD):
                    @block.tensor
                    def _(t, dp=dp, sg=sg, t0=t0):
                        t.wait_ge(s_gTe.h, s_gTe.n)
                        if dp == 0:
                            t.wait_ge(s_y.h, t0)
                        if dp >= 2:
                            t.wait_ge(s_b2e.h, s_b2i.n - 1)
                        pfb = pf2 if dp % 2 == 0 else pf3
                        s_b2i.inc(t.matmul(pfb[:, :],
                                           b2all[:, dp * 128:(dp + 1) * 128],
                                           gateT[:, 0:TOKSG],
                                           start=True, stop=True))

                    @block.vector
                    def _(v, dp=dp):
                        v.wait_ge(s_b2i.h, s_b2i.n)
                        pfb = pf2 if dp % 2 == 0 else pf3
                        s_b2e.inc(v.tensor_copy(
                            oaT[:, dp * TOKSG:(dp + 1) * TOKSG], pfb[:, 0:TOKSG]))

                for e in range(E):
                    bw1t, bw1e, bhmm, bhev = s_w1t.n, s_w1e.n, s_hmm.n, s_hev.n
                    for a in range(NKF):
                        @block.gpsimd
                        def _(g, a=a, e=e, sg=sg):
                            if a >= 2:
                                g.wait_ge(s_w1t.h, bw1t + a - 1)
                            r_w1.load(g, sg * E * NKF + e * NKF + a,
                                      w1stg[:, (a % 2) * D:(a % 2 + 1) * D],
                                      w1r[e, a])

                        @block.tensor
                        def _(t, a=a, e=e, sg=sg):
                            r_w1.wait(t, sg * E * NKF + e * NKF + a)
                            if a >= 2:
                                t.wait_ge(s_w1e.h, bw1e + a - 1)
                            pb = pbA if a % 2 == 0 else pbB
                            for k in range(NKD):
                                ins = t.transpose(
                                    pb[:, k * 128:(k + 1) * 128],
                                    w1stg[:, (a % 2) * D + k * 128:(a % 2) * D + (k + 1) * 128],
                                    ident[:, :])
                            s_w1t.inc(ins)

                        @block.vector
                        def _(v, a=a):
                            v.wait_ge(s_w1t.h, bw1t + a + 1)
                            if a >= 2:
                                v.wait_ge(s_hmm.h, bhmm + a - 1)
                            pb = pbA if a % 2 == 0 else pbB
                            s_w1e.inc(v.tensor_copy(
                                w1T[:, (a % 2) * 1024:(a % 2 + 1) * 1024], pb[:, 0:1024]))

                        @block.tensor
                        def _(t, a=a, t0=t0):
                            t.wait_ge(s_w1e.h, bw1e + a + 1)
                            if a >= 2:
                                t.wait_ge(s_hev.h, bhev + a - 1)
                            pfh = pf0 if a % 2 == 0 else pf1
                            for k in range(NKD):
                                ins = t.matmul(pfh[:, :],
                                               w1T[:, (a % 2) * 1024 + k * 128:(a % 2) * 1024 + (k + 1) * 128],
                                               pre.lnT_chunk(k, t0),
                                               start=(k == 0), stop=(k == NKD - 1))
                            s_hmm.inc(ins)

                        @block.scalar
                        def _(s, a=a, e=e):
                            s.wait_ge(s_hmm.h, bhmm + a + 1)
                            pfh = pf0 if a % 2 == 0 else pf1
                            s_hev.inc(s.activation(
                                hT[:, a * TOKSG:(a + 1) * TOKSG],
                                pfh[:, :], AF.Gelu,
                                bias=b1c[:, e * NKF + a:e * NKF + a + 1]))

                    @block.tensor
                    def _(t, e=e, t0=t0, sg=sg):
                        t.wait_ge(s_b2e.h, s_b2e.n)
                        if sg + e > 0:
                            t.wait_ge(s_gre.h, s_gre.n)
                        for mm in range(SG):
                            ins = t.transpose(
                                pf2[0:1, mm * 128:(mm + 1) * 128],
                                gates[:, (t0 + mm) * E + e:(t0 + mm) * E + e + 1],
                                identf[:, :])
                        s_grw.inc(ins)

                    @block.vector
                    def _(v, e=e):
                        v.wait_ge(s_grw.h, s_grw.n)
                        s_grw.inc(v.tensor_copy(growT[:, 0:TOKSG], pf2[0:1, 0:TOKSG]))

                    @block.tensor
                    def _(t):
                        t.wait_ge(s_grw.h, s_grw.n)
                        t.wait_ge(s_ones.h, s_ones.n)
                        s_grw.inc(t.matmul(pf3[:, :], onesf[:, 0:128],
                                           growT[:, 0:TOKSG], start=True, stop=True))

                    @block.vector
                    def _(v, e=e):
                        v.wait_ge(s_grw.h, s_grw.n)
                        s_gre.inc(v.tensor_copy(grep[:, 0:TOKSG], pf3[:, 0:TOKSG]))
                        v.wait_ge(s_gre.h, s_gre.n)
                        v.wait_ge(s_hev.h, s_hev.n)
                        for kf in range(NKF):
                            ins = v.tensor_tensor(hT[:, kf * TOKSG:(kf + 1) * TOKSG],
                                                  hT[:, kf * TOKSG:(kf + 1) * TOKSG],
                                                  grep[:, 0:TOKSG], ALU.mult)
                        s_hg.inc(ins)

                    bw2t, bw2e, bymm, byac = s_w2t.n, s_w2e.n, s_ymm.n, s_yac.n
                    for dp in range(NKD):
                        @block.gpsimd
                        def _(g, dp=dp, e=e, sg=sg):
                            if dp >= 1:
                                g.wait_ge(s_w2t.h, bw2t + 4 * dp)
                            r_w2.load(g, (sg * E + e) * NKD + dp, w2stg[:, :],
                                      w2r[e, dp])

                        @block.tensor
                        def _(t, dp=dp, e=e, sg=sg):
                            r_w2.wait(t, (sg * E + e) * NKD + dp)
                            for bt in range(4):
                                ii = dp * 4 + bt
                                if ii >= 2:
                                    t.wait_ge(s_w2e.h, bw2e + ii - 1)
                                pb = pbA if ii % 2 == 0 else pbB
                                for kk in range(8):
                                    ins = t.transpose(
                                        pb[:, kk * 128:(kk + 1) * 128],
                                        w2stg[:, (bt * 8 + kk) * 128:(bt * 8 + kk + 1) * 128],
                                        ident[:, :])
                                s_w2t.inc(ins)

                        @block.vector
                        def _(v, dp=dp):
                            for bt in range(4):
                                ii = dp * 4 + bt
                                v.wait_ge(s_w2t.h, bw2t + ii + 1)
                                if dp >= 1 and bt == 0:
                                    v.wait_ge(s_ymm.h, bymm + dp)
                                pb = pbA if ii % 2 == 0 else pbB
                                s_w2e.inc(v.tensor_copy(
                                    w2dpT[:, bt * 1024:(bt + 1) * 1024], pb[:, 0:1024]))

                        @block.tensor
                        def _(t, dp=dp):
                            t.wait_ge(s_w2e.h, bw2e + 4 * (dp + 1))
                            t.wait_ge(s_hg.h, s_hg.n)
                            if dp >= 2:
                                t.wait_ge(s_yac.h, byac + dp - 1)
                            pfy = pf0 if dp % 2 == 0 else pf1
                            for kf in range(NKF):
                                ins = t.matmul(
                                    pfy[:, :],
                                    w2dpT[:, kf * 128:(kf + 1) * 128],
                                    hT[:, kf * TOKSG:kf * TOKSG + TOKSG],
                                    start=(kf == 0), stop=(kf == NKF - 1))
                            s_ymm.inc(ins)

                        @block.vector
                        def _(v, dp=dp):
                            v.wait_ge(s_ymm.h, bymm + dp + 1)
                            pfy = pf0 if dp % 2 == 0 else pf1
                            sl = dp * TOKSG
                            s_yac.inc(v.tensor_tensor(oaT[:, sl:sl + TOKSG],
                                                      oaT[:, sl:sl + TOKSG],
                                                      pfy[:, 0:TOKSG], ALU.add))

                for mm in range(SG):
                    m = t0 + mm

                    @block.tensor
                    def _(t, m=m, mm=mm, sg=sg):
                        t.wait_ge(s_yac.h, s_yac.n)
                        for dp in range(NKD):
                            g = (sg * SG + mm) * NKD + dp
                            if g >= 2:
                                t.wait_ge(s_yfc.h, g - 1)
                            pff = pf2 if dp % 2 == 0 else pf3
                            s_ftr.inc(t.transpose(
                                pff[:, 0:128],
                                oaT[:, dp * TOKSG + mm * 128:dp * TOKSG + (mm + 1) * 128],
                                identf[:, :]))

                    @block.vector
                    def _(v, m=m, mm=mm):
                        for dp in range(NKD):
                            v.wait_ge(s_ftr.h, s_ftr.n - NKD + dp + 1)
                            if dp == 0 and m >= 2:
                                r_st.wait(v, m - 2)
                            pre.r_x2.wait(v, m)
                            pff = pf2 if dp % 2 == 0 else pf3
                            ins = v.tensor_tensor(
                                yst[:, (m % 2) * D + dp * 128:(m % 2) * D + (dp + 1) * 128],
                                pre.xring[:, (m % pre.nx) * D + dp * 128:(m % pre.nx) * D + (dp + 1) * 128],
                                pff[:, 0:128], ALU.add)
                            s_yfc.inc(ins)
                        s_y.inc(v.engine_nop())

                    @block.gpsimd
                    def _(g, m=m):
                        g.wait_ge(s_y.h, s_y.n)
                        r_st.load(g, m, yr[m], yst[:, (m % 2) * D:(m % 2 + 1) * D])

            @block.gpsimd
            def _(g):
                for s_ in r_st.sems:
                    g.wait_ge(s_.h, s_.n)
    return nc



def build_l1ffn_router():
    E = 8
    nc = bass.Bass("TRN2", target_bir_lowering=False)
    x_h = nc.dram_tensor("x", [T, D], F32, kind="ExternalInput").ap()
    wr_h = nc.dram_tensor("Wr", [E, D], F32, kind="ExternalInput").ap()
    g_h = nc.dram_tensor("gates", [T, E], F32, kind="ExternalOutput").ap()

    with ExitStack() as ctx:
        c = Ctx(nc, ctx)
        onesf = c.sb("onesf", [1, 128], F32)
        wrTf = c.sb("wrTf", [128, NKD * E], F32)
        wrstgf = c.sb("wrstgf", [8, D], F32)
        srep = c.sb("srep", [128, E], F32)
        scol = c.sb("scol", [8, 1], F32)
        srow = c.sb("srow", [1, E], F32)
        xTf = c.sb("xTf", [128, D], F32)
        rlog = c.sb("rlog", [128, E], F32)
        rmx = c.sb("rmx", [128, 1], F32)
        rsum = c.sb("rsum", [128, 1], F32)
        rprob = c.sb("rprob", [128, E], F32)
        rp2 = c.sb("rp2", [128, E], F32)
        rmsk = c.sb("rmsk", [128, E], F32)
        rmsk2 = c.sb("rmsk2", [128, E], F32)
        rm1 = c.sb("rm1", [128, 1], F32)
        rm2 = c.sb("rm2", [128, 1], F32)
        gout = c.sb("gout", [128, 2 * E], F32)
        pf2 = c.ps("pf2", [128, 512], F32)
        pf3 = c.ps("pf3", [128, 512], F32)
        pbA = c.ps("pbA", [128, 1024], BF16)
        pbB = c.ps("pbB", [128, 1024], BF16)

        s_ones = c.sem("s_ones")
        s_cst = c.sem("s_cst")
        s_wr = c.sem("s_wr")
        s_rmx = c.sem("s_rmx")
        s_rexp = c.sem("s_rexp")
        s_gate = c.sem("s_gate")
        s_fen = c.sem("s_fen")
        s_st = c.sem("s_st")
        r_st = DmaRing(c, "str", 2)
        gr = g_h.rearrange("(m p) e -> m p e", p=128)

        with nc.Block() as block:
            ident, identf, s_ident = emit_identity(c, block, with_f32=True)

            @block.vector
            def _(v):
                s_ones.inc(v.memset(onesf[:, :], 1.0))

            @block.gpsimd
            def _(g):
                s_cst.inc(g.dma_start(wrstgf[:, :], wr_h), 16)

            @block.vector
            def _(v):
                v.wait_ge(s_cst.h, s_cst.n)
                s_wr.inc(v.reduce_sum(scol[:, :], wrstgf[:, :], axis=AX.X))

            @block.tensor
            def _(t):
                t.wait_ge(s_ident.h, s_ident.n)
                t.wait_ge(s_wr.h, s_wr.n)
                for k in range(NKD):
                    ins = t.transpose(pf2[:, k * E:(k + 1) * E],
                                      wrstgf[:, k * 128:(k + 1) * 128], identf[0:8, 0:8])
                s_wr.inc(ins)
                s_wr.inc(t.transpose(pf3[0:1, 0:8], scol[:, :], identf[0:8, 0:8]))

            @block.vector
            def _(v):
                v.wait_ge(s_wr.h, s_wr.n)
                s_wr.inc(v.tensor_copy(wrTf[:, :], pf2[:, 0:NKD * E]))
                s_wr.inc(v.tensor_copy(srow[:, :], pf3[0:1, 0:8]))

            @block.tensor
            def _(t):
                t.wait_ge(s_wr.h, s_wr.n)
                t.wait_ge(s_ones.h, s_ones.n)
                s_wr.inc(t.matmul(pf2[:, 0:8], onesf[:, 0:128], srow[:, :],
                                  start=True, stop=True))

            @block.vector
            def _(v):
                v.wait_ge(s_wr.h, s_wr.n)
                s_wr.inc(v.tensor_copy(srep[:, :], pf2[:, 0:8]))

            pre = PrelnRing(c, "ln", x_h, ident, s_ident, pbA, pbB, nslots=2, nx=4)
            pre.emit_stats(block)
            pre.emit_x2(block, consumer=lambda m: [(s_gate.h, m + 1)])

            for m in range(NTT):
                @block.tensor
                def _(t, m=m):
                    if m == 0:
                        t.wait_ge(s_wr.h, s_wr.n)
                    t.wait_ge(s_rmx.h, s_rmx.n)
                    pre.r_x2.wait(t, m)
                    for k in range(NKD):
                        pfx = pf2 if k < 4 else pf3
                        ins = t.transpose(
                            pfx[:, (k % 4) * 128:(k % 4 + 1) * 128],
                            pre.xring[:, (m % pre.nx) * D + k * 128:(m % pre.nx) * D + (k + 1) * 128],
                            identf[:, :])
                    s_rmx.inc(ins)

                @block.scalar
                def _(s, m=m):
                    s.wait_ge(s_rmx.h, s_rmx.n)
                    s.copy(xTf[:, 0:512], pf2[:, :])
                    s_rexp.inc(s.copy(xTf[:, 512:1024], pf3[:, :]))

                @block.tensor
                def _(t, m=m):
                    t.wait_ge(s_rexp.h, s_rexp.n)
                    for k in range(NKD):
                        ins = t.matmul(pf2[0:128, 0:E], xTf[:, k * 128:(k + 1) * 128],
                                       wrTf[:, k * E:(k + 1) * E],
                                       start=(k == 0), stop=(k == NKD - 1))
                    s_rmx.inc(ins)

                @block.vector
                def _(v, m=m):
                    if m == 0:
                        v.wait_ge(pre.s_stat.h, pre.s_stat.n)
                    v.wait_ge(s_rmx.h, s_rmx.n)

                    def F(ins):
                        s_fen.inc(ins)
                        v.wait_ge(s_fen.h, s_fen.n)
                    F(v.scalar_tensor_tensor(rlog[:, :], srep[:, :],
                                             pre.mu[:, m:m + 1], pf2[:, 0:E],
                                             ALU.mult, ALU.subtract))
                    F(v.tensor_scalar(rlog[:, :], rlog[:, :],
                                      pre.rstd[:, m:m + 1], -1.0, ALU.mult, ALU.mult))
                    F(v.reduce_max(rmx[:, :], rlog[:, :], axis=AX.X, negate=True))
                    s_rmx.inc(v.engine_nop())

                @block.scalar
                def _(s, m=m):
                    s.wait_ge(s_rmx.h, s_rmx.n)
                    if m >= 1:
                        s.wait_ge(s_gate.h, m)
                    s_rexp.inc(s.activation(rprob[:, :], rlog[:, :], AF.Exp,
                                            bias=rmx[:, 0:1], accum_out=rsum[:, 0:1]))

                @block.vector
                def _(v, m=m):
                    v.wait_ge(s_rexp.h, s_rexp.n)

                    def F(ins):
                        s_fen.inc(ins)
                        v.wait_ge(s_fen.h, s_fen.n)
                    if m >= 2:
                        r_st.wait(v, m - 2)
                    F(v.reciprocal(rsum[:, :], rsum[:, :]))
                    F(v.tensor_scalar_mul(rprob[:, :], rprob[:, :], rsum[:, 0:1]))
                    F(v.reduce_max(rm1[:, :], rprob[:, :], axis=AX.X))
                    F(v.tensor_single_scalar(rmsk[:, :], rprob[:, :], rm1[:, 0:1],
                                             ALU.is_ge))
                    F(v.scalar_tensor_tensor(rp2[:, :], rmsk[:, :], -1e30,
                                             rprob[:, :], ALU.mult, ALU.add))
                    F(v.reduce_max(rm2[:, :], rp2[:, :], axis=AX.X))
                    F(v.tensor_single_scalar(rmsk2[:, :], rp2[:, :], rm2[:, 0:1],
                                             ALU.is_ge))
                    F(v.tensor_tensor(rm2[:, :], rm1[:, :], rm2[:, :], ALU.add))
                    F(v.tensor_scalar_max(rm2[:, :], rm2[:, :], 1e-9))
                    F(v.reciprocal(rm2[:, :], rm2[:, :]))
                    F(v.tensor_tensor(rmsk[:, :], rmsk[:, :], rmsk2[:, :], ALU.add))
                    F(v.tensor_tensor(rmsk[:, :], rmsk[:, :], rprob[:, :], ALU.mult))
                    s_gate.inc(v.tensor_single_scalar(gout[:, (m % 2) * E:(m % 2 + 1) * E],
                                                      rmsk[:, :], rm2[:, 0:1], ALU.mult))

                @block.gpsimd
                def _(g, m=m):
                    g.wait_ge(s_gate.h, s_gate.n)
                    r_st.load(g, m, gr[m], gout[:, (m % 2) * E:(m % 2 + 1) * E])

            @block.gpsimd
            def _(g):
                for s_ in r_st.sems:
                    g.wait_ge(s_.h, s_.n)
    return nc


def build_l1ffn_capacity():
    E = 8
    TF = NCORES * T
    FPC = TF // 128
    CAP = max(int(1.25 * (TF * 2 / E)), 4)
    nc = bass.Bass("TRN2", target_bir_lowering=False)
    g_h = nc.dram_tensor("gates_full", [TF, E], F32, kind="ExternalInput").ap()
    o_h = nc.dram_tensor("geff", [TF, E + 1], F32, kind="ExternalOutput").ap()

    with ExitStack() as ctx:
        c = Ctx(nc, ctx)
        gcol = c.sb("gcol", [128, FPC * E], F32)
        geff = c.sb("geff_sb", [128, FPC * E], F32)
        fb = c.sb("fb", [128, FPC], F32)
        match = c.sb("match", [128, FPC], F32)
        elig = c.sb("elig", [128, FPC], F32)
        csA = c.sb("csA", [128, FPC], F32)
        csB = c.sb("csB", [128, FPC], F32)
        tmp1 = c.sb("tmp1", [128, FPC], F32)
        rowsum = c.sb("rowsum", [128, 1], F32)
        prefix = c.sb("prefix", [128, 1], F32)
        ltones = c.sb("ltones", [128, 128], F32)
        pf0 = c.ps("pf0", [128, 512], F32)
        s_ld = c.sem("s_ld")
        s_lt = c.sem("s_lt")
        s_pre = c.sem("s_pre")
        s_mm = c.sem("s_mm")
        s_f = c.sem("s_f")
        s_out = c.sem("s_out")
        s_st = c.sem("s_st")

        gi = g_h.rearrange("(p f) e -> p (f e)", p=128)
        go = o_h.rearrange("(p f) e -> p (f e)", p=128)

        with nc.Block() as block:
            @block.gpsimd
            def _(g):
                s_ld.inc(g.dma_start(gcol[:, :], gi), 16)
                s_lt.inc(g.memset(ltones[:, :], 1.0))
                g.wait_ge(s_lt.h, s_lt.n)
                s_lt.inc(g.affine_select(ltones[:, :], ltones[:, :], [[1, 128]],
                                         ALU.is_gt, 0.0, base=0, channel_multiplier=-1))

            @block.vector
            def _(v):
                def F(ins):
                    s_f.inc(ins)
                    v.wait_ge(s_f.h, s_f.n)
                v.wait_ge(s_ld.h, s_ld.n)
                F(v.memset(fb[:, :], 0.0))
                for e in range(E):
                    gv = gcol[:, :].rearrange("p (f e) -> p f e", e=E)[:, :, e]
                    F(v.tensor_single_scalar(match[:, :], gv, 0.0, ALU.is_gt))
                    F(v.tensor_scalar(tmp1[:, :], fb[:, :], -1.0, 1.0, ALU.mult, ALU.add))
                    F(v.tensor_tensor(elig[:, :], match[:, :], tmp1[:, :], ALU.mult))
                    F(v.tensor_copy(csA[:, :], elig[:, :]))
                    src_, dst_ = csA, csB
                    sh = 1
                    while sh < FPC:
                        F(v.tensor_copy(dst_[:, 0:sh], src_[:, 0:sh]))
                        F(v.tensor_tensor(dst_[:, sh:FPC], src_[:, sh:FPC],
                                          src_[:, 0:FPC - sh], ALU.add))
                        src_, dst_ = dst_, src_
                        sh *= 2
                    cs = src_
                    F(v.tensor_copy(rowsum[:, :], cs[:, FPC - 1:FPC]))
                    s_pre.inc(v.engine_nop())
                    v.wait_ge(s_mm.h, e + 1)
                    F(v.tensor_copy(prefix[:, :], pf0[:, 0:1]))
                    F(v.tensor_scalar_add(cs[:, :], cs[:, :], prefix[:, 0:1]))
                    F(v.tensor_single_scalar(tmp1[:, :], cs[:, :], float(CAP), ALU.is_le))
                    F(v.tensor_tensor(tmp1[:, :], tmp1[:, :], elig[:, :], ALU.mult))
                    F(v.tensor_tensor(elig[:, :], elig[:, :], tmp1[:, :], ALU.subtract))
                    F(v.tensor_tensor(fb[:, :], fb[:, :], elig[:, :], ALU.add))
                    gef = geff[:, :].rearrange("p (f e) -> p f e", e=E)[:, :, e]
                    F(v.tensor_tensor(gef, gv, tmp1[:, :], ALU.mult))
                s_out.inc(v.engine_nop())

            @block.tensor
            def _(t):
                t.wait_ge(s_lt.h, s_lt.n)
                for e in range(E):
                    t.wait_ge(s_pre.h, e + 1)
                    s_mm.inc(t.matmul(pf0[:, 0:1], ltones[:, :], rowsum[:, :],
                                      start=True, stop=True))

            @block.sync
            def _(sy):
                sy.wait_ge(s_out.h, s_out.n)
                sy.dma_start(go.rearrange("p (f e) -> p f e", e=E + 1)[:, :, 0:E],
                             geff[:, :].rearrange("p (f e) -> p f e", e=E)).then_inc(s_st.h, 16)
                sy.dma_start(go.rearrange("p (f e) -> p f e", e=E + 1)[:, :, E:E + 1],
                             fb[:, :].rearrange("p (f o) -> p f o", o=1),
                             allow_slow_non_contiguous=True).then_inc(s_st.h, 16)
                sy.wait_ge(s_st.h, 32)
    return nc


def build_l1ffn_experts():
    FF = 4096
    NKF = FF // 128
    E = 8
    SG = 4
    NSG = NTT // SG
    TOKSG = SG * 128

    nc = bass.Bass("TRN2", target_bir_lowering=False)
    x_h = nc.dram_tensor("x", [T, D], F32, kind="ExternalInput").ap()
    ge_h = nc.dram_tensor("geff", [T, E + 1], F32, kind="ExternalInput").ap()
    w1_h = nc.dram_tensor("W1", [E, D, FF], F32, kind="ExternalInput").ap()
    b1_h = nc.dram_tensor("b1", [E, FF], F32, kind="ExternalInput").ap()
    w2_h = nc.dram_tensor("W2", [E, FF, D], F32, kind="ExternalInput").ap()
    b2_h = nc.dram_tensor("b2", [E, D], F32, kind="ExternalInput").ap()
    fw1_h = nc.dram_tensor("fw1", [D, FF], F32, kind="ExternalInput").ap()
    fb1_h = nc.dram_tensor("fb1", [FF], F32, kind="ExternalInput").ap()
    fw2_h = nc.dram_tensor("fw2", [FF, D], F32, kind="ExternalInput").ap()
    fb2_h = nc.dram_tensor("fb2", [D], F32, kind="ExternalInput").ap()
    y_h = nc.dram_tensor("y", [T, D], F32, kind="ExternalOutput").ap()

    NQ = 4
    FFQ = FF // NQ
    with ExitStack() as ctx:
        c = Ctx(nc, ctx)
        w1sb = c.sb("w1sb", [128, NKD * FFQ], BF16)
        hT = c.sb("hT", [128, NKF * TOKSG], BF16)
        w2sb = c.sb("w2sb", [128, 16 * D], BF16)
        yacc = c.sb("yacc", [128, SG * D], F32)
        oacc = c.sb("oacc", [128, SG * D], F32)
        gcols = c.sb("gcols", [128, NTT * (E + 1)], F32)
        gateT = c.sb("gateT", [8, TOKSG], BF16)
        b2all = c.sb("b2all", [8, D], BF16)
        fb2rep = c.sb("fb2rep", [128, D], F32)
        fb2row = c.sb("fb2row", [1, D], F32)
        b1c = c.sb("b1c", [128, (E + 1) * NKF], F32)
        onesf = c.sb("onesf", [1, 128], F32)
        yst = c.sb("yst", [128, 2 * D], F32)
        omv = c.sb("omv", [128, NTT], F32)

        pbA = c.ps("pbA", [128, 1024], BF16)
        pbB = c.ps("pbB", [128, 1024], BF16)
        pf0 = c.ps("pf0", [128, 512], F32)
        pf1 = c.ps("pf1", [128, 512], F32)
        pf2 = c.ps("pf2", [128, 512], F32)
        pf3 = c.ps("pf3", [128, 512], F32)

        s_ones = c.sem("s_ones")
        s_cst = c.sem("s_cst")
        s_rep = c.sem("s_rep")
        s_g = c.sem("s_g")
        s_gT = c.sem("s_gT")
        s_gTe = c.sem("s_gTe")
        s_b2i = c.sem("s_b2i")
        s_b2e = c.sem("s_b2e")
        s_hmm = c.sem("s_hmm")
        s_hev = c.sem("s_hev")
        s_ymm = c.sem("s_ymm")
        s_yev = c.sem("s_yev")
        s_oac = c.sem("s_oac")
        s_y = c.sem("s_y")
        s_fen = c.sem("s_fen")
        r_w1 = DmaRing(c, "w1r", 8)
        r_w2 = DmaRing(c, "w2r", 16)
        s_gld = c.sem("s_gld")
        r_st = DmaRing(c, "str", 2)

        w1r = w1_h.rearrange("e (k p) q -> e k p q", p=128)
        w2r = w2_h.rearrange("e (a p) q -> e a p q", p=128)
        fw1r = fw1_h.rearrange("(k p) q -> k p q", p=128)
        fw2r = fw2_h.rearrange("(a p) q -> a p q", p=128)
        b1r = b1_h.rearrange("e (a p) -> e p a", p=128)
        fb1r = fb1_h.rearrange("(a p) -> p a", p=128)
        ger = ge_h.rearrange("(m p) e -> m p e", p=128)
        yr = y_h.rearrange("(m p) d -> m p d", p=128)

        with nc.Block() as block:
            ident, identf, s_ident = emit_identity(c, block, with_f32=True)

            @block.vector
            def _(v):
                s_ones.inc(v.memset(onesf[:, :], 1.0))

            @block.gpsimd
            def _(g):
                s_cst.inc(g.dma_start(b2all[:, :], b2_h), 16)
                s_cst.inc(g.dma_start(fb2row[:, :],
                                      fb2_h.rearrange("(o n) -> o n", o=1)), 16)
                for e in range(E):
                    s_cst.inc(g.dma_start(b1c[:, e * NKF:(e + 1) * NKF], b1r[e],
                                          allow_slow_non_contiguous=True), 16)
                s_cst.inc(g.dma_start(b1c[:, E * NKF:(E + 1) * NKF], fb1r,
                                      allow_slow_non_contiguous=True), 16)

            @block.sync
            def _(sy):
                for m in range(NTT):
                    s_gld.inc(sy.dma_start(gcols[:, m * (E + 1):(m + 1) * (E + 1)],
                                           ger[m]), 16)

            @block.tensor
            def _(t):
                t.wait_ge(s_ones.h, s_ones.n)
                t.wait_ge(s_cst.h, s_cst.n)
                s_rep.inc(t.matmul(pf2[:, :], onesf[:, 0:128], fb2row[:, 0:512],
                                   start=True, stop=True))
                s_rep.inc(t.matmul(pf3[:, :], onesf[:, 0:128], fb2row[:, 512:1024],
                                   start=True, stop=True))

            @block.vector
            def _(v):
                v.wait_ge(s_rep.h, s_rep.n)
                v.tensor_copy(fb2rep[:, 0:512], pf2[:, :])
                s_rep.inc(v.tensor_copy(fb2rep[:, 512:1024], pf3[:, :]))
                v.wait_ge(s_gld.h, s_gld.n)
                for m in range(NTT):
                    ins = v.tensor_scalar(
                        omv[:, m:m + 1],
                        gcols[:, m * (E + 1) + E:m * (E + 1) + E + 1],
                        -1.0, 1.0, ALU.mult, ALU.add)
                s_g.inc(ins)

            pre = PrelnRing(c, "ln", x_h, ident, s_ident, pbA, pbB, nslots=SG, nx=8)
            pre.emit_stats(block)
            pre.emit_x2(block, consumer=lambda m: [(s_y.h, m + 1)])

            for sg in range(NSG):
                t0 = sg * SG
                for m in range(t0, t0 + SG):
                    @block.tensor
                    def _(t, m=m):
                        pre.transposes(t, m)
                        t.wait_ge(s_rep.h, s_rep.n)
                        t.wait_ge(s_g.h, s_g.n)
                        t.wait_ge(s_gTe.h, s_gTe.n)
                        if sg >= 1:
                            t.wait_ge(s_b2i.h, SG * 2 * sg)
                        s_gT.inc(t.transpose(
                            pf3[0:E, 0:128],
                            gcols[:, m * (E + 1):m * (E + 1) + E], identf[:, :]))

                    @block.scalar
                    def _(s, m=m, t0=t0):
                        pre.evac(s, m)
                        s.wait_ge(s_gT.h, s_gT.n)
                        s_gTe.inc(s.copy(gateT[:, (m - t0) * 128:(m - t0 + 1) * 128],
                                         pf3[0:E, 0:128]))

                    @block.vector
                    def _(v, m=m):
                        pre.apply(v, m)

                for mm in range(SG):
                    for chn in range(2):
                        @block.tensor
                        def _(t, mm=mm, chn=chn, t0=t0):
                            t.wait_ge(s_gTe.h, s_gTe.n)
                            i = mm * 2 + chn
                            if i == 0:
                                t.wait_ge(s_y.h, t0)
                            if i >= 2:
                                t.wait_ge(s_b2e.h, s_b2i.n - 1)
                            pfb = pf2 if i % 2 == 0 else pf3
                            s_b2i.inc(t.matmul(pfb[:, :],
                                               gateT[:, mm * 128:(mm + 1) * 128],
                                               b2all[:, chn * 512:(chn + 1) * 512],
                                               start=True, stop=True))

                        @block.vector
                        def _(v, mm=mm, chn=chn):
                            v.wait_ge(s_b2i.h, s_b2i.n)
                            pfb = pf2 if (mm * 2 + chn) % 2 == 0 else pf3
                            s_b2e.inc(v.tensor_copy(
                                oacc[:, mm * D + chn * 512:mm * D + (chn + 1) * 512],
                                pfb[:, :]))

                for e in range(E + 1):
                    bhmm, bhev = s_hmm.n, s_hev.n
                    if e == E:
                        for mm in range(SG):
                            @block.vector
                            def _(v, mm=mm, t0=t0):
                                v.wait_ge(s_oac.h, s_oac.n)

                                def F(ins):
                                    s_fen.inc(ins)
                                    v.wait_ge(s_fen.h, s_fen.n)
                                F(v.tensor_scalar_mul(oacc[:, mm * D:(mm + 1) * D],
                                                      oacc[:, mm * D:(mm + 1) * D],
                                                      omv[:, t0 + mm:t0 + mm + 1]))
                                s_oac.inc(v.scalar_tensor_tensor(
                                    oacc[:, mm * D:(mm + 1) * D], fb2rep[:, :],
                                    gcols[:, (t0 + mm) * (E + 1) + E:(t0 + mm) * (E + 1) + E + 1],
                                    oacc[:, mm * D:(mm + 1) * D], ALU.mult, ALU.add))
                    for q in range(NQ):
                        for k in range(NKD):
                            @block.gpsimd
                            def _(g, q=q, k=k, e=e, sg=sg):
                                idx = ((sg * (E + 1) + e) * NQ + q) * NKD + k
                                g.wait_ge(s_hmm.h, bhmm + q * 8)
                                src = (w1r[e, k, :, q * FFQ:(q + 1) * FFQ] if e < E
                                       else fw1r[k, :, q * FFQ:(q + 1) * FFQ])
                                r_w1.load(g, idx, w1sb[:, k * FFQ:(k + 1) * FFQ], src)

                        for a8 in range(FFQ // 128):
                            a = q * (FFQ // 128) + a8

                            @block.tensor
                            def _(t, q=q, a=a, a8=a8, e=e, sg=sg, t0=t0):
                                for k in range(NKD):
                                    r_w1.wait(t, ((sg * (E + 1) + e) * NQ + q) * NKD + k)
                                if a < 2:
                                    t.wait_ge(s_yev.h, s_yev.n)
                                if a >= 2:
                                    t.wait_ge(s_hev.h, bhev + a - 1)
                                pfh = pf0 if a % 2 == 0 else pf1
                                for k in range(NKD):
                                    ins = t.matmul(
                                        pfh[:, :],
                                        w1sb[:, k * FFQ + a8 * 128:k * FFQ + (a8 + 1) * 128],
                                        pre.lnT_chunk(k, t0),
                                        start=(k == 0), stop=(k == NKD - 1))
                                s_hmm.inc(ins)

                            @block.scalar
                            def _(s, a=a, e=e):
                                s.wait_ge(s_hmm.h, bhmm + a + 1)
                                pfh = pf0 if a % 2 == 0 else pf1
                                s_hev.inc(s.activation(
                                    hT[:, a * TOKSG:(a + 1) * TOKSG], pfh[:, :],
                                    AF.Gelu, bias=b1c[:, e * NKF + a:e * NKF + a + 1]))

                    bymm, byev = s_ymm.n, s_yev.n
                    for grp in range(4):
                        for kk in range(8):
                            @block.gpsimd
                            def _(g, grp=grp, kk=kk, e=e, sg=sg):
                                idx = ((sg * (E + 1) + e) * 4 + grp) * 8 + kk
                                src = (w2r[e, grp * 8 + kk] if e < E
                                       else fw2r[grp * 8 + kk])
                                if grp >= 2:
                                    g.wait_ge(s_ymm.h, bymm + (grp - 1) * 8)
                                else:
                                    g.wait_ge(s_ymm.h, bymm)
                                r_w2.load(g, idx,
                                          w2sb[:, ((grp % 2) * 8 + kk) * D:((grp % 2) * 8 + kk + 1) * D],
                                          src)

                        for mm in range(SG):
                            for chn in range(2):
                                @block.tensor
                                def _(t, grp=grp, mm=mm, chn=chn, e=e, sg=sg):
                                    for kk in range(8):
                                        r_w2.wait(t, ((sg * (E + 1) + e) * 4 + grp) * 8 + kk)
                                    t.wait_ge(s_hev.h, bhev + NKF)
                                    j = (grp * SG + mm) * 2 + chn
                                    if j >= 2:
                                        t.wait_ge(s_yev.h, byev + j - 1)
                                    pfy = pf0 if j % 2 == 0 else pf1
                                    for kk in range(8):
                                        kf = grp * 8 + kk
                                        ins = t.matmul(
                                            pfy[:, :],
                                            hT[:, kf * TOKSG + mm * 128:kf * TOKSG + (mm + 1) * 128],
                                            w2sb[:, ((grp % 2) * 8 + kk) * D + chn * 512:((grp % 2) * 8 + kk) * D + (chn + 1) * 512],
                                            start=(kk == 0), stop=(kk == 7))
                                    s_ymm.inc(ins)

                                @block.vector
                                def _(v, grp=grp, mm=mm, chn=chn):
                                    j = (grp * SG + mm) * 2 + chn
                                    v.wait_ge(s_ymm.h, bymm + j + 1)
                                    pfy = pf0 if j % 2 == 0 else pf1
                                    sl = mm * D + chn * 512
                                    if grp == 0:
                                        v.wait_ge(s_oac.h, s_oac.n)
                                        s_yev.inc(v.tensor_copy(yacc[:, sl:sl + 512],
                                                                pfy[:, :]))
                                    else:
                                        s_yev.inc(v.tensor_tensor(yacc[:, sl:sl + 512],
                                                                  yacc[:, sl:sl + 512],
                                                                  pfy[:, :], ALU.add))

                    for mm in range(SG):
                        @block.vector
                        def _(v, mm=mm, e=e, t0=t0):
                            v.wait_ge(s_yev.h, s_yev.n)
                            gc = gcols[:, (t0 + mm) * (E + 1) + min(e, E):(t0 + mm) * (E + 1) + min(e, E) + 1]
                            s_oac.inc(v.scalar_tensor_tensor(
                                oacc[:, mm * D:(mm + 1) * D],
                                yacc[:, mm * D:(mm + 1) * D], gc,
                                oacc[:, mm * D:(mm + 1) * D], ALU.mult, ALU.add))

                for mm in range(SG):
                    m = t0 + mm

                    @block.vector
                    def _(v, m=m, mm=mm):
                        v.wait_ge(s_oac.h, s_oac.n)
                        pre.r_x2.wait(v, m)
                        if m >= 2:
                            r_st.wait(v, m - 2)
                        s_y.inc(v.tensor_tensor(yst[:, (m % 2) * D:(m % 2 + 1) * D],
                                                pre.xring[:, (m % pre.nx) * D:(m % pre.nx + 1) * D],
                                                oacc[:, mm * D:(mm + 1) * D], ALU.add))

                    @block.gpsimd
                    def _(g, m=m):
                        g.wait_ge(s_y.h, s_y.n)
                        r_st.load(g, m, yr[m], yst[:, (m % 2) * D:(m % 2 + 1) * D])

            @block.gpsimd
            def _(g):
                for s_ in r_st.sems:
                    g.wait_ge(s_.h, s_.n)
    return nc



def build_l2attn():
    C2 = 2048
    NC2 = C2 // 128
    NST = 16
    DTR = 64
    CH = 512
    NCHK = T // CH
    NTC = CH // 128

    nc = bass.Bass("TRN2", target_bir_lowering=False)
    x_h = nc.dram_tensor("x", [T, D], F32, kind="ExternalInput").ap()
    win_h = nc.dram_tensor("W_in", [2 * C2, D], F32, kind="ExternalInput").ap()
    cw_h = nc.dram_tensor("conv_w", [C2, 1, 3], F32, kind="ExternalInput").ap()
    cb_h = nc.dram_tensor("conv_b", [C2], F32, kind="ExternalInput").ap()
    wxp_h = nc.dram_tensor("W_xproj", [2 * NST + DTR, C2], F32, kind="ExternalInput").ap()
    wdt_h = nc.dram_tensor("W_dt", [C2, DTR], F32, kind="ExternalInput").ap()
    bdt_h = nc.dram_tensor("b_dt", [C2], F32, kind="ExternalInput").ap()
    alog_h = nc.dram_tensor("A_log", [C2, NST], F32, kind="ExternalInput").ap()
    dd_h = nc.dram_tensor("Dp", [C2], F32, kind="ExternalInput").ap()
    wo_h = nc.dram_tensor("W_out", [D, C2], F32, kind="ExternalInput").ap()
    y_h = nc.dram_tensor("y", [T, D], F32, kind="ExternalOutput").ap()
    bc_d = nc.dram_tensor("bc_scratch", [NST, 2 * CH], BF16, kind="Internal").ap()

    with ExitStack() as ctx:
        c = Ctx(nc, ctx)
        yst = c.sb("yst", [128, 2 * D], F32)
        uT = c.sb("uT", [128, NC2 * CH], BF16)
        ygT = uT
        zT = c.sb("zT", [128, NC2 * CH], BF16)
        ucT = c.sb("ucT", [128, NC2 * CH], BF16)
        wostg = uT[:, 0:2 * C2]
        woT = c.sb("woT", [128, NC2 * D], BF16)
        winT = c.sb("winT", [128, 2 * NKD * 128], BF16)
        wxstg = winT
        winstg = ucT[:, 0:2 * D]
        wxpT = c.sb("wxpT", [128, NC2 * 96], BF16)
        wdtT = c.sb("wdtT", [64, NC2 * 128], BF16)
        wdtstg = c.sb("wdtstg", [128, 2 * DTR], BF16)
        Brep = c.sb("Brep", [128, NST * CH], BF16)
        Crep = c.sb("Crep", [128, NST * CH], BF16)
        Brow = c.sb("Brow", [1, 2 * CH], BF16)
        Bsb = c.sb("Bsb", [16, 2 * CH], BF16)
        hgrp = c.sb("hgrp", [128, 2 * CH], BF16)
        bgrp = c.sb("bgrp", [128, 2 * CH], BF16)
        dAg = c.sb("dAg", [128, 2 * CH], BF16)
        dlt = c.sb("dlt", [128, CH], F32)
        dup = c.sb("dup", [128, CH], F32)
        sp1 = c.sb("sp1", [128, CH], F32)
        sp2 = c.sb("sp2", [128, CH], F32)
        sp3 = c.sb("sp3", [128, CH], F32)
        szl = c.sb("szl", [128, CH], F32)
        yac = c.sb("yac", [128, CH], F32)
        dtTb = c.sb("dtTb", [64, CH], BF16)
        ustit = c.sb("ustit", [128, CH + 2], F32)
        haloL = c.sb("haloL", [128, NC2], F32)
        haloR = c.sb("haloR", [128, NC2], F32)
        states = c.sb("states", [128, NC2 * NST], F32)
        Acol = c.sb("Acol", [128, NC2 * NST], F32)
        cwc = c.sb("cwc", [128, 3 * NC2], F32)
        cbc = c.sb("cbc", [128, NC2], F32)
        dcol = c.sb("dcol", [128, NC2], F32)
        bdtc = c.sb("bdtc", [128, NC2], F32)
        onesf = c.sb("onesf", [1, 128], BF16)

        pbA = c.ps("pbA", [128, 1024], BF16)
        pbB = c.ps("pbB", [128, 1024], BF16)
        pf0 = c.ps("pf0", [128, 512], F32)
        pf1 = c.ps("pf1", [128, 512], F32)
        pf2 = c.ps("pf2", [128, 512], F32)
        pfB = c.ps("pfB", [16, 512], F32)
        pfC = c.ps("pfC", [16, 512], F32)
        pfd = c.ps("pfd", [64, 512], F32)

        s_init = c.sem("s_init")
        s_cst = c.sem("s_cst")
        s_acl = c.sem("s_acl")
        s_wxT = c.sem("s_wxT")
        s_wxE = c.sem("s_wxE")
        s_wdT = c.sem("s_wdT")
        s_wdE = c.sem("s_wdE")
        s_woT = c.sem("s_woT")
        s_woE = c.sem("s_woE")
        s_mmu = c.sem("s_mmu")
        s_uev = c.sem("s_uev")
        s_hal = c.sem("s_hal")
        s_cnv = c.sem("s_cnv")
        s_sil = c.sem("s_sil")
        s_proj = c.sem("s_proj")
        s_pev = c.sem("s_pev")
        s_bc = c.sem("s_bc")
        s_rep = c.sem("s_rep")
        s_dlm = c.sem("s_dlm")
        s_dlv = c.sem("s_dlv")
        s_spl = c.sem("s_spl")
        s_sfen = c.sem("s_sfen")
        s_dA = c.sem("s_dA")
        s_scan = c.sem("s_scan")
        s_yg = c.sem("s_yg")
        s_omm = c.sem("s_omm")
        s_y = c.sem("s_y")
        s_fen = c.sem("s_fen")
        r_win = DmaRing(c, "winr", 2)
        r_br = DmaRing(c, "brr", 2)
        r_wo = DmaRing(c, "wor", 2)
        r_wdt = DmaRing(c, "wdtr", 2)
        r_st = DmaRing(c, "str", 2)

        winr = win_h.rearrange("(a p) q -> a p q", p=128)
        wor = wo_h.rearrange("(a p) q -> a p q", p=128)
        yr = y_h.rearrange("(m p) d -> m p d", p=128)
        cwr = cw_h.rearrange("(a p) o k -> a p (o k)", p=128)
        cbr = cb_h.rearrange("(a p) -> p a", p=128)
        ddr = dd_h.rearrange("(a p) -> p a", p=128)
        bdtr = bdt_h.rearrange("(a p) -> p a", p=128)
        alogr = alog_h.rearrange("(a p) n -> a p n", p=128)
        wdtr = wdt_h.rearrange("(a p) r -> a p r", p=128)

        with nc.Block() as block:
            ident, s_ident = emit_identity(c, block)

            @block.vector
            def _(v):
                s_init.inc(v.memset(haloL[:, :], 0.0))
                s_init.inc(v.memset(states[:, :], 0.0))
                s_init.inc(v.memset(onesf[:, :], 1.0))

            @block.gpsimd
            def _(g):
                for a in range(NC2):
                    s_cst.inc(g.dma_start(cwc[:, 3 * a:3 * a + 3], cwr[a]), 16)
                s_cst.inc(g.dma_start(cbc[:, :], cbr, allow_slow_non_contiguous=True), 16)
                s_cst.inc(g.dma_start(dcol[:, :], ddr, allow_slow_non_contiguous=True), 16)
                s_cst.inc(g.dma_start(bdtc[:, :], bdtr, allow_slow_non_contiguous=True), 16)
                for a in range(NC2):
                    s_cst.inc(g.dma_start(Acol[:, a * NST:(a + 1) * NST], alogr[a]), 16)
                s_cst.inc(g.dma_start(wxstg[0:96, 0:C2], wxp_h), 16)

            @block.scalar
            def _(s):
                s.wait_ge(s_cst.h, s_cst.n)
                s_acl.inc(s.activation(Acol[:, :], Acol[:, :], AF.Exp))

            @block.vector
            def _(v):
                v.wait_ge(s_acl.h, s_acl.n)
                s_acl.inc(v.tensor_scalar_mul(Acol[:, :], Acol[:, :], -1.0))

            for a in range(NC2):
                @block.tensor
                def _(t, a=a):
                    if a == 0:
                        t.wait_ge(s_ident.h, s_ident.n)
                        t.wait_ge(s_cst.h, s_cst.n)
                    if a >= 2:
                        t.wait_ge(s_wxE.h, a - 1)
                    pb = pbA if a % 2 == 0 else pbB
                    s_wxT.inc(t.transpose(pb[0:128, 0:96],
                                          wxstg[0:96, a * 128:(a + 1) * 128],
                                          ident[0:96, 0:96]))

                @block.vector
                def _(v, a=a):
                    v.wait_ge(s_wxT.h, a + 1)
                    pb = pbA if a % 2 == 0 else pbB
                    s_wxE.inc(v.tensor_copy(wxpT[:, a * 96:(a + 1) * 96],
                                            pb[0:128, 0:96]))

            for a in range(NC2):
                @block.gpsimd
                def _(g, a=a):
                    if a >= 2:
                        g.wait_ge(s_wdT.h, a - 1)
                    r_wdt.load(g, a, wdtstg[:, (a % 2) * DTR:(a % 2 + 1) * DTR],
                               wdtr[a])

                @block.tensor
                def _(t, a=a):
                    r_wdt.wait(t, a)
                    if a == 0:
                        t.wait_ge(s_wxE.h, NC2)
                    if a >= 2:
                        t.wait_ge(s_wdE.h, a - 1)
                    pb = pbA if a % 2 == 0 else pbB
                    s_wdT.inc(t.transpose(pb[0:DTR, 0:128],
                                          wdtstg[:, (a % 2) * DTR:(a % 2 + 1) * DTR],
                                          ident[:, :]))

                @block.vector
                def _(v, a=a):
                    v.wait_ge(s_wdT.h, a + 1)
                    pb = pbA if a % 2 == 0 else pbB
                    s_wdE.inc(v.tensor_copy(wdtT[:, a * 128:(a + 1) * 128],
                                            pb[0:DTR, 0:128]))

            for dp in range(NKD):
                @block.gpsimd
                def _(g, dp=dp):
                    if dp >= 2:
                        g.wait_ge(s_woT.h, NC2 * (dp - 1))
                    r_wo.load(g, dp, wostg[:, (dp % 2) * C2:(dp % 2 + 1) * C2],
                              wor[dp])

                @block.tensor
                def _(t, dp=dp):
                    r_wo.wait(t, dp)
                    if dp == 0:
                        t.wait_ge(s_wdE.h, NC2)
                    for kc in range(NC2):
                        i = dp * NC2 + kc
                        if i >= 2:
                            t.wait_ge(s_woE.h, i - 1)
                        pb = pbA if i % 2 == 0 else pbB
                        s_woT.inc(t.transpose(
                            pb[:, 0:128],
                            wostg[:, (dp % 2) * C2 + kc * 128:(dp % 2) * C2 + (kc + 1) * 128],
                            ident[:, :]))

                @block.vector
                def _(v, dp=dp):
                    for kc in range(NC2):
                        i = dp * NC2 + kc
                        v.wait_ge(s_woT.h, i + 1)
                        pb = pbA if i % 2 == 0 else pbB
                        s_woE.inc(v.tensor_copy(
                            woT[:, kc * D + dp * 128:kc * D + (dp + 1) * 128],
                            pb[:, 0:128]))

            pre = PrelnRing(c, "ln", x_h, ident, s_ident, pbA, pbB, nslots=8,
                            nx=5, x1_sb=yst)
            pre.emit_stats(block)
            pre.emit_x2(block, consumer=lambda m: [(s_y.h, 2 * m + 2)])

            for tc in range(NCHK):
                t0 = tc * NTC
                hi = t0 + NTC if tc < NCHK - 1 else t0 + NTC - 1
                lo = 0 if tc == 0 else tc * NTC + 1
                for m in range(lo, hi + 1):
                    @block.vector
                    def _(v, m=m):
                        pre.apply(v, m)

                    @block.tensor
                    def _(t, m=m, tc=tc):
                        if tc == 0 and m < 2:
                            t.wait_ge(s_woE.h, NKD * NC2)
                        pre.transposes(t, m)

                    @block.scalar
                    def _(s, m=m):
                        pre.evac(s, m)

                for o in range(2 * NC2):
                    idx = tc * 2 * NC2 + o

                    @block.gpsimd
                    def _(g, o=o, idx=idx, tc=tc):
                        if tc >= 1 and o == 0:
                            g.wait_ge(s_yg.h, s_yg.n - 12)
                        if idx >= 2:
                            g.wait_ge(s_mmu.h, s_mmu.n - 5)
                        r_win.load(g, idx, winstg[:, (o % 2) * D:(o % 2 + 1) * D],
                                   winr[o])

                    @block.tensor
                    def _(t, o=o, idx=idx, tc=tc):
                        r_win.wait(t, idx)
                        if o < 2:
                            t.wait_ge(pre.s_tile.h, pre.s_tile.n)
                        if idx >= 2:
                            t.wait_ge(s_mmu.h, s_mmu.n - 3)
                        pw = pbA if o % 2 == 0 else pbB
                        for k in range(NKD):
                            ins = t.transpose(
                                pw[:, k * 128:(k + 1) * 128],
                                winstg[:, (o % 2) * D + k * 128:(o % 2) * D + (k + 1) * 128],
                                ident[:, :])
                        s_mmu.inc(ins)

                    @block.vector
                    def _(v, o=o, idx=idx):
                        v.wait_ge(s_mmu.h, s_mmu.n)
                        pw = pbA if o % 2 == 0 else pbB
                        s_mmu.inc(v.tensor_copy(winT[:, (o % 2) * 1024:(o % 2 + 1) * 1024],
                                                pw[:, 0:1024]))

                    @block.tensor
                    def _(t, o=o, idx=idx, tc=tc, t0=t0):
                        t.wait_ge(s_mmu.h, s_mmu.n)
                        if idx >= 2:
                            t.wait_ge(s_uev.h, s_uev.n - 1)
                        if o == 0:
                            if tc >= 1:
                                t.wait_ge(s_y.h, s_y.n)
                                t.wait_ge(s_rep.h, s_rep.n)
                        pf = pf0 if o % 2 == 0 else pf1
                        for k in range(NKD):
                            ins = t.matmul(pf[:, :],
                                           winT[:, (o % 2) * 1024 + k * 128:(o % 2) * 1024 + (k + 1) * 128],
                                           pre.lnT_span4(k, t0),
                                           start=(k == 0), stop=(k == NKD - 1))
                        s_mmu.inc(ins)
                        if o < NC2 and tc < NCHK - 1:
                            for k in range(NKD):
                                ins = t.matmul(pf2[:, o:o + 1],
                                               winT[:, (o % 2) * 1024 + k * 128:(o % 2) * 1024 + (k + 1) * 128],
                                               pre.lnT_tile(k, t0 + NTC)[:, 0:1],
                                               start=(k == 0), stop=(k == NKD - 1))
                            s_hal.inc(ins)

                    @block.vector
                    def _(v, o=o):
                        v.wait_ge(s_mmu.h, s_mmu.n)
                        pf = pf0 if o % 2 == 0 else pf1
                        dst = uT if o < NC2 else zT
                        oo = o % NC2
                        s_uev.inc(v.tensor_copy(dst[:, oo * CH:(oo + 1) * CH], pf[:, :]))

                @block.vector
                def _(v, tc=tc):
                    if tc < NCHK - 1:
                        v.wait_ge(s_hal.h, s_hal.n)
                        s_hal.inc(v.tensor_copy(haloR[:, :], pf2[:, 0:NC2]))
                    else:
                        s_hal.inc(v.memset(haloR[:, :], 0.0))
                        v.wait_ge(s_hal.h, s_hal.n)

                for a in range(NC2):
                    @block.vector
                    def _(v, a=a, tc=tc):
                        def F(ins):
                            s_fen.inc(ins)
                            v.wait_ge(s_fen.h, s_fen.n)
                        if tc == 0 and a == 0:
                            v.wait_ge(s_init.h, s_init.n)
                            v.wait_ge(s_cst.h, s_cst.n)
                        v.wait_ge(s_uev.h, s_uev.n)
                        v.wait_ge(s_hal.h, s_hal.n)
                        v.wait_ge(s_sil.h, s_sil.n)
                        F(v.tensor_copy(ustit[:, 0:1], haloL[:, a:a + 1]))
                        F(v.tensor_copy(ustit[:, 1:CH + 1], uT[:, a * CH:(a + 1) * CH]))
                        F(v.tensor_copy(ustit[:, CH + 1:CH + 2], haloR[:, a:a + 1]))
                        F(v.tensor_scalar_mul(sp1[:, :], ustit[:, 0:CH],
                                              cwc[:, 3 * a:3 * a + 1]))
                        F(v.scalar_tensor_tensor(sp2[:, :], ustit[:, 1:CH + 1],
                                                 cwc[:, 3 * a + 1:3 * a + 2], sp1[:, :],
                                                 ALU.mult, ALU.add))
                        F(v.scalar_tensor_tensor(sp1[:, :], ustit[:, 2:CH + 2],
                                                 cwc[:, 3 * a + 2:3 * a + 3], sp2[:, :],
                                                 ALU.mult, ALU.add))
                        s_cnv.inc(v.tensor_copy(haloL[:, a:a + 1],
                                                uT[:, (a + 1) * CH - 1:(a + 1) * CH]))

                    @block.scalar
                    def _(s, a=a, tc=tc):
                        if tc == 0 and a == 0:
                            s.wait_ge(s_cst.h, s_cst.n)
                        s.wait_ge(s_cnv.h, s_cnv.n)
                        s_sil.inc(s.activation(ucT[:, a * CH:(a + 1) * CH], sp1[:, :],
                                               AF.Silu, bias=cbc[:, a:a + 1]))

                @block.tensor
                def _(t, tc=tc):
                    t.wait_ge(s_sil.h, s_sil.n)
                    if tc == 0:
                        t.wait_ge(s_wxE.h, NC2)
                    else:
                        t.wait_ge(s_pev.h, s_pev.n)
                    for k in range(NC2):
                        t.matmul(pfB[:, :], wxpT[:, k * 96:k * 96 + 16],
                                 ucT[:, k * CH:(k + 1) * CH],
                                 start=(k == 0), stop=(k == NC2 - 1))
                    for k in range(NC2):
                        t.matmul(pfC[:, :], wxpT[:, k * 96 + 16:k * 96 + 32],
                                 ucT[:, k * CH:(k + 1) * CH],
                                 start=(k == 0), stop=(k == NC2 - 1))
                    for k in range(NC2):
                        ins = t.matmul(pfd[:, :], wxpT[:, k * 96 + 32:k * 96 + 96],
                                       ucT[:, k * CH:(k + 1) * CH],
                                       start=(k == 0), stop=(k == NC2 - 1))
                    s_proj.inc(ins)

                @block.vector
                def _(v, tc=tc):
                    v.wait_ge(s_proj.h, s_proj.n)
                    s_pev.inc(v.tensor_copy(Bsb[:, 0:CH], pfB[:, :]))
                    s_pev.inc(v.tensor_copy(Bsb[:, CH:2 * CH], pfC[:, :]))
                    s_pev.inc(v.tensor_copy(dtTb[:, :], pfd[:, :]))

                @block.gpsimd
                def _(g, tc=tc):
                    g.wait_ge(s_pev.h, s_pev.n)
                    s_bc.inc(g.dma_start(bc_d, Bsb[:, :]), 16)

                for j in range(2 * NST):
                    n_, which = j % NST, j // NST

                    @block.gpsimd
                    def _(g, j=j, n_=n_, which=which, tc=tc):
                        g.wait_ge(s_bc.h, s_bc.n)
                        if tc * 2 * NST + j >= 2:
                            g.wait_ge(s_rep.h, s_rep.n - 2)
                        r_br.load(g, tc * 2 * NST + j,
                                  Brow[:, (j % 2) * CH:(j % 2 + 1) * CH],
                                  bc_d[n_:n_ + 1, which * CH:(which + 1) * CH])

                    @block.tensor
                    def _(t, j=j, tc=tc):
                        r_br.wait(t, tc * 2 * NST + j)
                        if tc == 0 and j == 0:
                            t.wait_ge(s_init.h, s_init.n)
                            t.wait_ge(s_hal.h, s_hal.n)
                        if tc * 2 * NST + j >= 1:
                            t.wait_ge(s_rep.h, s_rep.n)
                        s_rep.inc(t.matmul(pf2[:, :], onesf[:, 0:128],
                                           Brow[:, (j % 2) * CH:(j % 2 + 1) * CH],
                                           start=True, stop=True))

                    @block.vector
                    def _(v, j=j, n_=n_, which=which):
                        v.wait_ge(s_rep.h, s_rep.n)
                        dst = Brep if which == 0 else Crep
                        s_rep.inc(v.tensor_copy(dst[:, n_ * CH:(n_ + 1) * CH],
                                                pf2[:, :]))

                for a in range(NC2):
                    @block.tensor
                    def _(t, a=a, tc=tc):
                        t.wait_ge(s_pev.h, s_pev.n)
                        if tc == 0 and a == 0:
                            t.wait_ge(s_wdE.h, NC2)
                        if tc * NC2 + a >= 2:
                            t.wait_ge(s_dlv.h, s_dlv.n - 1)
                        pf = pf0 if a % 2 == 0 else pf1
                        s_dlm.inc(t.matmul(pf[:, :], wdtT[:, a * 128:(a + 1) * 128],
                                           dtTb[:, :], start=True, stop=True))

                    @block.vector
                    def _(v, a=a, tc=tc):
                        if tc == 0 and a == 0:
                            v.wait_ge(s_cst.h, s_cst.n)
                        v.wait_ge(s_dlm.h, s_dlm.n)
                        pf = pf0 if a % 2 == 0 else pf1
                        s_dlv.inc(v.tensor_scalar_add(dlt[:, :], pf[:, :],
                                                      bdtc[:, a:a + 1]))

                    @block.scalar
                    def _(s, a=a):
                        s.wait_ge(s_dlv.h, s_dlv.n)
                        s_sfen.inc(s.activation(sp1[:, :], dlt[:, :], AF.Abs))
                        s.wait_ge(s_sfen.h, s_sfen.n)
                        s_sfen.inc(s.activation(sp1[:, :], sp1[:, :], AF.Exp,
                                                scale=-1.0))
                        s.wait_ge(s_sfen.h, s_sfen.n)
                        s_sfen.inc(s.activation(sp1[:, :], sp1[:, :], AF.Ln,
                                                bias=1.0))
                        s.wait_ge(s_sfen.h, s_sfen.n)
                        s_sfen.inc(s.activation(sp3[:, :], dlt[:, :], AF.Relu))

                    @block.vector
                    def _(v, a=a):
                        def F(ins):
                            s_fen.inc(ins)
                            v.wait_ge(s_fen.h, s_fen.n)
                        v.wait_ge(s_sfen.h, s_sfen.n)
                        F(v.tensor_tensor(dlt[:, :], sp3[:, :], sp1[:, :], ALU.add))
                        F(v.tensor_tensor(dup[:, :], dlt[:, :],
                                          ucT[:, a * CH:(a + 1) * CH], ALU.mult))
                        s_spl.inc(v.nop())

                    for n in range(NST):
                        g_ = (tc * NC2 + a) * NST + n

                        @block.scalar
                        def _(s, a=a, n=n, g_=g_, tc=tc):
                            if g_ == 0:
                                s.wait_ge(s_acl.h, s_acl.n)
                            s.wait_ge(s_spl.h, s_spl.n)
                            if g_ >= 2:
                                s.wait_ge(s_scan.h, s_scan.n - 1)
                            s_dA.inc(s.activation(dAg[:, (n % 2) * CH:(n % 2 + 1) * CH],
                                                  dlt[:, :], AF.Exp,
                                                  scale=Acol[:, a * NST + n:a * NST + n + 1]))

                        @block.vector
                        def _(v, a=a, n=n, g_=g_, tc=tc):
                            def F(ins):
                                s_fen.inc(ins)
                                v.wait_ge(s_fen.h, s_fen.n)
                            if g_ == 0:
                                v.wait_ge(s_init.h, s_init.n)
                            v.wait_ge(s_dA.h, s_dA.n)
                            v.wait_ge(s_rep.h, s_rep.n)
                            F(v.tensor_tensor(bgrp[:, (n % 2) * CH:(n % 2 + 1) * CH],
                                              dup[:, :], Brep[:, n * CH:(n + 1) * CH],
                                              ALU.mult))
                            s_scan.inc(v.tensor_tensor_scan(
                                hgrp[:, (n % 2) * CH:(n % 2 + 1) * CH],
                                dAg[:, (n % 2) * CH:(n % 2 + 1) * CH],
                                bgrp[:, (n % 2) * CH:(n % 2 + 1) * CH],
                                states[:, a * NST + n:a * NST + n + 1],
                                ALU.mult, ALU.add))
                            v.wait_ge(s_scan.h, s_scan.n)
                            F(v.tensor_copy(states[:, a * NST + n:a * NST + n + 1],
                                            hgrp[:, (n % 2 + 1) * CH - 1:(n % 2 + 1) * CH]))
                            F(v.tensor_tensor(sp2[:, :],
                                              hgrp[:, (n % 2) * CH:(n % 2 + 1) * CH],
                                              Crep[:, n * CH:(n + 1) * CH], ALU.mult))
                            if n == 0:
                                F(v.tensor_copy(yac[:, :], sp2[:, :]))
                            else:
                                F(v.tensor_tensor(yac[:, :], yac[:, :], sp2[:, :],
                                                  ALU.add))

                    @block.scalar
                    def _(s, a=a):
                        s.wait_ge(s_scan.h, s_scan.n)
                        s_sfen.inc(s.activation(szl[:, :], zT[:, a * CH:(a + 1) * CH],
                                                AF.Silu))

                    @block.vector
                    def _(v, a=a, tc=tc):
                        def F(ins):
                            s_fen.inc(ins)
                            v.wait_ge(s_fen.h, s_fen.n)
                        if tc == 0 and a == 0:
                            v.wait_ge(s_cst.h, s_cst.n)
                        v.wait_ge(s_sfen.h, s_sfen.n)
                        F(v.scalar_tensor_tensor(yac[:, :], ucT[:, a * CH:(a + 1) * CH],
                                                 dcol[:, a:a + 1], yac[:, :],
                                                 ALU.mult, ALU.add))
                        s_yg.inc(v.tensor_tensor(ygT[:, a * CH:(a + 1) * CH],
                                                 yac[:, :], szl[:, :], ALU.mult))

                for mt in range(NTC):
                    m = t0 + mt
                    for chn in range(2):
                        jj = m * 2 + chn

                        @block.tensor
                        def _(t, mt=mt, chn=chn, jj=jj, tc=tc):
                            t.wait_ge(s_yg.h, s_yg.n)
                            if tc == 0 and jj < 2:
                                t.wait_ge(s_woE.h, NKD * NC2)
                            if jj >= 2:
                                t.wait_ge(s_y.h, jj - 1)
                            pf = pf0 if jj % 2 == 0 else pf1
                            for k in range(NC2):
                                ins = t.matmul(
                                    pf[:, :],
                                    ygT[:, k * CH + mt * 128:k * CH + (mt + 1) * 128],
                                    woT[:, k * D + chn * 512:k * D + (chn + 1) * 512],
                                    start=(k == 0), stop=(k == NC2 - 1))
                            s_omm.inc(ins)

                        @block.vector
                        def _(v, m=m, chn=chn, jj=jj):
                            v.wait_ge(s_omm.h, s_omm.n)
                            pre.r_x2.wait(v, m)
                            if chn == 0 and m >= 2:
                                r_st.wait(v, m - 2)
                            pf = pf0 if jj % 2 == 0 else pf1
                            s_y.inc(v.tensor_tensor(
                                yst[:, (m % 2) * D + chn * 512:(m % 2) * D + (chn + 1) * 512],
                                pre.xring[:, (m % pre.nx) * D + chn * 512:(m % pre.nx) * D + (chn + 1) * 512],
                                pf[:, :], ALU.add))

                    @block.gpsimd
                    def _(g, m=m):
                        g.wait_ge(s_y.h, s_y.n)
                        r_st.load(g, m, yr[m], yst[:, (m % 2) * D:(m % 2 + 1) * D])

            @block.gpsimd
            def _(g):
                for s_ in r_st.sems:
                    g.wait_ge(s_.h, s_.n)
    return nc



def build_l0attn():
    H = 8
    HD = 64
    E2 = 128
    LI = 0.8 - 0.6 * math.exp(-0.3 * 12)
    SC = 1.0 / (64.0 ** 0.25)

    nc = bass.Bass("TRN2", target_bir_lowering=False)
    x_h = nc.dram_tensor("x", [T, D], F32, kind="ExternalInput").ap()
    wq_h = nc.dram_tensor("Wq", [D, D], F32, kind="ExternalInput").ap()
    wk_h = nc.dram_tensor("Wk", [D, D], F32, kind="ExternalInput").ap()
    wv_h = nc.dram_tensor("Wv", [D, D], F32, kind="ExternalInput").ap()
    wo_h = nc.dram_tensor("Wo", [D, D], F32, kind="ExternalInput").ap()
    lq1_h = nc.dram_tensor("lq1", [HD], F32, kind="ExternalInput").ap()
    lk1_h = nc.dram_tensor("lk1", [HD], F32, kind="ExternalInput").ap()
    lq2_h = nc.dram_tensor("lq2", [HD], F32, kind="ExternalInput").ap()
    lk2_h = nc.dram_tensor("lk2", [HD], F32, kind="ExternalInput").ap()
    lnw_h = nc.dram_tensor("ln_w", [E2], F32, kind="ExternalInput").ap()
    lnb_h = nc.dram_tensor("ln_b", [E2], F32, kind="ExternalInput").ap()
    cst_h = nc.dram_tensor("cost", [T, 32], F32, kind="ExternalInput").ap()
    snt_h = nc.dram_tensor("sint", [T, 32], F32, kind="ExternalInput").ap()
    y_h = nc.dram_tensor("y", [T, D], F32, kind="ExternalOutput").ap()

    with ExitStack() as ctx:
        c = Ctx(nc, ctx)
        wqs = c.sb("wqs", [128, NKD * D], BF16)
        wks = c.sb("wks", [128, NKD * D], BF16)
        wvs = c.sb("wvs", [128, NKD * D], BF16)
        wos = c.sb("wos", [128, NKD * D], BF16)
        qsb = c.sb("qsb", [128, D], F32)
        ksb = c.sb("ksb", [128, D], F32)
        vsb = c.sb("vsb", [128, D], BF16)
        csr = c.sb("csr", [128, 64], F32)
        cosr = c.sb("cosr", [128, 512], F32)
        sinr = c.sb("sinr", [128, 512], F32)
        re1 = c.sb("re1", [128, 512], F32)
        re2 = c.sb("re2", [128, 512], F32)
        sc64 = c.sb("sc64", [128, 64], F32)
        s1 = c.sb("s1", [128, 64], F32)
        s2 = c.sb("s2", [128, 64], F32)
        nmx = c.sb("nmx", [128, 16], F32)
        ssum = c.sb("ssum", [128, 16], F32)
        rsum = c.sb("rsum", [128, 16], F32)
        a1 = c.sb("a1", [128, D], F32)
        a2 = c.sb("a2", [128, D], F32)
        ab = c.sb("ab", [128, D], BF16)
        aT = c.sb("aTb", [128, D], BF16)
        rms = c.sb("rms", [128, 8], F32)
        lamc = c.sb("lamc", [128, 1], F32)
        lnwr = c.sb("lnwr", [128, E2], F32)
        lnbr = c.sb("lnbr", [128, E2], F32)
        epst = c.sb("epst", [128, 1], F32)
        lrow = c.sb("lrow", [1, 4 * HD], F32)
        lsc = c.sb("lsc", [1, HD], F32)
        lss = c.sb("lss", [1, 4], F32)
        onesb = c.sb("onesb", [1, 128], BF16)
        onesf2 = c.sb("onesf2", [1, 128], F32)
        yst = c.sb("yst", [128, 2 * D], F32)

        psA = c.ps("psA", [128, 1024], BF16)
        psB = c.ps("psB", [128, 1024], BF16)
        pq0 = c.ps("pq0", [128, 512], F32)
        pq1 = c.ps("pq1", [128, 512], F32)
        pk0 = c.ps("pk0", [128, 512], F32)
        pk1 = c.ps("pk1", [128, 512], F32)
        pv0 = c.ps("pv0", [128, 512], F32)
        pv1 = c.ps("pv1", [128, 512], F32)

        s_w = c.sem("s_w")
        s_init = c.sem("s_init")
        s_lam = c.sem("s_lam")
        s_lrep = c.sem("s_lrep")
        s_mmq = c.sem("s_mmq")
        s_qe = c.sem("s_qe")
        s_rep = c.sem("s_rep")
        s_cs = c.sem("s_cs")
        s_rot = c.sem("s_rot")
        s_sc = c.sem("s_sc")
        s_sm = c.sem("s_sm")
        s_at = c.sem("s_at")
        s_nrm = c.sem("s_nrm")
        s_abt = c.sem("s_abt")
        s_aev = c.sem("s_aev")
        s_omm = c.sem("s_omm")
        s_y = c.sem("s_y")
        s_fen = c.sem("s_fen")
        s_sfen = c.sem("s_sfen")
        r_cs = DmaRing(c, "csr_r", 2)
        r_st = DmaRing(c, "str", 2)

        xr = x_h.rearrange("(m p) d -> m p d", p=128)
        yr = y_h.rearrange("(m p) d -> m p d", p=128)
        cstr = cst_h.rearrange("(m p) f -> m p f", p=128)
        sntr = snt_h.rearrange("(m p) f -> m p f", p=128)

        with nc.Block() as block:
            ident, s_ident = emit_identity(c, block)

            @block.gpsimd
            def _(g):
                for wsb, wh in ((wqs, wq_h), (wks, wk_h), (wvs, wv_h), (wos, wo_h)):
                    rr = wh.rearrange("(a p) q -> a p q", p=128)
                    for a in range(NKD):
                        s_w.inc(g.dma_start(wsb[:, a * D:(a + 1) * D], rr[a]), 16)
                s_w.inc(g.dma_start(lrow[:, 0:HD],
                                    lq1_h.rearrange("(o d) -> o d", o=1)), 16)
                s_w.inc(g.dma_start(lrow[:, HD:2 * HD],
                                    lk1_h.rearrange("(o d) -> o d", o=1)), 16)
                s_w.inc(g.dma_start(lrow[:, 2 * HD:3 * HD],
                                    lq2_h.rearrange("(o d) -> o d", o=1)), 16)
                s_w.inc(g.dma_start(lrow[:, 3 * HD:4 * HD],
                                    lk2_h.rearrange("(o d) -> o d", o=1)), 16)

            @block.vector
            def _(v):
                s_init.inc(v.memset(epst[:, :], 1e-8))
                s_init.inc(v.memset(onesb[:, :], 1.0))
                s_init.inc(v.memset(onesf2[:, :], 1.0))

            @block.vector
            def _(v):
                def F(ins):
                    s_fen.inc(ins)
                    v.wait_ge(s_fen.h, s_fen.n)
                v.wait_ge(s_w.h, s_w.n)
                F(v.tensor_tensor(lsc[:, :], lrow[:, 0:HD], lrow[:, HD:2 * HD],
                                  ALU.mult))
                F(v.reduce_sum(lss[:, 0:1], lsc[:, :], axis=AX.X))
                F(v.tensor_tensor(lsc[:, :], lrow[:, 2 * HD:3 * HD],
                                  lrow[:, 3 * HD:4 * HD], ALU.mult))
                F(v.reduce_sum(lss[:, 1:2], lsc[:, :], axis=AX.X))
                s_lam.inc(v.nop())

            @block.scalar
            def _(s):
                s.wait_ge(s_lam.h, s_lam.n)
                s_lam.inc(s.activation(lss[:, 2:3], lss[:, 0:1], AF.Exp))
                s.wait_ge(s_lam.h, s_lam.n)
                s_lam.inc(s.activation(lss[:, 3:4], lss[:, 1:2], AF.Exp))

            @block.vector
            def _(v):
                def F(ins):
                    s_fen.inc(ins)
                    v.wait_ge(s_fen.h, s_fen.n)
                v.wait_ge(s_lam.h, s_lam.n)
                F(v.tensor_tensor(lss[:, 0:1], lss[:, 3:4], lss[:, 2:3],
                                  ALU.subtract))
                F(v.tensor_scalar_add(lss[:, 0:1], lss[:, 0:1], -LI))
                s_lam.inc(v.nop())

            @block.gpsimd
            def _(g):
                g.wait_ge(s_lam.h, s_lam.n)
                s_w.inc(g.dma_start(lrow[:, 0:E2],
                                    lnw_h.rearrange("(o d) -> o d", o=1)), 16)

            @block.tensor
            def _(t):
                t.wait_ge(s_lam.h, s_lam.n)
                t.wait_ge(s_init.h, s_init.n)
                t.wait_ge(s_w.h, s_w.n)
                s_rep.inc(t.matmul(pq0[:, 0:1], onesf2[:, :], lss[:, 0:1],
                                   start=True, stop=True))
                s_rep.inc(t.matmul(pq0[:, 1:1 + E2], onesf2[:, :], lrow[:, 0:E2],
                                   start=True, stop=True))

            @block.vector
            def _(v):
                v.wait_ge(s_rep.h, s_rep.n)
                s_rep.inc(v.tensor_copy(lamc[:, :], pq0[:, 0:1]))
                s_rep.inc(v.tensor_scalar_mul(lnwr[:, :], pq0[:, 1:1 + E2], 1.0 - LI))

            @block.gpsimd
            def _(g):
                g.wait_ge(s_rep.h, s_rep.n)
                s_w.inc(g.dma_start(lrow[:, 0:E2],
                                    lnb_h.rearrange("(o d) -> o d", o=1)), 16)

            @block.tensor
            def _(t):
                t.wait_ge(s_w.h, s_w.n)
                s_rep.inc(t.matmul(pq0[:, 1:1 + E2], onesf2[:, :], lrow[:, 0:E2],
                                   start=True, stop=True))

            @block.vector
            def _(v):
                v.wait_ge(s_rep.h, s_rep.n)
                s_rep.inc(v.tensor_scalar_mul(lnbr[:, :], pq0[:, 1:1 + E2], 1.0 - LI))

            pre = PrelnRing(c, "ln", x_h, ident, s_ident, psA, psB, nslots=2, nx=4)
            pre.emit_stats(block)
            pre.emit_x2(block, consumer=lambda m: [(s_y.h, 2 * m + 2)])

            for m in range(NTT):
                @block.vector
                def _(v, m=m):
                    pre.apply(v, m)

                @block.tensor
                def _(t, m=m):
                    pre.transposes(t, m)

                @block.scalar
                def _(s, m=m):
                    pre.evac(s, m)

                @block.gpsimd
                def _(g, m=m):
                    if m >= 1:
                        g.wait_ge(s_cs.h, s_cs.n)
                    r_cs.load(g, 2 * m, csr[:, 0:32], cstr[m])
                    r_cs.load(g, 2 * m + 1, csr[:, 32:64], sntr[m])

                @block.tensor
                def _(t, m=m):
                    t.wait_ge(pre.s_tile.h, pre.s_tile.n)
                    if m == 0:
                        t.wait_ge(s_w.h, s_w.n)
                        t.wait_ge(s_rep.h, s_rep.n)
                    else:
                        t.wait_ge(s_y.h, s_y.n)
                        t.wait_ge(s_qe.h, s_qe.n)
                    for half in range(2):
                        for wsb, pf in ((wqs, (pq0, pq1)), (wks, (pk0, pk1)),
                                        (wvs, (pv0, pv1))):
                            for k in range(NKD):
                                ins = t.matmul(
                                    pf[half][:, :],
                                    pre.lnT_tile(k, m),
                                    wsb[:, k * D + half * 512:k * D + (half + 1) * 512],
                                    start=(k == 0), stop=(k == NKD - 1))
                            s_mmq.inc(ins)

                @block.scalar
                def _(s, m=m):
                    s.wait_ge(s_mmq.h, s_mmq.n)
                    cp = s.copy if hasattr(s, "copy") else s.tensor_copy
                    s_qe.inc(cp(qsb[:, 0:512], pq0[:, :]))
                    s_qe.inc(cp(qsb[:, 512:1024], pq1[:, :]))
                    s_qe.inc(cp(ksb[:, 0:512], pk0[:, :]))
                    s_qe.inc(cp(ksb[:, 512:1024], pk1[:, :]))
                    s_qe.inc(cp(vsb[:, 0:512], pv0[:, :]))
                    s_qe.inc(cp(vsb[:, 512:1024], pv1[:, :]))
                    r_cs.wait(s, 2 * m)
                    r_cs.wait(s, 2 * m + 1)
                    cix = []
                    for hh in range(16):
                        cix.append(cp(cosr[:, hh * 32:(hh + 1) * 32], csr[:, 0:32]))
                        cix.append(cp(sinr[:, hh * 32:(hh + 1) * 32], csr[:, 32:64]))
                    s_cs.inc(cix[-2])
                    s_cs.inc(cix[-1])

                @block.vector
                def _(v, m=m):
                    def F(ins):
                        s_fen.inc(ins)
                        v.wait_ge(s_fen.h, s_fen.n)
                    v.wait_ge(s_qe.h, s_qe.n)
                    v.wait_ge(s_cs.h, s_cs.n)
                    r3 = lambda t: t.rearrange("p (o g) -> p o g", o=1)
                    cos3, sin3 = r3(cosr[:, :]), r3(sinr[:, :])
                    re1v, re2v = r3(re1[:, :]), r3(re2[:, :])
                    for tsb in (qsb, ksb):
                        tv = tsb[:, :].rearrange("p (g two) -> p two g", two=2)
                        te, to = tv[:, 0:1, :], tv[:, 1:2, :]
                        F(v.tensor_tensor(re1v, te, cos3, ALU.mult))
                        F(v.tensor_tensor(re2v, to, sin3, ALU.mult))
                        F(v.tensor_tensor(re2v, re1v, re2v, ALU.subtract))
                        F(v.tensor_tensor(re1v, to, cos3, ALU.mult))
                        F(v.tensor_tensor(to, te, sin3, ALU.mult))
                        F(v.tensor_tensor(to, re1v, to, ALU.add))
                        F(v.tensor_copy(te, re2v))
                    F(v.tensor_scalar_mul(qsb[:, :], qsb[:, :], SC))
                    s_rot.inc(v.nop())

                @block.vector
                def _(v, m=m):
                    def F(ins):
                        s_fen.inc(ins)
                        v.wait_ge(s_fen.h, s_fen.n)
                    v.wait_ge(s_rot.h, s_rot.n)
                    for att, ssb in ((0, s1), (1, s2)):
                        for n in range(8):
                            qb = qsb[:, (2 * n + att) * HD:(2 * n + att + 1) * HD]
                            for mh in range(8):
                                kb = ksb[:, (2 * mh + att) * HD:(2 * mh + att + 1) * HD]
                                F(v.tensor_tensor(sc64[:, :], qb, kb, ALU.mult))
                                F(v.reduce_sum(ssb[:, n * 8 + mh:n * 8 + mh + 1],
                                               sc64[:, :], axis=AX.X))
                    rix = []
                    for att, ssb in ((0, s1), (1, s2)):
                        for n in range(8):
                            rix.append(v.reduce_max(
                                nmx[:, att * 8 + n:att * 8 + n + 1],
                                ssb[:, n * 8:(n + 1) * 8], axis=AX.X,
                                negate=True))
                    s_sc.inc(rix[-2])
                    s_sc.inc(rix[-1])

                @block.scalar
                def _(s, m=m):
                    s.wait_ge(s_sc.h, s_sc.n)
                    eix = []
                    for att, ssb in ((0, s1), (1, s2)):
                        for n in range(8):
                            j = att * 8 + n
                            eix.append(s.activation(ssb[:, n * 8:(n + 1) * 8],
                                                    ssb[:, n * 8:(n + 1) * 8], AF.Exp,
                                                    bias=nmx[:, j:j + 1],
                                                    accum_out=ssum[:, j:j + 1]))
                    s_sm.inc(eix[-2])
                    s_sm.inc(eix[-1])

                @block.vector
                def _(v, m=m):
                    v.wait_ge(s_sm.h, s_sm.n)
                    s_sm.inc(v.reciprocal(rsum[:, :], ssum[:, :]))
                    v.wait_ge(s_sm.h, s_sm.n)
                    def F(ins):
                        s_fen.inc(ins)
                        v.wait_ge(s_fen.h, s_fen.n)
                    for att, ssb, asb in ((0, s1, a1), (1, s2, a2)):
                        for n in range(8):
                            blk = asb[:, n * E2:(n + 1) * E2]
                            for mh in range(8):
                                awc = ssb[:, n * 8 + mh:n * 8 + mh + 1]
                                vb = vsb[:, mh * E2:(mh + 1) * E2]
                                if mh == 0:
                                    F(v.tensor_scalar_mul(blk, vb, awc))
                                else:
                                    F(v.scalar_tensor_tensor(blk, vb, awc, blk,
                                                             ALU.mult, ALU.add))
                    for n in range(8):
                        F(v.tensor_scalar_mul(a1[:, n * E2:(n + 1) * E2],
                                              a1[:, n * E2:(n + 1) * E2],
                                              rsum[:, n:n + 1]))
                        F(v.tensor_scalar_mul(a2[:, n * E2:(n + 1) * E2],
                                              a2[:, n * E2:(n + 1) * E2],
                                              rsum[:, 8 + n:8 + n + 1]))
                        F(v.scalar_tensor_tensor(a1[:, n * E2:(n + 1) * E2],
                                                 a2[:, n * E2:(n + 1) * E2],
                                                 lamc[:, 0:1],
                                                 a1[:, n * E2:(n + 1) * E2],
                                                 ALU.mult, ALU.add))
                    s_at.inc(v.nop())

                @block.scalar
                def _(s, m=m):
                    s.wait_ge(s_at.h, s_at.n)
                    qix = []
                    for n in range(8):
                        qix.append(s.activation(re1[:, n * 64:(n + 1) * 64],
                                                a1[:, n * E2 + 0:n * E2 + 64],
                                                AF.Square, accum_out=rms[:, n:n + 1]))
                    s_sfen.inc(qix[-2])
                    s_sfen.inc(qix[-1])
                    s.wait_ge(s_sfen.h, s_sfen.n)
                    qix = []
                    for n in range(8):
                        qix.append(s.activation(re2[:, n * 64:(n + 1) * 64],
                                                a1[:, n * E2 + 64:n * E2 + 128],
                                                AF.Square, accum_out=ssum[:, n:n + 1]))
                    s_sfen.inc(qix[-2])
                    s_sfen.inc(qix[-1])

                @block.vector
                def _(v, m=m):
                    def F(ins):
                        s_fen.inc(ins)
                        v.wait_ge(s_fen.h, s_fen.n)
                    v.wait_ge(s_sfen.h, s_sfen.n)
                    F(v.tensor_tensor(rms[:, :], rms[:, :], ssum[:, 0:8], ALU.add))
                    F(v.tensor_scalar_mul(rms[:, :], rms[:, :], 1.0 / E2))
                    s_nrm.inc(v.nop())

                @block.scalar
                def _(s, m=m):
                    s.wait_ge(s_nrm.h, s_nrm.n)
                    s_nrm.inc(s.activation(rms[:, :], rms[:, :], AF.Sqrt,
                                           bias=epst[:, 0:1]))

                @block.vector
                def _(v, m=m):
                    def F(ins):
                        s_fen.inc(ins)
                        v.wait_ge(s_fen.h, s_fen.n)
                    v.wait_ge(s_nrm.h, s_nrm.n)
                    F(v.reciprocal(rms[:, :], rms[:, :]))
                    for n in range(8):
                        blk = a1[:, n * E2:(n + 1) * E2]
                        F(v.tensor_scalar_mul(blk, blk, rms[:, n:n + 1]))
                        F(v.tensor_tensor(blk, blk, lnwr[:, :], ALU.mult))
                        F(v.tensor_tensor(ab[:, n * E2:(n + 1) * E2], blk,
                                          lnbr[:, :], ALU.add))
                    s_abt.inc(v.nop())

                @block.tensor
                def _(t, m=m):
                    t.wait_ge(s_abt.h, s_abt.n)
                    if m >= 1:
                        t.wait_ge(s_aev.h, s_aev.n)
                    pa = psA if m % 2 == 0 else psB
                    for k in range(NKD):
                        ins = t.transpose(pa[:, k * 128:(k + 1) * 128],
                                          ab[:, k * 128:(k + 1) * 128],
                                          ident[:, :])
                    s_abt.inc(ins)

                @block.scalar
                def _(s, m=m):
                    s.wait_ge(s_abt.h, s_abt.n)
                    cp = s.copy if hasattr(s, "copy") else s.tensor_copy
                    pa = psA if m % 2 == 0 else psB
                    s_aev.inc(cp(aT[:, :], pa[:, 0:1024]))

                @block.tensor
                def _(t, m=m):
                    t.wait_ge(s_aev.h, s_aev.n)
                    for half in range(2):
                        pf = pq0 if half == 0 else pq1
                        for k in range(NKD):
                            ins = t.matmul(
                                pf[:, :], aT[:, k * 128:(k + 1) * 128],
                                wos[:, k * D + half * 512:k * D + (half + 1) * 512],
                                start=(k == 0), stop=(k == NKD - 1))
                        s_omm.inc(ins)

                @block.vector
                def _(v, m=m):
                    v.wait_ge(s_omm.h, s_omm.n)
                    pre.r_x2.wait(v, m)
                    if m >= 2:
                        r_st.wait(v, m - 2)
                    for half in range(2):
                        pf = pq0 if half == 0 else pq1
                        s_y.inc(v.tensor_tensor(
                            yst[:, (m % 2) * D + half * 512:(m % 2) * D + (half + 1) * 512],
                            pre.xring[:, (m % pre.nx) * D + half * 512:(m % pre.nx) * D + (half + 1) * 512],
                            pf[:, :], ALU.add))

                @block.gpsimd
                def _(g, m=m):
                    g.wait_ge(s_y.h, s_y.n)
                    r_st.load(g, m, yr[m], yst[:, (m % 2) * D:(m % 2 + 1) * D])

            @block.gpsimd
            def _(g):
                for s_ in r_st.sems:
                    g.wait_ge(s_.h, s_.n)
    return nc



def build_l3attn(ts=None):
    TS = ts if ts is not None else T

    nc = bass.Bass("TRN2", target_bir_lowering=False)
    x_h = nc.dram_tensor("x", [T, D], F32, kind="ExternalInput").ap()
    a_h = nc.dram_tensor("A", [D, D], F32, kind="ExternalInput").ap()
    b_h = nc.dram_tensor("B", [D, D], F32, kind="ExternalInput").ap()
    win_h = nc.dram_tensor("W_in", [D, D], F32, kind="ExternalInput").ap()
    bin_h = nc.dram_tensor("b_in", [D], F32, kind="ExternalInput").ap()
    wg_h = nc.dram_tensor("W_gate", [D, D], F32, kind="ExternalInput").ap()
    bg_h = nc.dram_tensor("b_gate", [D], F32, kind="ExternalInput").ap()
    wo_h = nc.dram_tensor("W_out", [D, D], F32, kind="ExternalInput").ap()
    bo_h = nc.dram_tensor("b_out", [D], F32, kind="ExternalInput").ap()
    y_h = nc.dram_tensor("y", [T, D], F32, kind="ExternalOutput").ap()
    igu_d = nc.dram_tensor("igu_scr", [T, 3 * D], F32, kind="Internal").ap()
    srow_d = nc.dram_tensor("srow_scr", [T, D], F32, kind="Internal").ap()

    with ExitStack() as ctx:
        c = Ctx(nc, ctx)
        winT = c.sb("winT", [128, NKD * D], BF16)
        wgT = c.sb("wgT", [128, NKD * D], BF16)
        bT = c.sb("bT", [128, NKD * D], BF16)
        aT = c.sb("aT", [128, NKD * D], BF16)
        woT = winT
        wstg = c.sb("wstg", [128, 2 * D], BF16)
        binrep = c.sb("binrep", [128, D], F32)
        bgrep = c.sb("bgrep", [128, D], F32)
        borep = c.sb("borep", [128, D], F32)
        onesf = c.sb("onesf3", [1, 128], F32)
        inpf = c.sb("inpf", [128, D], F32)
        inpb = c.sb("inpb", [128, D], BF16)
        inpT = c.sb("inpT", [128, D], BF16)
        gsb = c.sb("gsb", [128, 2 * D], F32)
        usb = c.sb("usb", [128, 2 * D], F32)
        gif = c.sb("gif", [128, 2 * D], F32)
        t1f = c.sb("t1f", [128, D], F32)
        igrow = c.sb("igrow", [1, 2 * 3 * D], F32)
        srow = c.sb("srow", [1, 2 * D], F32)
        trow = c.sb("trow", [1, D], F32)
        brow = trow
        t3row = c.sb("t3row", [1, D], F32)
        scol = c.sb("scol", [128, 2 * NKD], BF16)
        stb = c.sb("stb", [128, 2 * D], BF16)
        stT = c.sb("stT", [128, D], BF16)
        x3b = c.sb("x3b", [128, 2 * D], F32)
        yst = c.sb("yst", [128, 2 * D], F32)

        psA = c.ps("psA", [128, 1024], BF16)
        psB = c.ps("psB", [128, 1024], BF16)
        p0 = c.ps("p0", [128, 512], F32)
        p1 = c.ps("p1", [128, 512], F32)
        p2 = c.ps("p2", [128, 512], F32)
        p3 = c.ps("p3", [128, 512], F32)

        s_init = c.sem("s_init")
        s_brl = c.sem("s_brl")
        s_rep = c.sem("s_rep")
        s_wT = c.sem("s_wT")
        s_wE = c.sem("s_wE")
        s_mmA = c.sem("s_mmA")
        s_evA = c.sem("s_evA")
        s_trA = c.sem("s_trA")
        s_teA = c.sem("s_teA")
        s_mm2 = c.sem("s_mm2")
        s_ev2 = c.sem("s_ev2")
        s_sig = c.sem("s_sig")
        s_gi = c.sem("s_gi")
        s_mmB = c.sem("s_mmB")
        s_pr = c.sem("s_pr")
        s_tnh = c.sem("s_tnh")
        s_v3 = c.sem("s_v3")
        s_stp = c.sem("s_stp")
        s_trC = c.sem("s_trC")
        s_teC = c.sem("s_teC")
        s_mmC = c.sem("s_mmC")
        s_y = c.sem("s_y")
        s_fen = c.sem("s_fen")
        r_w = DmaRing(c, "wld", 2)
        r_s3 = DmaRing(c, "s3r", 2)
        r_ig = DmaRing(c, "igr", 2)
        r_sb = DmaRing(c, "sbr", 2)
        r_sc = DmaRing(c, "scr", 2)
        r_stb = DmaRing(c, "stbr", 2)
        r_x3 = DmaRing(c, "x3r", 2)
        r_st = DmaRing(c, "str", 2)

        xr = x_h.rearrange("(m p) d -> m p d", p=128)
        yr = y_h.rearrange("(m p) d -> m p d", p=128)
        igur = igu_d.rearrange("(m p) c -> m p c", p=128)
        srowr = srow_d.rearrange("(m p) c -> m p c", p=128)

        with nc.Block() as block:
            ident, s_ident = emit_identity(c, block)

            @block.vector
            def _(v):
                s_init.inc(v.memset(onesf[:, :], 1.0))

            wlist = [(winT, win_h), (wgT, wg_h), (bT, b_h), (aT, a_h)]
            for wi, (wT, wh) in enumerate(wlist):
                rr = wh.rearrange("(a p) q -> a p q", p=128)
                for a in range(NKD):
                    gi_ = wi * NKD + a

                    @block.gpsimd
                    def _(g, gi_=gi_, a=a, rr=rr):
                        if gi_ >= 2:
                            g.wait_ge(s_wT.h, NKD * (gi_ - 1))
                        r_w.load(g, gi_, wstg[:, (gi_ % 2) * D:(gi_ % 2 + 1) * D],
                                 rr[a])

                    @block.tensor
                    def _(t, gi_=gi_, a=a):
                        r_w.wait(t, gi_)
                        if gi_ == 0:
                            t.wait_ge(s_ident.h, s_ident.n)
                        for kc in range(NKD):
                            i = gi_ * NKD + kc
                            if i >= 2:
                                t.wait_ge(s_wE.h, i - 1)
                            pb = psA if i % 2 == 0 else psB
                            s_wT.inc(t.transpose(
                                pb[:, 0:128],
                                wstg[:, (gi_ % 2) * D + kc * 128:(gi_ % 2) * D + (kc + 1) * 128],
                                ident[:, :]))

                    @block.vector
                    def _(v, gi_=gi_, a=a, wT=wT):
                        for kc in range(NKD):
                            i = gi_ * NKD + kc
                            v.wait_ge(s_wT.h, i + 1)
                            pb = psA if i % 2 == 0 else psB
                            s_wE.inc(v.tensor_copy(
                                wT[:, kc * D + a * 128:kc * D + (a + 1) * 128],
                                pb[:, 0:128]))

            for bi, (bh, brp) in enumerate([(bin_h, binrep), (bg_h, bgrep),
                                            (bo_h, borep)]):
                @block.gpsimd
                def _(g, bh=bh, bi=bi):
                    if bi >= 1:
                        g.wait_ge(s_rep.h, s_rep.n)
                    s_brl.inc(g.dma_start(brow[:, :],
                                          bh.rearrange("(o d) -> o d", o=1)), 16)

                @block.tensor
                def _(t, bi=bi):
                    t.wait_ge(s_brl.h, s_brl.n)
                    t.wait_ge(s_init.h, s_init.n)
                    if bi == 0:
                        t.wait_ge(s_wE.h, 4 * NKD * NKD)
                    else:
                        t.wait_ge(s_rep.h, s_rep.n)
                    s_rep.inc(t.matmul(p0[:, :], onesf[:, :], brow[:, 0:512],
                                       start=True, stop=True))
                    s_rep.inc(t.matmul(p1[:, :], onesf[:, :], brow[:, 512:1024],
                                       start=True, stop=True))

                @block.vector
                def _(v, brp=brp):
                    v.wait_ge(s_rep.h, s_rep.n)
                    s_rep.inc(v.tensor_copy(brp[:, 0:512], p0[:, :]))
                    s_rep.inc(v.tensor_copy(brp[:, 512:1024], p1[:, :]))

            pre = PrelnRing(c, "ln", x_h, ident, s_ident, psA, psB, nslots=2,
                            nx=2, x1_sb=yst)
            pre.emit_stats(block)
            pre.emit_x2(block, consumer=lambda m: [(pre.s_ln.h, m + 1)])

            for m in range(NTT):
                @block.vector
                def _(v, m=m):
                    pre.apply(v, m)

                @block.tensor
                def _(t, m=m):
                    pre.transposes(t, m)

                @block.scalar
                def _(s, m=m):
                    pre.evac(s, m)

                @block.tensor
                def _(t, m=m):
                    t.wait_ge(pre.s_tile.h, pre.s_tile.n)
                    if m == 0:
                        t.wait_ge(s_rep.h, s_rep.n)
                        t.wait_ge(s_wE.h, s_wE.n)
                    else:
                        t.wait_ge(s_evA.h, s_evA.n)
                        t.wait_ge(s_mm2.h, s_mm2.n)
                    for half in range(2):
                        pf = p0 if half == 0 else p1
                        for k in range(NKD):
                            ins = t.matmul(pf[:, :], pre.lnT_tile(k, m),
                                           winT[:, k * D + half * 512:k * D + (half + 1) * 512],
                                           start=(k == 0), stop=(k == NKD - 1))
                        s_mmA.inc(ins)

                @block.vector
                def _(v, m=m):
                    def F(ins):
                        s_fen.inc(ins)
                        v.wait_ge(s_fen.h, s_fen.n)
                    v.wait_ge(s_mmA.h, s_mmA.n)
                    F(v.tensor_tensor(inpf[:, 0:512], p0[:, :], binrep[:, 0:512],
                                      ALU.add))
                    F(v.tensor_tensor(inpf[:, 512:1024], p1[:, :],
                                      binrep[:, 512:1024], ALU.add))
                    F(v.tensor_copy(inpb[:, :], inpf[:, :]))
                    s_evA.inc(v.nop())

                @block.tensor
                def _(t, m=m):
                    t.wait_ge(s_evA.h, s_evA.n)
                    if m >= 1:
                        t.wait_ge(s_teA.h, s_teA.n)
                    pa = psA if m % 2 == 0 else psB
                    for k in range(NKD):
                        ins = t.transpose(pa[:, k * 128:(k + 1) * 128],
                                          inpb[:, k * 128:(k + 1) * 128],
                                          ident[:, :])
                    s_trA.inc(ins)

                @block.scalar
                def _(s, m=m):
                    s.wait_ge(s_trA.h, s_trA.n)
                    cp = s.copy if hasattr(s, "copy") else s.tensor_copy
                    pa = psA if m % 2 == 0 else psB
                    s_teA.inc(cp(inpT[:, :], pa[:, 0:1024]))

                @block.tensor
                def _(t, m=m):
                    t.wait_ge(s_teA.h, s_teA.n)
                    if m >= 1:
                        t.wait_ge(s_gi.h, s_gi.n)
                    for half in range(2):
                        pf = p2 if half == 0 else p3
                        for k in range(NKD):
                            ins = t.matmul(pf[:, :], inpT[:, k * 128:(k + 1) * 128],
                                           wgT[:, k * D + half * 512:k * D + (half + 1) * 512],
                                           start=(k == 0), stop=(k == NKD - 1))
                        s_mm2.inc(ins)
                    for half in range(2):
                        pf = p0 if half == 0 else p1
                        for k in range(NKD):
                            ins = t.matmul(pf[:, :], inpT[:, k * 128:(k + 1) * 128],
                                           bT[:, k * D + half * 512:k * D + (half + 1) * 512],
                                           start=(k == 0), stop=(k == NKD - 1))
                        s_mm2.inc(ins)

                @block.vector
                def _(v, m=m):
                    def F(ins):
                        s_fen.inc(ins)
                        v.wait_ge(s_fen.h, s_fen.n)
                    v.wait_ge(s_mm2.h, s_mm2.n)
                    if m >= 2:
                        v.wait_ge(r_s3.sems[m % 2].h,
                                  48 * ((m - 2) // 2 + 1))
                    sl = (m % 2) * D
                    F(v.tensor_tensor(t1f[:, 0:512], p2[:, :], bgrep[:, 0:512],
                                      ALU.add))
                    F(v.tensor_tensor(t1f[:, 512:1024], p3[:, :],
                                      bgrep[:, 512:1024], ALU.add))
                    F(v.tensor_copy(usb[:, sl:sl + 512], p0[:, :]))
                    F(v.tensor_copy(usb[:, sl + 512:sl + 1024], p1[:, :]))
                    s_ev2.inc(v.nop())

                @block.scalar
                def _(s, m=m):
                    s.wait_ge(s_ev2.h, s_ev2.n)
                    sl = (m % 2) * D
                    s_sig.inc(s.activation(gsb[:, sl:sl + D], t1f[:, :], AF.Sigmoid))

                @block.vector
                def _(v, m=m):
                    def F(ins):
                        s_fen.inc(ins)
                        v.wait_ge(s_fen.h, s_fen.n)
                    v.wait_ge(s_sig.h, s_sig.n)
                    sl = (m % 2) * D
                    F(v.tensor_tensor(gif[:, sl:sl + D], gsb[:, sl:sl + D],
                                      inpf[:, :], ALU.mult))
                    F(v.tensor_tensor(gif[:, sl:sl + D], inpf[:, :],
                                      gif[:, sl:sl + D], ALU.subtract))
                    s_gi.inc(v.nop())

                @block.gpsimd
                def _(g, m=m):
                    g.wait_ge(s_gi.h, s_gi.n)
                    sl = (m % 2) * D
                    sset = r_s3.sems[m % 2]
                    sset.inc(g.dma_start(igur[m][:, 0:D], usb[:, sl:sl + D]), 16)
                    sset.inc(g.dma_start(igur[m][:, D:2 * D], gsb[:, sl:sl + D]), 16)
                    sset.inc(g.dma_start(igur[m][:, 2 * D:3 * D],
                                         gif[:, sl:sl + D]), 16)

            worr = wo_h.rearrange("(a p) q -> a p q", p=128)
            for a in range(NKD):
                gi_ = 4 * NKD + a

                @block.gpsimd
                def _(g, gi_=gi_, a=a):
                    g.wait_ge(s_mmA.h, 2 * NTT)
                    if a >= 2:
                        g.wait_ge(s_wT.h, NKD * (gi_ - 1))
                    r_w.load(g, gi_, wstg[:, (gi_ % 2) * D:(gi_ % 2 + 1) * D],
                             worr[a])

                @block.tensor
                def _(t, gi_=gi_, a=a):
                    r_w.wait(t, gi_)
                    if a == 0:
                        t.wait_ge(s_teA.h, NTT)
                    for kc in range(NKD):
                        i = gi_ * NKD + kc
                        if i >= 2:
                            t.wait_ge(s_wE.h, i - 1)
                        pb = psA if i % 2 == 0 else psB
                        s_wT.inc(t.transpose(
                            pb[:, 0:128],
                            wstg[:, (gi_ % 2) * D + kc * 128:(gi_ % 2) * D + (kc + 1) * 128],
                            ident[:, :]))

                @block.vector
                def _(v, gi_=gi_, a=a):
                    for kc in range(NKD):
                        i = gi_ * NKD + kc
                        v.wait_ge(s_wT.h, i + 1)
                        pb = psA if i % 2 == 0 else psB
                        s_wE.inc(v.tensor_copy(
                            woT[:, kc * D + a * 128:kc * D + (a + 1) * 128],
                            pb[:, 0:128]))

            @block.vector
            def _(v):
                v.wait_ge(s_gi.h, s_gi.n)

            for t_ in range(TS):
                sl3 = (t_ % 2) * 3 * D
                sls = (t_ % 2) * D

                @block.sync
                def _(sy, t_=t_, sl3=sl3):
                    if t_ >= 2:
                        sy.wait_ge(s_stp.h, t_ - 1)
                    r_ig.load(sy, t_, igrow[:, sl3:sl3 + 3 * D],
                              igu_d[t_:t_ + 1, :])

                if t_ > 0:
                    @block.tensor
                    def _(t, t_=t_):
                        r_sc.wait(t, t_ - 1)
                        if t_ >= 2:
                            t.wait_ge(s_pr.h, 2 * (t_ - 1))
                        pfa = p0 if t_ % 2 == 0 else p2
                        pfb = p1 if t_ % 2 == 0 else p3
                        if t_ == 1:
                            t.wait_ge(s_gi.h, s_gi.n)
                        for k in range(NKD):
                            ins = t.matmul(pfa[0:1, :],
                                           scol[:, ((t_ - 1) % 2) * NKD + k:((t_ - 1) % 2) * NKD + k + 1],
                                           aT[:, k * D:k * D + 512],
                                           start=(k == 0), stop=(k == NKD - 1))
                        s_mmB.inc(ins)
                        for k in range(NKD):
                            ins = t.matmul(pfb[0:1, :],
                                           scol[:, ((t_ - 1) % 2) * NKD + k:((t_ - 1) % 2) * NKD + k + 1],
                                           aT[:, k * D + 512:(k + 1) * D],
                                           start=(k == 0), stop=(k == NKD - 1))
                        s_mmB.inc(ins)

                @block.vector
                def _(v, t_=t_, sl3=sl3):
                    r_ig.wait(v, t_)
                    if t_ >= 1:
                        v.wait_ge(s_tnh.h, t_)
                    if t_ == 0:
                        s_pr.inc(v.tensor_copy(t1f[0:1, 0:512],
                                               igrow[:, sl3:sl3 + 512]))
                        s_pr.inc(v.tensor_copy(t1f[0:1, 512:1024],
                                               igrow[:, sl3 + 512:sl3 + D]))
                    else:
                        v.wait_ge(s_mmB.h, s_mmB.n)
                        pfa = p0 if t_ % 2 == 0 else p2
                        pfb = p1 if t_ % 2 == 0 else p3
                        s_pr.inc(v.tensor_tensor(
                            t1f[0:1, 0:512], pfa[0:1, :],
                            igrow[:, sl3:sl3 + 512], ALU.add))
                        s_pr.inc(v.tensor_tensor(
                            t1f[0:1, 512:1024], pfb[0:1, :],
                            igrow[:, sl3 + 512:sl3 + D], ALU.add))

                @block.scalar
                def _(s, t_=t_):
                    s.wait_ge(s_pr.h, 2 * (t_ + 1))
                    s_tnh.inc(s.activation(trow[:, :], t1f[0:1, 0:1024], AF.Tanh))

                @block.vector
                def _(v, t_=t_, sl3=sl3, sls=sls):
                    v.wait_ge(s_tnh.h, s_tnh.n)
                    if t_ >= 2:
                        r_sb.wait(v, t_ - 2)
                    s_v3.inc(v.tensor_tensor(t3row[:, :], trow[:, :],
                                             igrow[:, sl3 + D:sl3 + 2 * D],
                                             ALU.mult))
                    v.wait_ge(s_v3.h, s_v3.n)
                    s_stp.inc(v.tensor_tensor(srow[:, sls:sls + D], t3row[:, :],
                                              igrow[:, sl3 + 2 * D:sl3 + 3 * D],
                                              ALU.add))

                @block.gpsimd
                def _(g, t_=t_, sls=sls):
                    g.wait_ge(s_stp.h, t_ + 1)
                    r_sb.load(g, t_, srow_d[t_:t_ + 1, :], srow[:, sls:sls + D])
                    if t_ < TS - 1:
                        r_sb.wait(g, t_)
                        r_sc.load(g, t_, scol[:, (t_ % 2) * NKD:(t_ % 2 + 1) * NKD],
                                  srow_d[t_].rearrange("(k p) -> p k", p=128),
                                  allow_slow_non_contiguous=True)

            for m in range(NTT):
                if (m + 1) * 128 > TS:
                    break

                @block.gpsimd
                def _(g, m=m):
                    if m == 0:
                        for s_ in r_sb.sems:
                            g.wait_ge(s_.h, s_.n)
                    if m >= 2:
                        g.wait_ge(s_trC.h, s_trC.n - 1)
                    r_stb.load(g, m, stb[:, (m % 2) * D:(m % 2 + 1) * D], srowr[m])
                    if m >= 2:
                        g.wait_ge(s_y.h, s_y.n - 2)
                    r_x3.load(g, m, x3b[:, (m % 2) * D:(m % 2 + 1) * D], xr[m])

                @block.tensor
                def _(t, m=m):
                    r_stb.wait(t, m)
                    if m == 0:
                        t.wait_ge(s_stp.h, TS)
                        t.wait_ge(s_wE.h, 5 * NKD * NKD)
                    if m >= 1:
                        t.wait_ge(s_teC.h, s_teC.n)
                    pa = psA if m % 2 == 0 else psB
                    for k in range(NKD):
                        ins = t.transpose(pa[:, k * 128:(k + 1) * 128],
                                          stb[:, (m % 2) * D + k * 128:(m % 2) * D + (k + 1) * 128],
                                          ident[:, :])
                    s_trC.inc(ins)

                @block.scalar
                def _(s, m=m):
                    s.wait_ge(s_trC.h, s_trC.n)
                    cp = s.copy if hasattr(s, "copy") else s.tensor_copy
                    pa = psA if m % 2 == 0 else psB
                    s_teC.inc(cp(stT[:, :], pa[:, 0:1024]))

                @block.tensor
                def _(t, m=m):
                    t.wait_ge(s_teC.h, s_teC.n)
                    if m >= 1:
                        t.wait_ge(s_y.h, s_y.n)
                    for half in range(2):
                        pf = p0 if half == 0 else p1
                        for k in range(NKD):
                            ins = t.matmul(pf[:, :], stT[:, k * 128:(k + 1) * 128],
                                           woT[:, k * D + half * 512:k * D + (half + 1) * 512],
                                           start=(k == 0), stop=(k == NKD - 1))
                        s_mmC.inc(ins)

                @block.vector
                def _(v, m=m):
                    def F(ins):
                        s_fen.inc(ins)
                        v.wait_ge(s_fen.h, s_fen.n)
                    v.wait_ge(s_mmC.h, s_mmC.n)
                    r_x3.wait(v, m)
                    if m >= 2:
                        r_st.wait(v, m - 2)
                    for half in range(2):
                        pf = p0 if half == 0 else p1
                        F(v.tensor_tensor(yst[:, (m % 2) * D + half * 512:(m % 2) * D + (half + 1) * 512],
                                          pf[:, :], borep[:, half * 512:(half + 1) * 512],
                                          ALU.add))
                        s_y.inc(v.tensor_tensor(
                            yst[:, (m % 2) * D + half * 512:(m % 2) * D + (half + 1) * 512],
                            x3b[:, (m % 2) * D + half * 512:(m % 2) * D + (half + 1) * 512],
                            yst[:, (m % 2) * D + half * 512:(m % 2) * D + (half + 1) * 512],
                            ALU.add))

                @block.gpsimd
                def _(g, m=m):
                    g.wait_ge(s_y.h, s_y.n)
                    r_st.load(g, m, yr[m], yst[:, (m % 2) * D:(m % 2 + 1) * D])

            @block.gpsimd
            def _(g):
                for s_ in r_st.sems:
                    g.wait_ge(s_.h, s_.n)
    return nc



def build_l0ffn(npass=4):
    HE, HD2 = 8, 64
    INNER = 512
    WIN = 512
    NW = T // WIN
    HID = 2730
    NJ = 22
    QKS = 8.0

    nc = bass.Bass("TRN2", target_bir_lowering=False)
    x_h = nc.dram_tensor("x", [T, D], F32, kind="ExternalInput").ap()
    wqkv_h = nc.dram_tensor("wqkv", [2, D, 3 * INNER], F32, kind="ExternalInput").ap()
    qs_h = nc.dram_tensor("q_scale", [2, HD2], F32, kind="ExternalInput").ap()
    ks_h = nc.dram_tensor("k_scale", [2, HD2], F32, kind="ExternalInput").ap()
    wo_h = nc.dram_tensor("wo", [2, INNER, D], F32, kind="ExternalInput").ap()
    lg_h = nc.dram_tensor("ln_g", [2, D], F32, kind="ExternalInput").ap()
    lb_h = nc.dram_tensor("ln_b", [2, D], F32, kind="ExternalInput").ap()
    w1_h = nc.dram_tensor("w1", [2, D, 2 * HID], F32, kind="ExternalInput").ap()
    b1_h = nc.dram_tensor("b1", [2, 2 * HID], F32, kind="ExternalInput").ap()
    w2_h = nc.dram_tensor("w2", [2, HID, D], F32, kind="ExternalInput").ap()
    b2_h = nc.dram_tensor("b2", [2, D], F32, kind="ExternalInput").ap()
    tbl_h = nc.dram_tensor("tbl", [T, 128], F32, kind="ExternalInput").ap()
    y_h = nc.dram_tensor("y", [T, D], F32, kind="ExternalOutput").ap()
    xsA_d = nc.dram_tensor("xsA", [T, D], F32, kind="Internal").ap()
    xsB_d = nc.dram_tensor("xsB", [T, D], F32, kind="Internal").ap()
    wb1_d = nc.dram_tensor("wb1", [2, D, 2 * HID], BF16, kind="Internal").ap()

    with ExitStack() as ctx:
        c = Ctx(nc, ctx)
        arA = c.sb("arA", [128, NJ * D], BF16)
        arB = c.sb("arB", [128, NJ * 512], BF16)
        arC = c.sb("arC", [128, 4096], BF16)
        arD = c.sb("arD", [128, 4096], BF16)
        vw = c.sb("vw", [128, 4096], BF16)
        qb = c.sb("qb", [128, 2048], BF16)
        kb = c.sb("kb", [128, 2048], BF16)
        awT = c.sb("awT", [128, 1024], BF16)
        oTw = c.sb("oTw", [128, 2048], BF16)
        lnbb = c.sb("lnbb", [128, 1024], BF16)
        arE = c.sb("arE", [128, 4096], F32)
        qw = c.sb("qw", [128, 2048], F32)
        kw = c.sb("kw", [128, 2048], F32)
        xnew = c.sb("xnew", [128, 2048], F32)
        sexp = c.sb("sexp", [128, 1024], F32)
        tblb = c.sb("tblb", [128, 2 * 128], F32)
        csb = c.sb("csb", [128, 128], F32)
        nrm = c.sb("nrm", [128, 16], F32)
        sqs2 = c.sb("sqs2", [128, 16], F32)
        stato = c.sb("stato", [128, 16], F32)
        halo = c.sb("halo", [128, 4], BF16)
        b1c = c.sb("b1c", [128, 2 * NJ], F32)
        brow = c.sb("brow", [1, 1024], F32)
        onesf = c.sb("onesf4", [1, 128], F32)
        sscr = c.sb("sscr", [128, 1024], F32)
        xrb = c.sb("xrb", [128, 2048], F32)
        xob = c.sb("xob", [128, 2048], F32)

        psA = c.ps("psA", [128, 1024], BF16)
        psB = c.ps("psB", [128, 1024], BF16)
        p0 = c.ps("p0", [128, 512], F32)
        p1 = c.ps("p1", [128, 512], F32)
        p2 = c.ps("p2", [128, 512], F32)
        p3 = c.ps("p3", [128, 512], F32)
        po0 = c.ps("po0", [128, 512], F32)
        po1 = c.ps("po1", [128, 512], F32)

        S = {}
        for nm in ["init", "w", "brl", "rep", "p0s", "xw", "cast", "trx", "tex",
                   "sht", "mmq", "evq", "tbl", "nrm1", "nrm2", "rot", "cvt",
                   "trq", "teq", "msk", "mms", "mskA", "smax", "sexpS", "rs",
                   "aws", "trA", "teA", "mmo", "teo", "mmw", "yv", "fen", "sfen",
                   "lnS", "lnV", "trL", "teL", "w1l", "mmh", "evh", "gel",
                   "hts", "mmy", "b2v", "wcp"]:
            S[nm] = c.sem("s_" + nm)
        r_x = DmaRing(c, "xld", 4)
        r_tb = DmaRing(c, "tbl", 2)
        r_st = DmaRing(c, "stx", 2)
        r_xt = DmaRing(c, "xtl", 4)
        r_w1 = DmaRing(c, "w1l", 4)
        r_xo = DmaRing(c, "xol", 2)
        r_xr = DmaRing(c, "xrl", 2)

        xr = x_h.rearrange("(m p) d -> m p d", p=128)
        yr = y_h.rearrange("(m p) d -> m p d", p=128)
        xsAr = xsA_d.rearrange("(m p) d -> m p d", p=128)
        xsBr = xsB_d.rearrange("(m p) d -> m p d", p=128)
        tblr = tbl_h.rearrange("(m p) f -> m p f", p=128)

        with nc.Block() as block:
            ident, s_ident = emit_identity(c, block)

            @block.vector
            def _(v):
                S["init"].inc(v.memset(onesf[:, :], 1.0))
                S["init"].inc(v.memset(halo[:, :], 0.0))

            @block.gpsimd
            def _(g):
                for l in range(2):
                    for a in range(NKD):
                        S["wcp"].inc(g.dma_start(
                            wb1_d[l].rearrange("(a p) f -> a p f", p=128)[a],
                            w1_h[l].rearrange("(a p) f -> a p f", p=128)[a]), 16)

            for m in range(NTT):
                @block.gpsimd
                def _(g, m=m):
                    if m >= 4:
                        g.wait_ge(S["p0s"].h, m - 3)
                    r_x.load(g, m, arE[:, (m % 4) * D:(m % 4 + 1) * D], xr[m])

                @block.vector
                def _(v, m=m):
                    def F(ins):
                        S["fen"].inc(ins)
                        v.wait_ge(S["fen"].h, S["fen"].n)
                    r_x.wait(v, m)
                    sl = (m % 4) * D
                    F(v.reduce_sum(stato[:, 0:1], arE[:, sl:sl + D], axis=AX.X))
                    S["lnV"].inc(v.nop())

                @block.scalar
                def _(s, m=m):
                    s.wait_ge(S["lnV"].h, S["lnV"].n)
                    sl = (m % 4) * D
                    S["lnS"].inc(s.activation(sscr[:, 0:1024],
                                              arE[:, sl:sl + D], AF.Square,
                                              accum_out=stato[:, 1:2]))

                @block.vector
                def _(v, m=m):
                    def F(ins):
                        S["fen"].inc(ins)
                        v.wait_ge(S["fen"].h, S["fen"].n)
                    v.wait_ge(S["lnS"].h, S["lnS"].n)
                    sl = (m % 4) * D
                    F(v.tensor_scalar_mul(stato[:, 2:3], stato[:, 0:1], 1.0 / D))
                    F(v.tensor_scalar_mul(stato[:, 3:4], stato[:, 1:2], 1.0 / D))
                    F(v.tensor_tensor(stato[:, 4:5], stato[:, 2:3], stato[:, 2:3],
                                      ALU.mult))
                    F(v.tensor_tensor(stato[:, 3:4], stato[:, 3:4], stato[:, 4:5],
                                      ALU.subtract))
                    F(v.tensor_scalar_add(stato[:, 3:4], stato[:, 3:4], 1e-5))
                    S["lnV"].inc(v.nop())

                @block.scalar
                def _(s, m=m):
                    s.wait_ge(S["lnV"].h, S["lnV"].n)
                    S["lnS"].inc(s.activation(stato[:, 3:4], stato[:, 3:4],
                                              AF.Sqrt))

                @block.vector
                def _(v, m=m):
                    def F(ins):
                        S["fen"].inc(ins)
                        v.wait_ge(S["fen"].h, S["fen"].n)
                    v.wait_ge(S["lnS"].h, S["lnS"].n)
                    sl = (m % 4) * D
                    F(v.reciprocal(stato[:, 3:4], stato[:, 3:4]))
                    if m >= 2:
                        r_st.wait(v, m - 2)
                    F(v.tensor_scalar(xnew[:, (m % 2) * D:(m % 2 + 1) * D],
                                      arE[:, sl:sl + D], stato[:, 2:3],
                                      stato[:, 3:4], ALU.subtract, ALU.mult))
                    S["p0s"].inc(v.nop())

                @block.gpsimd
                def _(g, m=m):
                    g.wait_ge(S["p0s"].h, S["p0s"].n)
                    r_st.load(g, m, xsAr[m], xnew[:, (m % 2) * D:(m % 2 + 1) * D])

            wqkv_sb = arA[:, 0:NKD * 3 * INNER]
            wo_sb = arA[:, NKD * 3 * INNER:NKD * 3 * INNER + 4 * D]
            w2sb = arA
            xwb = arB[:, 0:4096]
            xT = arB[:, 4096:8192]
            shT = arB[:, 8192:10240]
            sexpb = arB[:, 10240:11264]
            hT = arB
            lnTc = arC
            kT2 = arC
            masks = arD
            w1stg = vw
            xwf = arE
            lgrep = qw[:, 0:1024]
            lbrep = qw[:, 1024:2048]
            b2rep = kw[:, 0:1024]
            geb = kw[:, 1024:1536]
            ha = kw[:, 1536:2048]

            def emit_attn_pass(lay, src_r, dst_r, pwait):
                @block.gpsimd
                def _(g):
                    for sh, cnt in pwait:
                        g.wait_ge(sh, cnt)
                    g.wait_ge(S["mmy"].h, S["mmy"].n)
                    rr = wqkv_h[lay].rearrange("(a p) f -> a p f", p=128)
                    for a in range(NKD):
                        S["w"].inc(g.dma_start(
                            wqkv_sb[:, a * 3 * INNER:(a + 1) * 3 * INNER], rr[a]), 16)
                    rr2 = wo_h[lay].rearrange("(a p) f -> a p f", p=128)
                    for a in range(4):
                        S["w"].inc(g.dma_start(wo_sb[:, a * D:(a + 1) * D],
                                               rr2[a]), 16)
                    S["brl"].inc(g.dma_start(brow[:, 0:HD2],
                                             qs_h[lay].rearrange("(o d) -> o d", o=1)), 16)
                    S["brl"].inc(g.dma_start(brow[:, HD2:2 * HD2],
                                             ks_h[lay].rearrange("(o d) -> o d", o=1)), 16)
                    g.wait_ge(S["mskA"].h, S["mskA"].n)
                    S["msk"].inc(g.memset(masks[:, :], 0.0))
                    g.wait_ge(S["msk"].h, S["msk"].n)
                    for mt in range(4):
                        S["msk"].inc(g.affine_select(
                            masks[:, mt * 512:(mt + 1) * 512],
                            masks[:, mt * 512:(mt + 1) * 512], [[1, 512]],
                            ALU.is_ge, -1e30, base=-mt * 128,
                            channel_multiplier=-1))
                        S["msk"].inc(g.affine_select(
                            masks[:, 2048 + mt * 512:2048 + (mt + 1) * 512],
                            masks[:, 2048 + mt * 512:2048 + (mt + 1) * 512],
                            [[-1, 512]], ALU.is_ge, -1e30, base=mt * 128,
                            channel_multiplier=1))

                qsr = sexp[:, 0:64]
                ksr = sexp[:, 64:128]

                @block.tensor
                def _(t):
                    t.wait_ge(S["brl"].h, S["brl"].n)
                    t.wait_ge(S["init"].h, S["init"].n)
                    t.wait_ge(S["yv"].h, S["yv"].n)
                    S["rep"].inc(t.matmul(p2[:, 0:64], onesf[:, :], brow[:, 0:64],
                                          start=True, stop=True))
                    S["rep"].inc(t.matmul(p2[:, 64:128], onesf[:, :],
                                          brow[:, 64:128], start=True, stop=True))

                @block.vector
                def _(v):
                    v.wait_ge(S["rep"].h, S["rep"].n)
                    v.wait_ge(S["rs"].h, S["rs"].n)
                    S["rep"].inc(v.tensor_copy(qsr, p2[:, 0:64]))
                    S["rep"].inc(v.tensor_copy(ksr, p2[:, 64:128]))

                for wd in range(NW):
                    t0 = wd * 4
                    for mt in range(4):
                        @block.gpsimd
                        def _(g, mt=mt, wd=wd, lay=lay):
                            m = wd * 4 + mt
                            gm = (1 + lay) * NTT + m
                            if wd > 0 and mt == 0:
                                g.wait_ge(S["yv"].h, S["yv"].n - 8)
                            if wd == 0 and mt == 0:
                                for s_ in r_st.sems:
                                    g.wait_ge(s_.h, s_.n)
                            r_x.load(g, gm, xwf[:, mt * D:(mt + 1) * D], src_r[m])

                    @block.vector
                    def _(v, wd=wd):
                        def F(ins):
                            S["fen"].inc(ins)
                            v.wait_ge(S["fen"].h, S["fen"].n)
                        if wd > 0:
                            v.wait_ge(S["tex"].h, S["tex"].n)
                            for k in range(4):
                                F(v.tensor_copy(halo[:, k:k + 1],
                                                xT[:, (4 + k) * 512 + 511:(4 + k) * 512 + 512]))
                        else:
                            F(v.memset(halo[:, :], 0.0))
                            v.wait_ge(S["mmy"].h, S["mmy"].n)
                        for mt in range(4):
                            r_x.wait(v, (1 + lay) * NTT + wd * 4 + mt)
                        for mt in range(4):
                            F(v.tensor_copy(xwb[:, mt * D:(mt + 1) * D],
                                            xwf[:, mt * D:(mt + 1) * D]))
                        S["cast"].inc(v.nop())

                    @block.tensor
                    def _(t, wd=wd):
                        t.wait_ge(S["cast"].h, S["cast"].n)
                        if wd > 0 or lay > 0:
                            t.wait_ge(S["tex"].h, S["tex"].n)
                        t.wait_ge(s_ident.h, s_ident.n)
                        for mt in range(4):
                            pa = psA if mt % 2 == 0 else psB
                            if mt >= 2:
                                t.wait_ge(S["tex"].h, S["tex"].n + mt - 1)
                            for k in range(NKD):
                                ins = t.transpose(pa[:, k * 128:(k + 1) * 128],
                                                  xwb[:, mt * D + k * 128:mt * D + (k + 1) * 128],
                                                  ident[:, :])
                            S["trx"].inc(ins)

                    @block.scalar
                    def _(s, wd=wd):
                        cp = s.copy if hasattr(s, "copy") else s.tensor_copy
                        for mt in range(4):
                            s.wait_ge(S["trx"].h, S["trx"].n - 3 + mt)
                            pa = psA if mt % 2 == 0 else psB
                            for k in range(NKD):
                                ins = cp(xT[:, k * 512 + mt * 128:k * 512 + (mt + 1) * 128],
                                         pa[:, k * 128:(k + 1) * 128])
                            S["tex"].inc(ins)
                        S["tex"].inc(s.nop() if hasattr(s, "nop") else
                                     cp(xT[:, 0:1], xT[:, 0:1]))

                    @block.vector
                    def _(v, wd=wd):
                        def F(ins):
                            S["fen"].inc(ins)
                            v.wait_ge(S["fen"].h, S["fen"].n)
                        v.wait_ge(S["tex"].h, S["tex"].n)
                        v.wait_ge(S["mmq"].h, S["mmq"].n)
                        for k in range(4):
                            F(v.tensor_copy(shT[:, k * 512:k * 512 + 1],
                                            halo[:, k:k + 1]))
                            F(v.tensor_copy(shT[:, k * 512 + 1:(k + 1) * 512],
                                            xT[:, (4 + k) * 512:(4 + k) * 512 + 511]))
                        S["sht"].inc(v.nop())

                    for mt in range(4):
                        for third in range(3):
                            @block.tensor
                            def _(t, mt=mt, third=third, wd=wd):
                                t.wait_ge(S["sht"].h, S["sht"].n)
                                if lay + wd > 0 or mt + third > 0:
                                    t.wait_ge(S["evq"].h, S["evq"].n - 1)
                                if wd == 0 and mt == 0 and third == 0:
                                    t.wait_ge(S["w"].h, S["w"].n)
                                pf = p2 if (mt * 3 + third) % 2 == 0 else p3
                                for k in range(NKD):
                                    lhs = (xT[:, k * 512 + mt * 128:k * 512 + (mt + 1) * 128]
                                           if k < 4 else
                                           shT[:, (k - 4) * 512 + mt * 128:(k - 4) * 512 + (mt + 1) * 128])
                                    ins = t.matmul(
                                        pf[:, :], lhs,
                                        wqkv_sb[:, k * 3 * INNER + third * INNER:k * 3 * INNER + (third + 1) * INNER],
                                        start=(k == 0), stop=(k == NKD - 1))
                                S["mmq"].inc(ins)

                            @block.vector
                            def _(v, mt=mt, third=third, wd=wd):
                                v.wait_ge(S["mmq"].h, S["mmq"].n)
                                pf = p2 if (mt * 3 + third) % 2 == 0 else p3
                                if third == 0:
                                    dst = qw[:, mt * 512:(mt + 1) * 512]
                                elif third == 1:
                                    dst = kw[:, mt * 512:(mt + 1) * 512]
                                else:
                                    dst = vw[:, (wd % 2) * 2048 + mt * 512:(wd % 2) * 2048 + (mt + 1) * 512]
                                if third == 2 and wd >= 2 and mt == 0:
                                    v.wait_ge(S["mmo"].h, S["mmo"].n)
                                if third == 2 and wd == 0 and mt == 0:
                                    v.wait_ge(S["mmh"].h, S["mmh"].n)
                                S["evq"].inc(v.tensor_copy(dst, pf[:, :]))

                    for mt in range(4):
                        @block.gpsimd
                        def _(g, mt=mt, wd=wd, lay=lay):
                            gm = lay * NTT + wd * 4 + mt
                            if gm >= 2:
                                g.wait_ge(S["rot"].h, S["rot"].n - 1)
                            r_tb.load(g, gm, tblb[:, (gm % 2) * 128:(gm % 2 + 1) * 128],
                                      tblr[wd * 4 + mt])

                        @block.scalar
                        def _(s, mt=mt, wd=wd, lay=lay):
                            s.wait_ge(S["evq"].h, S["evq"].n)
                            qix = []
                            for h in range(HE):
                                qix.append(s.activation(
                                    sscr[:, h * 64:(h + 1) * 64],
                                    qw[:, mt * 512 + h * 64:mt * 512 + (h + 1) * 64],
                                    AF.Square, accum_out=nrm[:, h:h + 1]))
                            for h in range(HE):
                                qix.append(s.activation(
                                    sscr[:, 512 + h * 64:512 + (h + 1) * 64],
                                    kw[:, mt * 512 + h * 64:mt * 512 + (h + 1) * 64],
                                    AF.Square, accum_out=nrm[:, 8 + h:9 + h]))
                            S["nrm1"].inc(qix[-2])
                            S["nrm1"].inc(qix[-1])

                        @block.vector
                        def _(v, mt=mt, wd=wd, lay=lay):
                            def F(ins):
                                S["fen"].inc(ins)
                                v.wait_ge(S["fen"].h, S["fen"].n)
                            gm = lay * NTT + wd * 4 + mt
                            v.wait_ge(S["nrm1"].h, S["nrm1"].n)
                            S["nrm2"].inc(v.nop())
                            v.wait_ge(S["nrm2"].h, S["nrm2"].n)
                            S["nrm2"].inc(v.tensor_scalar_max(sqs2[:, :], nrm[:, :],
                                                              1e-24))
                            v.wait_ge(S["nrm2"].h, S["nrm2"].n)
                            S["lnV"].inc(v.nop())

                        @block.scalar
                        def _(s, mt=mt):
                            s.wait_ge(S["lnV"].h, S["lnV"].n)
                            S["nrm1"].inc(s.activation(sqs2[:, :], sqs2[:, :],
                                                       AF.Sqrt))

                        @block.vector
                        def _(v, mt=mt, wd=wd, lay=lay):
                            def F(ins):
                                S["fen"].inc(ins)
                                v.wait_ge(S["fen"].h, S["fen"].n)
                            gm = lay * NTT + wd * 4 + mt
                            v.wait_ge(S["nrm1"].h, S["nrm1"].n)
                            F(v.reciprocal(sqs2[:, :], sqs2[:, :]))
                            r_tb.wait(v, gm)
                            ts = tblb[:, (gm % 2) * 128:(gm % 2) * 128 + 128]
                            F(v.tensor_tensor(csb[:, 0:32], ts[:, 0:32],
                                              ts[:, 64:96], ALU.mult))
                            F(v.tensor_scalar_mul(csb[:, 0:32], csb[:, 0:32], QKS))
                            F(v.tensor_tensor(csb[:, 32:64], ts[:, 32:64],
                                              ts[:, 64:96], ALU.mult))
                            F(v.tensor_scalar_mul(csb[:, 32:64], csb[:, 32:64], QKS))
                            F(v.tensor_tensor(csb[:, 64:96], ts[:, 0:32],
                                              ts[:, 96:128], ALU.mult))
                            F(v.tensor_tensor(csb[:, 96:128], ts[:, 32:64],
                                              ts[:, 96:128], ALU.mult))
                            for srcw, dstb, nof, csf, snf, scv in (
                                    (qw, qb, 0, csb[:, 0:32], csb[:, 32:64], qsr),
                                    (kw, kb, 8, csb[:, 64:96], csb[:, 96:128], ksr)):
                                for h in range(HE):
                                    blk = srcw[:, mt * 512 + h * 64:mt * 512 + (h + 1) * 64]
                                    F(v.tensor_scalar_mul(blk, blk,
                                                          sqs2[:, nof + h:nof + h + 1]))
                                    F(v.tensor_tensor(blk, blk, scv, ALU.mult))
                                    x1 = blk[:, 0:32]
                                    x2 = blk[:, 32:64]
                                    ob = dstb[:, mt * 512 + h * 64:mt * 512 + (h + 1) * 64]
                                    F(v.tensor_tensor(sexp[:, 128:160], x1, csf,
                                                      ALU.mult))
                                    F(v.tensor_tensor(sexp[:, 160:192], x2, snf,
                                                      ALU.mult))
                                    F(v.tensor_tensor(ob[:, 0:32], sexp[:, 128:160],
                                                      sexp[:, 160:192], ALU.subtract))
                                    F(v.tensor_tensor(sexp[:, 128:160], x2, csf,
                                                      ALU.mult))
                                    F(v.tensor_tensor(sexp[:, 160:192], x1, snf,
                                                      ALU.mult))
                                    F(v.tensor_tensor(ob[:, 32:64], sexp[:, 128:160],
                                                      sexp[:, 160:192], ALU.add))
                            S["rot"].inc(v.nop())

                    qT = xwb[:, 0:2048]
                    kTc = kT2[:, (wd % 2) * 2048:(wd % 2 + 1) * 2048]

                    @block.tensor
                    def _(t, wd=wd):
                        t.wait_ge(S["rot"].h, S["rot"].n)
                        if lay + wd > 0:
                            t.wait_ge(S["teq"].h, S["teq"].n)
                        for mt in range(4):
                            for j in range(4):
                                i = mt * 4 + j
                                pa = psA if i % 2 == 0 else psB
                                if i >= 2:
                                    t.wait_ge(S["teq"].h, S["teq"].n + i - 1)
                                ins = t.transpose(pa[:, 0:128],
                                                  qb[:, mt * 512 + j * 128:mt * 512 + (j + 1) * 128],
                                                  ident[:, :])
                                ins2 = t.transpose(pa[:, 128:256],
                                                   kb[:, mt * 512 + j * 128:mt * 512 + (j + 1) * 128],
                                                   ident[:, :])
                                S["trq"].inc(ins2)

                    @block.scalar
                    def _(s, wd=wd):
                        cp = s.copy if hasattr(s, "copy") else s.tensor_copy
                        for mt in range(4):
                            for j in range(4):
                                i = mt * 4 + j
                                s.wait_ge(S["trq"].h, S["trq"].n - 15 + i)
                                pa = psA if i % 2 == 0 else psB
                                cp(qT[:, j * 512 + mt * 128:j * 512 + (mt + 1) * 128],
                                   pa[:, 0:128])
                                ins = cp(kTc[:, j * 512 + mt * 128:j * 512 + (mt + 1) * 128],
                                         pa[:, 128:256])
                                S["teq"].inc(ins)

                    for h in range(HE):
                        for mt in range(4):
                            j, ro = h // 2, (h % 2) * 64

                            @block.tensor
                            def _(t, h=h, mt=mt, j=j, ro=ro, wd=wd):
                                t.wait_ge(S["teq"].h, S["teq"].n)
                                t.wait_ge(S["smax"].h, S["smax"].n)
                                kprev = kT2[:, ((wd - 1) % 2) * 2048:((wd - 1) % 2) * 2048 + 2048]
                                lhs = qT[ro:ro + 64, j * 512 + mt * 128:j * 512 + (mt + 1) * 128]
                                if wd > 0:
                                    S["mms"].inc(t.matmul(
                                        p0[:, :], lhs,
                                        kprev[ro:ro + 64, j * 512:(j + 1) * 512],
                                        start=True, stop=True))
                                S["mms"].inc(t.matmul(
                                    p1[:, :], lhs,
                                    kTc[ro:ro + 64, j * 512:(j + 1) * 512],
                                    start=True, stop=True))

                            @block.vector
                            def _(v, h=h, mt=mt, wd=wd):
                                def F(ins):
                                    S["fen"].inc(ins)
                                    v.wait_ge(S["fen"].h, S["fen"].n)
                                v.wait_ge(S["mms"].h, S["mms"].n)
                                v.wait_ge(S["msk"].h, S["msk"].n)
                                v.wait_ge(S["sexpS"].h, S["sexpS"].n)
                                if wd > 0:
                                    F(v.tensor_tensor(sexp[:, 192:704], p0[:, :],
                                                      masks[:, mt * 512:(mt + 1) * 512],
                                                      ALU.add))
                                F(v.tensor_tensor(sexp[:, 704:1216] if False else
                                                  sscr[:, 0:512], p1[:, :],
                                                  masks[:, 2048 + mt * 512:2048 + (mt + 1) * 512],
                                                  ALU.add))
                                if wd > 0:
                                    F(v.reduce_max(nrm[:, 0:1], sexp[:, 192:704],
                                                   axis=AX.X, negate=True))
                                    F(v.reduce_max(nrm[:, 1:2], sscr[:, 0:512],
                                                   axis=AX.X, negate=True))
                                    F(v.tensor_tensor(nrm[:, 2:3], nrm[:, 0:1],
                                                      nrm[:, 1:2], ALU.min))
                                else:
                                    F(v.reduce_max(nrm[:, 2:3], sscr[:, 0:512],
                                                   axis=AX.X, negate=True))
                                S["smax"].inc(v.nop())

                            @block.scalar
                            def _(s, h=h, mt=mt, wd=wd):
                                s.wait_ge(S["smax"].h, S["smax"].n)
                                if wd > 0:
                                    S["sexpS"].inc(s.activation(
                                        sexpb[:, 0:512], sexp[:, 192:704], AF.Exp,
                                        bias=nrm[:, 2:3], accum_out=nrm[:, 4:5]))
                                S["sexpS"].inc(s.activation(
                                    sexpb[:, 512:1024], sscr[:, 0:512], AF.Exp,
                                    bias=nrm[:, 2:3], accum_out=nrm[:, 5:6]))

                            @block.vector
                            def _(v, h=h, mt=mt, wd=wd):
                                def F(ins):
                                    S["fen"].inc(ins)
                                    v.wait_ge(S["fen"].h, S["fen"].n)
                                v.wait_ge(S["sexpS"].h, S["sexpS"].n)
                                if wd > 0:
                                    F(v.tensor_tensor(nrm[:, 6:7], nrm[:, 4:5],
                                                      nrm[:, 5:6], ALU.add))
                                else:
                                    F(v.tensor_copy(nrm[:, 6:7], nrm[:, 5:6]))
                                F(v.reciprocal(nrm[:, 6:7], nrm[:, 6:7]))
                                lo = 0 if wd > 0 else 512
                                F(v.tensor_scalar_mul(sexpb[:, lo:1024],
                                                      sexpb[:, lo:1024],
                                                      nrm[:, 6:7]))
                                S["rs"].inc(v.nop())

                            @block.tensor
                            def _(t, h=h, mt=mt, wd=wd):
                                t.wait_ge(S["rs"].h, S["rs"].n)
                                t.wait_ge(S["teA"].h, S["teA"].n)
                                lo = 0 if wd > 0 else 4
                                for kt in range(lo, 8):
                                    pa = psA if kt % 2 == 0 else psB
                                    ins = t.transpose(pa[:, (kt // 2) * 128:(kt // 2 + 1) * 128],
                                                      sexpb[:, kt * 128:(kt + 1) * 128],
                                                      ident[:, :])
                                S["trA"].inc(ins)

                            @block.scalar
                            def _(s, h=h, mt=mt, wd=wd):
                                s.wait_ge(S["trA"].h, S["trA"].n)
                                cp = s.copy if hasattr(s, "copy") else s.tensor_copy
                                lo = 0 if wd > 0 else 4
                                for kt in range(lo, 8):
                                    pa = psA if kt % 2 == 0 else psB
                                    ins = cp(awT[:, kt * 128:(kt + 1) * 128],
                                             pa[:, (kt // 2) * 128:(kt // 2 + 1) * 128])
                                S["teA"].inc(ins)

                            @block.tensor
                            def _(t, h=h, mt=mt, j=j, ro=ro, wd=wd):
                                t.wait_ge(S["teA"].h, S["teA"].n)
                                if lay + wd + h + mt > 0:
                                    t.wait_ge(S["teo"].h, S["teo"].n)
                                po = po0 if (h * 4 + mt) % 2 == 0 else po1
                                lo = 0 if wd > 0 else 4
                                for kt in range(lo, 8):
                                    vt = (kT2 if False else vw)[
                                        :, (((wd - 1) if kt < 4 else wd) % 2) * 2048 +
                                        (kt % 4) * 512 + h * 64:
                                        (((wd - 1) if kt < 4 else wd) % 2) * 2048 +
                                        (kt % 4) * 512 + (h + 1) * 64]
                                    ins = t.matmul(po[0:64, 0:128], vt,
                                                   awT[:, kt * 128:(kt + 1) * 128],
                                                   start=(kt == lo), stop=(kt == 7))
                                S["mmo"].inc(ins)

                            @block.scalar
                            def _(s, h=h, mt=mt, j=j, ro=ro):
                                s.wait_ge(S["mmo"].h, S["mmo"].n)
                                cp = s.copy if hasattr(s, "copy") else s.tensor_copy
                                po = po0 if (h * 4 + mt) % 2 == 0 else po1
                                S["teo"].inc(cp(
                                    oTw[ro:ro + 64, j * 512 + mt * 128:j * 512 + (mt + 1) * 128],
                                    po[0:64, 0:128]))

                    for mt in range(4):
                        @block.tensor
                        def _(t, mt=mt, wd=wd):
                            t.wait_ge(S["teo"].h, S["teo"].n)
                            t.wait_ge(S["yv"].h, S["yv"].n)
                            for half in range(2):
                                pf = p2 if half == 0 else p3
                                for k in range(4):
                                    ins = t.matmul(
                                        pf[:, :],
                                        oTw[:, k * 512 + mt * 128:k * 512 + (mt + 1) * 128],
                                        wo_sb[:, k * D + half * 512:k * D + (half + 1) * 512],
                                        start=(k == 0), stop=(k == 3))
                                S["mmw"].inc(ins)

                        @block.vector
                        def _(v, mt=mt, wd=wd, lay=lay):
                            v.wait_ge(S["mmw"].h, S["mmw"].n)
                            m = wd * 4 + mt
                            if m >= 2:
                                r_st.wait(v, lay * 2 * NTT + NTT + m - 2)
                            for half in range(2):
                                pf = p2 if half == 0 else p3
                                S["yv"].inc(v.tensor_tensor(
                                    xnew[:, (m % 2) * D + half * 512:(m % 2) * D + (half + 1) * 512],
                                    xwf[:, mt * D + half * 512:mt * D + (half + 1) * 512],
                                    pf[:, :], ALU.add))

                        @block.gpsimd
                        def _(g, mt=mt, wd=wd, lay=lay):
                            g.wait_ge(S["yv"].h, S["yv"].n)
                            m = wd * 4 + mt
                            r_st.load(g, lay * 2 * NTT + NTT + m, dst_r[m],
                                      xnew[:, (m % 2) * D:(m % 2 + 1) * D])

            def emit_ffn_pass(lay, src_r, dst_r, final):
                @block.gpsimd
                def _(g, lay=lay):
                    g.wait_ge(S["mmy"].h, S["mmy"].n)
                    g.wait_ge(S["mmq"].h, S["mmq"].n)
                    g.wait_ge(S["mmw"].h, S["mmw"].n)
                    rr = w2_h[lay].rearrange("f d -> f d")
                    for j in range(NJ):
                        rows = 128 if j < NJ - 1 else HID - 128 * (NJ - 1)
                        S["w"].inc(g.dma_start(
                            w2sb[0:rows, j * D:(j + 1) * D],
                            w2_h[lay][j * 128:j * 128 + rows, :]), 16)
                    S["brl"].inc(g.dma_start(
                        brow[:, :], lg_h[lay].rearrange("(o d) -> o d", o=1)), 16)
                    b1l = b1_h[lay]
                    S["w"].inc(g.dma_start(
                        b1c[0:128, 0:21],
                        b1l[0:2688].rearrange("(j p) -> p j", p=128),
                        allow_slow_non_contiguous=True), 16)
                    S["w"].inc(g.dma_start(
                        b1c[0:42, 21:22],
                        b1l[2688:2730].rearrange("(p o) -> p o", o=1)), 16)
                    S["w"].inc(g.dma_start(
                        b1c[0:128, 22:43],
                        b1l[2730:5418].rearrange("(j p) -> p j", p=128),
                        allow_slow_non_contiguous=True), 16)
                    S["w"].inc(g.dma_start(
                        b1c[0:42, 43:44],
                        b1l[5418:5460].rearrange("(p o) -> p o", o=1)), 16)

                for bi, (src_b, dst_rep) in enumerate([
                        (lg_h[lay], lgrep), (lb_h[lay], lbrep), (b2_h[lay], b2rep)]):
                    @block.gpsimd
                    def _(g, bi=bi, src_b=src_b, lay=lay):
                        if bi > 0:
                            g.wait_ge(S["rep"].h, S["rep"].n)
                        else:
                            g.wait_ge(S["brl"].h, S["brl"].n)
                            g.wait_ge(S["rot"].h, S["rot"].n)
                        S["brl"].inc(g.dma_start(
                            brow[:, :], src_b.rearrange("(o d) -> o d", o=1)), 16)

                    @block.tensor
                    def _(t, bi=bi):
                        t.wait_ge(S["brl"].h, S["brl"].n)
                        if bi == 0:
                            t.wait_ge(S["yv"].h, S["yv"].n)
                        else:
                            t.wait_ge(S["rep"].h, S["rep"].n)
                        S["rep"].inc(t.matmul(p2[:, :], onesf[:, :],
                                              brow[:, 0:512], start=True, stop=True))
                        S["rep"].inc(t.matmul(p3[:, :], onesf[:, :],
                                              brow[:, 512:1024], start=True, stop=True))

                    @block.vector
                    def _(v, bi=bi, dst_rep=dst_rep):
                        v.wait_ge(S["rep"].h, S["rep"].n)
                        if bi == 0:
                            v.wait_ge(S["smax"].h, S["smax"].n)
                            v.wait_ge(S["rs"].h, S["rs"].n)
                        S["rep"].inc(v.tensor_copy(dst_rep[:, 0:512], p2[:, :]))
                        S["rep"].inc(v.tensor_copy(dst_rep[:, 512:1024], p3[:, :]))

                xt = arE[:, 0:2048]
                xshh = arE[:, 2048:3072]
                lnf = arE[:, 3072:4096]

                for chn in range(NTT // 4):
                    for mt in range(4):
                        m = chn * 4 + mt
                        gm = lay * NTT + m

                        @block.gpsimd
                        def _(g, m=m, gm=gm, lay=lay):
                            if m == 0:
                                g.wait_ge(S["yv"].h, S["yv"].n)
                                for s_ in r_st.sems:
                                    g.wait_ge(s_.h, s_.n)
                            if m >= 2:
                                g.wait_ge(S["lnV"].h, S["lnV"].n - 1)
                            sl = (m % 2)
                            r_xt.load(g, 2 * gm, xt[:, sl * D:(sl + 1) * D], src_r[m])
                            if m == 0:
                                g.memset(xshh[0:1, sl * 512:sl * 512 + 512], 0.0)
                                r_xt.load(g, 2 * gm + 1,
                                          xshh[1:128, sl * 512:(sl + 1) * 512],
                                          src_r.rearrange("m p d -> (m p) d")[
                                              0:127, 512:1024])
                            else:
                                r_xt.load(g, 2 * gm + 1,
                                          xshh[:, sl * 512:(sl + 1) * 512],
                                          src_r.rearrange("m p d -> (m p) d")[
                                              m * 128 - 1:m * 128 + 127, 512:1024])

                        @block.vector
                        def _(v, m=m, gm=gm):
                            def F(ins):
                                S["fen"].inc(ins)
                                v.wait_ge(S["fen"].h, S["fen"].n)
                            r_xt.wait(v, 2 * gm)
                            r_xt.wait(v, 2 * gm + 1)
                            sl = (m % 2)
                            F(v.reduce_sum(stato[:, 0:1], xt[:, sl * D:sl * D + 512],
                                           axis=AX.X))
                            F(v.reduce_sum(stato[:, 1:2],
                                           xshh[:, sl * 512:(sl + 1) * 512],
                                           axis=AX.X))
                            F(v.tensor_tensor(stato[:, 0:1], stato[:, 0:1],
                                              stato[:, 1:2], ALU.add))
                            S["lnV"].inc(v.nop())

                        @block.scalar
                        def _(s, m=m):
                            s.wait_ge(S["lnV"].h, S["lnV"].n)
                            sl = (m % 2)
                            s.activation(sscr[:, 0:512], xt[:, sl * D:sl * D + 512],
                                         AF.Square, accum_out=stato[:, 2:3])
                            S["lnS"].inc(s.activation(
                                sscr[:, 512:1024], xshh[:, sl * 512:(sl + 1) * 512],
                                AF.Square, accum_out=stato[:, 3:4]))
                            s.wait_ge(S["lnS"].h, S["lnS"].n)

                        @block.vector
                        def _(v, m=m):
                            def F(ins):
                                S["fen"].inc(ins)
                                v.wait_ge(S["fen"].h, S["fen"].n)
                            v.wait_ge(S["lnS"].h, S["lnS"].n)
                            F(v.tensor_tensor(stato[:, 2:3], stato[:, 2:3],
                                              stato[:, 3:4], ALU.add))
                            F(v.tensor_scalar_mul(stato[:, 0:1], stato[:, 0:1],
                                                  1.0 / D))
                            F(v.tensor_scalar_mul(stato[:, 2:3], stato[:, 2:3],
                                                  1.0 / D))
                            F(v.tensor_tensor(stato[:, 4:5], stato[:, 0:1],
                                              stato[:, 0:1], ALU.mult))
                            F(v.tensor_tensor(stato[:, 2:3], stato[:, 2:3],
                                              stato[:, 4:5], ALU.subtract))
                            F(v.tensor_scalar_add(stato[:, 2:3], stato[:, 2:3],
                                                  1e-5))
                            S["lnV"].inc(v.nop())

                        @block.scalar
                        def _(s, m=m):
                            s.wait_ge(S["lnV"].h, S["lnV"].n)
                            S["lnS"].inc(s.activation(stato[:, 2:3], stato[:, 2:3],
                                                      AF.Sqrt))

                        @block.vector
                        def _(v, m=m, gm=gm):
                            def F(ins):
                                S["fen"].inc(ins)
                                v.wait_ge(S["fen"].h, S["fen"].n)
                            v.wait_ge(S["lnS"].h, S["lnS"].n)
                            v.wait_ge(S["rep"].h, S["rep"].n)
                            sl = (m % 2)
                            F(v.reciprocal(stato[:, 2:3], stato[:, 2:3]))
                            F(v.tensor_scalar(lnf[:, 0:512],
                                              xt[:, sl * D:sl * D + 512],
                                              stato[:, 0:1], stato[:, 2:3],
                                              ALU.subtract, ALU.mult))
                            F(v.tensor_scalar(lnf[:, 512:1024],
                                              xshh[:, sl * 512:(sl + 1) * 512],
                                              stato[:, 0:1], stato[:, 2:3],
                                              ALU.subtract, ALU.mult))
                            F(v.tensor_tensor(lnf[:, :], lnf[:, :], lgrep[:, :],
                                              ALU.mult))
                            F(v.tensor_tensor(lnbb[:, :], lnf[:, :], lbrep[:, :],
                                              ALU.add))
                            S["cvt"].inc(v.nop())

                        @block.tensor
                        def _(t, m=m, mt=mt):
                            t.wait_ge(S["cvt"].h, S["cvt"].n)
                            if lay > 0 or m > 0:
                                t.wait_ge(S["teL"].h, S["teL"].n)
                            if m == 0:
                                t.wait_ge(S["mmh"].h, S["mmh"].n)
                            pa = psA if m % 2 == 0 else psB
                            for k in range(NKD):
                                ins = t.transpose(pa[:, k * 128:(k + 1) * 128],
                                                  lnbb[:, k * 128:(k + 1) * 128],
                                                  ident[:, :])
                            S["trL"].inc(ins)

                        @block.scalar
                        def _(s, m=m, mt=mt):
                            s.wait_ge(S["trL"].h, S["trL"].n)
                            cp = s.copy if hasattr(s, "copy") else s.tensor_copy
                            pa = psA if m % 2 == 0 else psB
                            for k in range(NKD):
                                ins = cp(lnTc[:, k * 512 + mt * 128:k * 512 + (mt + 1) * 128],
                                         pa[:, k * 128:(k + 1) * 128])
                            S["teL"].inc(ins)

                    for j in range(NJ):
                        rows = 128 if j < NJ - 1 else HID - 128 * (NJ - 1)

                        @block.sync
                        def _(sy, j=j, chn=chn, lay=lay, rows=rows):
                            gj = (lay * 8 + chn) * 2 * NJ + 2 * j
                            if gj >= 4:
                                sy.wait_ge(S["mmh"].h, S["mmh"].n - 2)
                            if j == 0 and chn == 0:
                                sy.wait_ge(S["wcp"].h, S["wcp"].n)
                                sy.wait_ge(S["mmo"].h, S["mmo"].n)
                                sy.wait_ge(S["evq"].h, S["evq"].n)
                            src = wb1_d[lay].rearrange("(k p) f -> p k f", p=128)
                            d0 = w1stg[:, (gj % 4) * 1024:(gj % 4) * 1024 + 1024]\
                                .rearrange("p (k f) -> p k f", f=128)[:, :, 0:rows]
                            d1 = w1stg[:, ((gj + 1) % 4) * 1024:((gj + 1) % 4) * 1024 + 1024]\
                                .rearrange("p (k f) -> p k f", f=128)[:, :, 0:rows]
                            r_w1.load(sy, gj, d0, src[:, :, j * 128:j * 128 + rows])
                            r_w1.load(sy, gj + 1, d1,
                                      src[:, :, HID + j * 128:HID + j * 128 + rows])

                        @block.tensor
                        def _(t, j=j, chn=chn, lay=lay, rows=rows):
                            gj = (lay * 8 + chn) * 2 * NJ + 2 * j
                            r_w1.wait(t, gj)
                            r_w1.wait(t, gj + 1)
                            if j >= 1 or chn + lay > 0:
                                t.wait_ge(S["evh"].h, S["evh"].n)
                            if j == 0:
                                t.wait_ge(S["teL"].h, S["teL"].n)
                            for k in range(NKD):
                                ins = t.matmul(
                                    p0[0:rows, :],
                                    w1stg[:, (gj % 4) * 1024 + k * 128:(gj % 4) * 1024 + k * 128 + rows],
                                    lnTc[:, k * 512:(k + 1) * 512],
                                    start=(k == 0), stop=(k == NKD - 1))
                            S["mmh"].inc(ins)
                            for k in range(NKD):
                                ins = t.matmul(
                                    p1[0:rows, :],
                                    w1stg[:, ((gj + 1) % 4) * 1024 + k * 128:((gj + 1) % 4) * 1024 + k * 128 + rows],
                                    lnTc[:, k * 512:(k + 1) * 512],
                                    start=(k == 0), stop=(k == NKD - 1))
                            S["mmh"].inc(ins)

                        @block.scalar
                        def _(s, j=j, rows=rows, chn=chn):
                            if j == 0 and chn == 0:
                                s.wait_ge(S["w"].h, S["w"].n)
                            s.wait_ge(S["mmh"].h, S["mmh"].n)
                            S["gel"].inc(s.activation(geb[0:rows, :], p1[0:rows, :],
                                                      AF.Gelu,
                                                      bias=b1c[0:rows, 22 + j:23 + j]))

                        @block.vector
                        def _(v, j=j, rows=rows):
                            def F(ins):
                                S["fen"].inc(ins)
                                v.wait_ge(S["fen"].h, S["fen"].n)
                            v.wait_ge(S["gel"].h, S["gel"].n)
                            F(v.tensor_scalar_add(ha[0:rows, :], p0[0:rows, :],
                                                  b1c[0:rows, j:j + 1]))
                            S["evh"].inc(v.tensor_tensor(
                                hT[0:rows, j * 512:(j + 1) * 512], ha[0:rows, :],
                                geb[0:rows, :], ALU.mult))

                    for mt in range(4):
                        m = chn * 4 + mt
                        gm = lay * NTT + m

                        @block.tensor
                        def _(t, m=m, mt=mt, chn=chn, lay=lay):
                            if chn == 0 and mt == 0:
                                t.wait_ge(S["w"].h, S["w"].n)
                            t.wait_ge(S["evh"].h, S["evh"].n)
                            if lay + chn + mt > 0:
                                t.wait_ge(S["b2v"].h, S["b2v"].n)
                            for half in range(2):
                                pf = p2 if half == 0 else p3
                                for j in range(NJ):
                                    rows = 128 if j < NJ - 1 else HID - 128 * (NJ - 1)
                                    ins = t.matmul(
                                        pf[:, :],
                                        hT[0:rows, j * 512 + mt * 128:j * 512 + (mt + 1) * 128],
                                        w2sb[0:rows, j * D + half * 512:j * D + (half + 1) * 512],
                                        start=(j == 0), stop=(j == NJ - 1))
                                S["mmy"].inc(ins)

                        @block.gpsimd
                        def _(g, m=m, gm=gm, lay=lay, final=final):
                            if m >= 2:
                                g.wait_ge(S["b2v"].h, S["b2v"].n - 2)
                            r_xr.load(g, gm, xrb[:, (m % 2) * D:(m % 2 + 1) * D],
                                      src_r[m])
                            if final:
                                r_xo.load(g, m, xob[:, (m % 2) * D:(m % 2 + 1) * D],
                                          xr[m])

                        @block.vector
                        def _(v, m=m, gm=gm, mt=mt, final=final, lay=lay):
                            def F(ins):
                                S["fen"].inc(ins)
                                v.wait_ge(S["fen"].h, S["fen"].n)
                            v.wait_ge(S["mmy"].h, S["mmy"].n)
                            if m >= 2:
                                r_st.wait(v, (2 + lay * 2) * NTT + m - 2)
                            sl = (m % 2)
                            r_xr.wait(v, gm)
                            F(v.tensor_tensor(xnew[:, sl * D:sl * D + 512],
                                              p2[:, :], b2rep[:, 0:512], ALU.add))
                            F(v.tensor_tensor(xnew[:, sl * D + 512:(sl + 1) * D],
                                              p3[:, :], b2rep[:, 512:1024], ALU.add))
                            F(v.tensor_tensor(xnew[:, sl * D:(sl + 1) * D],
                                              xnew[:, sl * D:(sl + 1) * D],
                                              xrb[:, sl * D:(sl + 1) * D], ALU.add))
                            if final:
                                r_xo.wait(v, m)
                                F(v.tensor_tensor(
                                    xnew[:, sl * D:(sl + 1) * D],
                                    xnew[:, sl * D:(sl + 1) * D],
                                    xob[:, sl * D:(sl + 1) * D], ALU.add))
                            S["b2v"].inc(v.nop())

                        @block.gpsimd
                        def _(g, m=m, lay=lay):
                            g.wait_ge(S["b2v"].h, S["b2v"].n)
                            r_st.load(g, (2 + lay * 2) * NTT + m, dst_r[m],
                                      xnew[:, (m % 2) * D:(m % 2 + 1) * D])

            if npass >= 1:
                emit_attn_pass(0, xsAr, xsBr, [(S["p0s"].h, NTT)])
            if npass >= 2:
                emit_ffn_pass(0, xsBr, xsAr, False)
            if npass >= 3:
                emit_attn_pass(1, xsAr, xsBr, [])
            if npass >= 4:
                emit_ffn_pass(1, xsBr, yr, True)

            @block.gpsimd
            def _(g):
                for s_ in r_st.sems:
                    g.wait_ge(s_.h, s_.n)
    return nc



def _rot_tables():
    t = np.arange(T, dtype=np.float64)
    invf = 1.0 / 10000.0 ** (np.arange(0, 64, 2) / 64)
    fr = t[:, None] * invf[None, :]
    cost = np.cos(fr).astype(np.float32)
    sint = np.sin(fr).astype(np.float32)
    half = np.arange(0, 64, 2, dtype=np.float64)
    invf2 = 1.0 / 10000.0 ** (half / 64)
    fr2 = t[:, None] * invf2[None, :]
    base = (half + 0.4 * 64) / (1.4 * 64)
    power = (t - T // 2) / (512 // 2)
    scale = base[None, :] ** power[:, None]
    tbl = np.concatenate([np.cos(fr2), np.sin(fr2), scale, 1.0 / scale],
                         axis=1).astype(np.float32)
    return cost, sint, tbl


def kernel(**inp):
    NB = NCORES
    cost, sint, tbl = _rot_tables()
    cores = list(range(NB))

    def launch(nc, dicts, ids=None):
        return run_bass_kernel_spmd(nc, dicts,
                                    core_ids=ids if ids is not None else cores).results

    x = [np.ascontiguousarray(inp["L0_attn_x"][b]) for b in range(NB)]

    res = launch(build_l0attn(), [dict(
        x=x[b], Wq=inp["L0_attn_Wq"], Wk=inp["L0_attn_Wk"], Wv=inp["L0_attn_Wv"],
        Wo=inp["L0_attn_Wo"], lq1=inp["L0_attn_lambda_q1"],
        lk1=inp["L0_attn_lambda_k1"], lq2=inp["L0_attn_lambda_q2"],
        lk2=inp["L0_attn_lambda_k2"], ln_w=inp["L0_attn_ln_w"],
        ln_b=inp["L0_attn_ln_b"], cost=cost, sint=sint) for b in range(NB)])
    x = [r["y"] for r in res]

    res = launch(build_l0ffn(), [dict(
        x=x[b], wqkv=inp["L0_ffn_wqkv"], q_scale=inp["L0_ffn_q_scale"],
        k_scale=inp["L0_ffn_k_scale"], wo=inp["L0_ffn_wo"],
        ln_g=inp["L0_ffn_ln_g"], ln_b=inp["L0_ffn_ln_b"], w1=inp["L0_ffn_w1"],
        b1=inp["L0_ffn_b1"], w2=inp["L0_ffn_w2"], b2=inp["L0_ffn_b2"],
        tbl=tbl) for b in range(NB)])
    x = [r["y"] for r in res]

    res = launch(build_l1attn(), [dict(
        x=x[b], log_lambda_raw=inp["L1_attn_log_lambda_raw"],
        Wb=inp["L1_attn_Wb"], bb=inp["L1_attn_bb"], Wc=inp["L1_attn_Wc"],
        bc=inp["L1_attn_bc"], gamma=inp["L1_attn_gamma"],
        beta=inp["L1_attn_beta"]) for b in range(NB)])
    x = [r["y"] for r in res]

    res = launch(build_l1ffn_router(), [dict(
        x=x[b], Wr=inp["L1_ffn_Wr"]) for b in range(NB)])
    gfull = np.concatenate([r["gates"] for r in res], axis=0)
    geff = launch(build_l1ffn_capacity(), [dict(gates_full=gfull)],
                  ids=[0])[0]["geff"]
    res = launch(build_l1ffn_experts(), [dict(
        x=x[b], geff=np.ascontiguousarray(geff[b * T:(b + 1) * T]),
        W1=inp["L1_ffn_W1"], b1=inp["L1_ffn_b1"], W2=inp["L1_ffn_W2"],
        b2=inp["L1_ffn_b2"], fw1=inp["L1_ffn_fw1"], fb1=inp["L1_ffn_fb1"],
        fw2=inp["L1_ffn_fw2"], fb2=inp["L1_ffn_fb2"]) for b in range(NB)])
    x = [r["y"] for r in res]

    res = launch(build_l2attn(), [dict(
        x=x[b], W_in=inp["L2_attn_W_in"], conv_w=inp["L2_attn_conv_w"],
        conv_b=inp["L2_attn_conv_b"], W_xproj=inp["L2_attn_W_xproj"],
        W_dt=inp["L2_attn_W_dt"], b_dt=inp["L2_attn_b_dt"],
        A_log=inp["L2_attn_A_log"], Dp=inp["L2_attn_D"],
        W_out=inp["L2_attn_W_out"]) for b in range(NB)])
    x = [r["y"] for r in res]

    res = launch(build_l2ffn(), [dict(
        x=x[b], weight=inp["L2_ffn_weight"], disha=inp["L2_ffn_disha"])
        for b in range(NB)])
    x = [r["y"] for r in res]

    res = launch(build_l3attn(), [dict(
        x=x[b], A=inp["L3_attn_A"], B=inp["L3_attn_B"],
        W_in=inp["L3_attn_W_in"], b_in=inp["L3_attn_b_in"],
        W_gate=inp["L3_attn_W_gate"], b_gate=inp["L3_attn_b_gate"],
        W_out=inp["L3_attn_W_out"], b_out=inp["L3_attn_b_out"])
        for b in range(NB)])
    x = [r["y"] for r in res]

    res = launch(build_l3ffn(), [dict(
        x=x[b], Wr=inp["L3_ffn_Wr"], W1=inp["L3_ffn_W1"], b1=inp["L3_ffn_b1"],
        W2=inp["L3_ffn_W2"], b2=inp["L3_ffn_b2"]) for b in range(NB)])
    return np.stack([r["y"] for r in res]).astype(np.float32)
